# version 02
# baseline (speedup 1.0000x reference)
import numpy as np
import ml_dtypes
from contextlib import ExitStack

import concourse.bass as bass
import concourse.mybir as mybir
from concourse.bass_utils import run_bass_kernel_spmd

F32 = mybir.dt.float32
BF16 = mybir.dt.bfloat16
I32 = mybir.dt.int32
U32 = mybir.dt.uint32
AF = mybir.ActivationFunctionType
ALU = mybir.AluOpType

B, S, D = 4, 2048, 2048
NCORES = 8
TOK = 1024
RSQRT_HD = float(1.0 / np.sqrt(128.0))


class Ctr:

    def __init__(self, nc, es, name):
        self.h = es.enter_context(nc.semaphore(name))
        self.n = 0

    def bump(self, inst, k=1):
        inst.then_inc(self.h, k)
        self.n += k
        return self.n


def bfcast(x):
    return np.ascontiguousarray(x).astype(ml_dtypes.bfloat16)



def emit_l0_attn(nc, blk_outer, io, pfx="a"):
    xTb, wqT, wkT, wvT = io["xTb"], io["wqT"], io["wkT"], io["wvT"]
    triu, onesc, attnT = io["triu"], io["onesc"], io["attnT"]
    dbg_q = dbg_k = dbg_vn = dbg_rc = None

    es = ExitStack()
    sb = lambda name, shape, dt: es.enter_context(nc.sbuf_tensor(pfx + name, shape, dt))
    ps = lambda name, shape, dt: es.enter_context(nc.psum_tensor(pfx + name, shape, dt))

    xT = sb("xT", [128, 16 * S], BF16)
    wq = sb("wq", [128, 16 * 1024], BF16)
    wk = sb("wk", [128, 16 * 1024], BF16)
    wv = sb("wv", [128, 16 * 1024], BF16)
    triu_b = sb("triu_b", [128, 128], BF16)
    ones_b = sb("ones_b", [128, 128], BF16)
    qT = sb("qT", [128, S], BF16)
    kT = sb("kT", [128, S], BF16)
    vT = sb("vT", [128, S], BF16)
    vn = sb("vn", [128, S], BF16)
    exps = [sb(f"exps{i}", [128, 1024], BF16) for i in range(2)]
    recip = sb("recip", [128, 1024], F32)
    att_o = sb("att_o", [128, S], BF16)

    ps_a = ps("ps_a", [128, 1024], F32)
    ps_b = ps("ps_b", [128, 1024], F32)
    ps_den = ps("ps_den", [128, 1024], F32)
    ps_att = ps("ps_att", [128, 1024], F32)

    if True:
        blk = blk_outer
        sems = ExitStack()
        g_ld = Ctr(nc, sems, pfx + "g_ld")
        mm = Ctr(nc, sems, pfx + "mm")
        ev = Ctr(nc, sems, pfx + "ev")
        ex = Ctr(nc, sems, pfx + "ex")
        mk = Ctr(nc, sems, pfx + "mk")
        tr = Ctr(nc, sems, pfx + "tr")
        st = Ctr(nc, sems, pfx + "st")

        @blk.gpsimd
        def _(g):
            for k in range(16):
                i = g.dma_start(out=xT[:, k * S:(k + 1) * S],
                                in_=xTb[k * 128:(k + 1) * 128, :])
                g_ld.bump(i, 16)
            for w_sb, w_dr in ((wq, wqT), (wk, wkT), (wv, wvT)):
                for k in range(16):
                    i = g.dma_start(out=w_sb[:, k * 1024:(k + 1) * 1024],
                                    in_=w_dr[k * 128:(k + 1) * 128, :])
                    g_ld.bump(i, 16)
            g_ld.bump(g.dma_start(out=triu_b[:, :], in_=triu[:, :]), 16)
            g_ld.bump(g.dma_start(out=ones_b[:, :], in_=onesc[:, :]), 16)

        LD_ALL = g_ld.n

        for h in range(8):
            ev_base = ev.n

            @blk.tensor
            def _(t, h=h, ev_base=ev_base):
                if h == 0:
                    t.wait_ge(g_ld.h, LD_ALL)
                for pi, w_sb in enumerate((wq, wk, wv)):
                    if pi > 0:
                        t.wait_ge(ev.h, ev_base + pi)
                    elif h > 0:
                        t.wait_ge(ev.h, ev_base)
                    last = None
                    for k in range(16):
                        lhsT = w_sb[:, k * 1024 + h * 128: k * 1024 + (h + 1) * 128]
                        for n in range(4):
                            dst = (ps_a if n < 2 else ps_b)[:, (n % 2) * 512:(n % 2 + 1) * 512]
                            last = t.matmul(dst, lhsT,
                                            xT[:, k * S + n * 512: k * S + (n + 1) * 512],
                                            start=(k == 0), stop=(k == 15))
                    mm.bump(last)

            mm_pbase = mm.n - 3

            @blk.vector
            def _(v, h=h, mm_pbase=mm_pbase):
                for i, dst in enumerate((qT, kT, vT)):
                    v.wait_ge(mm.h, mm_pbase + i + 1)
                    if dst is vT:
                        v.wait_ge(tr.h, tr.n)
                    if i == 0 and h > 0:
                        v.wait_ge(st.h, st.n)
                    v.tensor_copy(dst[:, 0:1024], ps_a[:, :])
                    ev.bump(v.tensor_copy(dst[:, 1024:2048], ps_b[:, :]))

            @blk.sync
            def _(s):
                s.wait_ge(ev.h, ev.n)
                for kt in range(16):
                    i = s.dma_start_transpose(vn[:, kt * 128:(kt + 1) * 128],
                                              vT[:, kt * 128:(kt + 1) * 128])
                    tr.bump(i, 16)

            for qh in range(2):
                kts = list(range(0, 8 * (qh + 1)))
                K = len(kts)
                q0 = qh * 1024
                mm_base = mm.n
                ex_base = ex.n
                mk_base = mk.n
                ev_evac = ev.n
                diag = {kt: kt * 128 - q0 for kt in kts if q0 <= kt * 128 < q0 + 1024}

                def emit_scores(t, i, kt):
                    pss = (ps_a, ps_b)[i % 2]
                    co = max(0, kt * 128 - q0)
                    last = None
                    for n in range(2):
                        lo, hi = max(n * 512, co), (n + 1) * 512
                        if lo >= hi:
                            continue
                        last = t.matmul(pss[:, lo:hi],
                                        kT[:, kt * 128:(kt + 1) * 128],
                                        qT[:, q0 + lo:q0 + hi],
                                        start=True, stop=True)
                    mm.bump(last)

                @blk.tensor
                def _(t, kts=kts, K=K, q0=q0, mm_base=mm_base, ex_base=ex_base,
                      mk_base=mk_base, ev_evac=ev_evac, diag=diag, qh=qh, h=h):
                    if qh == 0 and h == 0:
                        t.wait_ge(tr.h, tr.n)
                    elif qh == 0:
                        t.wait_ge(tr.h, tr.n)
                    t.wait_ge(ev.h, ev_evac)
                    emit_scores(t, 0, kts[0])
                    if K > 1:
                        emit_scores(t, 1, kts[1])
                    ndiag_seen = 0
                    for i, kt in enumerate(kts):
                        t.wait_ge(ex.h, ex_base + i + 1)
                        if kt in diag:
                            ndiag_seen += 1
                            t.wait_ge(mk.h, mk_base + ndiag_seen)
                        first, last_i = (i == 0), (i == K - 1)
                        co = max(0, kt * 128 - q0)
                        last = None
                        for n in range(2):
                            lo, hi = max(n * 512, co), (n + 1) * 512
                            if lo >= hi:
                                continue
                            t.matmul(ps_den[:, lo:hi], ones_b[:, :],
                                     exps[i % 2][:, lo:hi], start=first, stop=last_i)
                            last = t.matmul(ps_att[:, lo:hi],
                                            vn[:, kt * 128:(kt + 1) * 128],
                                            exps[i % 2][:, lo:hi], start=first, stop=last_i)
                        mm.bump(last)
                        if i + 2 < K:
                            emit_scores(t, i + 2, kts[i + 2])

                @blk.scalar
                def _(sc, kts=kts, K=K, q0=q0, mm_base=mm_base, mk_base=mk_base,
                      diag=diag):
                    ndiag_seen = 0
                    for i, kt in enumerate(kts):
                        sc.wait_ge(mm.h, mm_base + (i + 1 if i < 2 else 2 * i))
                        if i >= 2:
                            sc.wait_ge(mm.h, mm_base + 3 + 2 * (i - 2))
                            if kts[i - 2] in diag:
                                ndiag_seen += 1
                                sc.wait_ge(mk.h, mk_base + ndiag_seen)
                        co = max(0, kt * 128 - q0)
                        pss = (ps_a, ps_b)[i % 2]
                        a = sc.activation(exps[i % 2][:, co:1024], pss[:, co:1024],
                                          AF.Exp, scale=RSQRT_HD)
                        ex.bump(a)

                @blk.vector
                def _(v, kts=kts, q0=q0, ex_base=ex_base, diag=diag):
                    for kt, co in diag.items():
                        i = kts.index(kt)
                        v.wait_ge(ex.h, ex_base + i + 1)
                        m = v.tensor_tensor(exps[i % 2][:, co:co + 128],
                                            exps[i % 2][:, co:co + 128],
                                            triu_b[:, :], ALU.mult)
                        mk.bump(m)

                @blk.vector
                def _(v, q0=q0):
                    v.wait_ge(mm.h, mm.n)
                    v.reciprocal(recip[:, :], ps_den[:, :])
                    e = v.tensor_tensor(att_o[:, q0:q0 + 1024], ps_att[:, :],
                                        recip[:, :], ALU.mult)
                    ev.bump(e)

            @blk.sync
            def _(s, h=h):
                s.wait_ge(ev.h, ev.n)
                i = s.dma_start(out=attnT[h * 128:(h + 1) * 128, :],
                                in_=att_o[:, :])
                st.bump(i, 16)
                if h == 0 and dbg_q is not None:
                    st.bump(s.dma_start(out=dbg_q[:, :], in_=qT[:, :]), 16)
                    st.bump(s.dma_start(out=dbg_k[:, :], in_=kT[:, :]), 16)
                    st.bump(s.dma_start(out=dbg_vn[:, :], in_=vn[:, :]), 16)
                    st.bump(s.dma_start(out=dbg_rc[:, :], in_=recip[:, :]), 16)

        @blk.sync
        def _(s):
            s.wait_ge(st.h, st.n)

        sems.close()
    es.close()
    return st


def build_l0_attn():
    nc = bass.Bass("TRN2", target_bir_lowering=False, debug=False,
                   detect_race_conditions=False)
    io = {
        "xTb": nc.dram_tensor("xTb", [D, S], F32, kind="ExternalInput"),
        "wqT": nc.dram_tensor("wqT", [D, 1024], F32, kind="ExternalInput"),
        "wkT": nc.dram_tensor("wkT", [D, 1024], F32, kind="ExternalInput"),
        "wvT": nc.dram_tensor("wvT", [D, 1024], F32, kind="ExternalInput"),
        "triu": nc.dram_tensor("triu", [128, 128], F32, kind="ExternalInput"),
        "onesc": nc.dram_tensor("onesc", [128, 128], F32, kind="ExternalInput"),
        "attnT": nc.dram_tensor("attnT", [1024, S], BF16, kind="ExternalOutput"),
    }
    with nc.Block() as blk:
        emit_l0_attn(nc, blk, io)
    return nc


def host_prep_l0_attn(inputs):
    x = inputs["L0_attn_hidden_states"]
    w_qkvT = np.ascontiguousarray(inputs["L0_attn_w_qkv"].T)
    triu = np.ascontiguousarray(np.triu(np.ones((128, 128), np.float32)))
    ones = np.ones((128, 128), np.float32)
    maps = []
    for c in range(NCORES):
        b, half = c // 2, c % 2
        lo = 8 * half * 128
        maps.append({
            "xTb": np.ascontiguousarray(x[b].T),
            "wqT": np.ascontiguousarray(w_qkvT[:, lo:lo + 1024]),
            "wkT": np.ascontiguousarray(w_qkvT[:, 2048 + lo:2048 + lo + 1024]),
            "wvT": np.ascontiguousarray(w_qkvT[:, 4096 + lo:4096 + lo + 1024]),
            "triu": triu,
            "onesc": ones,
        })
    return maps





NIO = 43


def emit_l0_ffn(nc, blk_outer, io, mkctr=None, upto="ALL", pfx="f"):
    attnTb, xresT, invn = io["attnTb"], io["xresT"], io["invn"]
    woP_ap, wgP_ap, wuP_ap, wdP_ap = (io["woP_ap"], io["wgP_ap"],
                                      io["wuP_ap"], io["wdP_ap"])
    x2T, x1T, rowsD = io["x2T"], io["x1T"], io["rowsD"]

    es = ExitStack()
    sb = lambda name, shape, dt: es.enter_context(nc.sbuf_tensor(pfx + name, shape, dt))
    ps = lambda name, shape, dt: es.enter_context(nc.psum_tensor(pfx + name, shape, dt))

    xr_sb = sb("xr_sb", [128, 16 * 512], F32)
    y_sb = sb("y_sb", [128, 16 * 512], F32)
    ybf = [sb(f"ybf{i}", [128, 512], BF16) for i in range(2)]
    x1b = sb("x1b", [128, 16 * 1024], BF16)
    wo_sb = [sb(f"wo_sb{i}", [128, 16 * 128], BF16) for i in range(2)]
    wg_sb = [sb(f"wg_sb{i}", [128, 16 * 128], BF16) for i in range(2)]
    wu_sb = [sb(f"wu_sb{i}", [128, 16 * 128], BF16) for i in range(2)]
    wd_sb = [sb(f"wd_sb{i}", [128, NIO * 128], BF16) for i in range(2)]
    h_sb = sb("h_sb", [128, NIO * 512], BF16)
    at_sb = h_sb
    sg_sb = sb("sg_sb", [128, 512], BF16)
    sq_sb = sb("sq_sb", [128, 512], BF16)
    inv_sb = sb("inv_sb", [128, 1], BF16)
    r1 = sb("r1", [1, 512], F32)
    r2 = sb("r2", [1, 512], F32)
    rtmp = sb("rtmp", [1, 512], F32)
    mb = sb("mb", [128, 512], F32)
    rbb = sb("rbb", [128, 512], F32)
    x1f = [sb(f"x1f{i}", [128, 512], F32) for i in range(2)]

    ps_y = [ps(f"ps_y{i}", [128, 512], F32) for i in range(2)]
    ps_u = [ps(f"ps_u{i}", [128, 512], F32) for i in range(2)]
    ps_m = ps("ps_m", [1, 512], F32)
    ps_v = ps("ps_v", [1, 512], F32)

    if True:
        blk = blk_outer
        sems = ExitStack()
        if mkctr is None:
            mkctr = lambda name: Ctr(nc, sems, pfx + name)
        gld = mkctr("gld")
        sld = mkctr("sld")
        mm = mkctr("mm")
        ev = mkctr("ev")
        ac = mkctr("ac")
        st = mkctr("st")
        bc = mkctr("bc")

        @blk.gpsimd
        def _(g):
            gld.bump(g.dma_start(out=inv_sb[:, :], in_=invn[:, :]), 16)

        def emit_stats(oc, src_bf):
            @blk.scalar
            def _(sc, oc=oc, src_bf=src_bf):
                sc.wait_ge(ev.h, ev.n)
                sc.wait_ge(mm.h, mm.n)
                ac.bump(sc.activation(sq_sb[:, :], src_bf[:, :], AF.Square))

            @blk.tensor
            def _(t, oc=oc, src_bf=src_bf):
                t.wait_ge(ev.h, ev.n)
                t.matmul(ps_m[:, :], inv_sb[:, :], src_bf[:, :],
                         start=(oc == 0), stop=(oc == 15))
                t.wait_ge(ac.h, ac.n)
                mm.bump(t.matmul(ps_v[:, :], inv_sb[:, :], sq_sb[:, :],
                                 start=(oc == 0), stop=(oc == 15)))

        def emit_ln(th, out_bf_full, out_dram):
            @blk.vector
            def _(v):
                v.wait_ge(mm.h, mm.n)
                v.tensor_copy(r1[:, :], ps_m[:, :])
                v.tensor_copy(r2[:, :], ps_v[:, :])
                v.tensor_tensor(rtmp[0:1, :], r1[0:1, :], r1[0:1, :], ALU.mult)
                v.tensor_tensor(r2[0:1, :], r2[0:1, :], rtmp[0:1, :],
                                ALU.subtract)
                ev.bump(v.tensor_scalar(r2[0:1, :], r2[0:1, :], 1e-5, None,
                                        ALU.add))

            @blk.scalar
            def _(sc):
                sc.wait_ge(ev.h, ev.n)
                ac.bump(sc.activation(r2[0:1, :], r2[0:1, :], AF.Sqrt))

            @blk.vector
            def _(v):
                v.wait_ge(ac.h, ac.n)
                ev.bump(v.reciprocal(r2[0:1, :], r2[0:1, :]))

            @blk.gpsimd
            def _(g):
                g.wait_ge(ev.h, ev.n)
                g.dma_start(out=rowsD[0:1, :], in_=r1[0:1, :]).then_inc(bc.h, 16)
                g.dma_start(out=rowsD[1:2, :], in_=r2[0:1, :]).then_inc(bc.h, 16)
                g.wait_ge(bc.h, bc.n + 32)
                bm = bass.AP(tensor=rowsD, offset=0, ap=[[0, 128], [1, 512]])
                br = bass.AP(tensor=rowsD, offset=512, ap=[[0, 128], [1, 512]])
                g.dma_start(out=mb[:, :], in_=bm).then_inc(bc.h, 16)
                g.dma_start(out=rbb[:, :], in_=br).then_inc(bc.h, 16)
            bc.n += 64

            for oc in range(16):
                @blk.vector
                def _(v, oc=oc, th=th):
                    if oc == 0:
                        v.wait_ge(bc.h, bc.n)
                    if oc >= 2:
                        v.wait_ge(st.h, st.n - 16)
                    xf = x1f[oc % 2]
                    v.tensor_tensor(xf[:, :], y_sb[:, oc * 512:(oc + 1) * 512],
                                    mb[:, :], ALU.subtract)
                    v.tensor_tensor(xf[:, :], xf[:, :], rbb[:, :], ALU.mult)
                    ev.bump(v.tensor_copy(
                        out_bf_full[:, oc * 1024 + th * 512: oc * 1024 + th * 512 + 512],
                        xf[:, :]))

                @blk.scalar
                def _(sc, oc=oc, th=th):
                    sc.wait_ge(ev.h, ev.n)
                    st.bump(sc.dma_start(
                        out=out_dram[oc * 128:(oc + 1) * 128, th * 512:th * 512 + 512],
                        in_=x1f[oc % 2][:, :]), 16)

        for th in range(2):
            t0 = th * 512

            @blk.sync
            def _(s, th=th, t0=t0):
                if th == 1:
                    s.wait_ge(mm.h, mm.n)
                for k in range(16):
                    sld.bump(s.dma_start(
                        out=at_sb[:, k * 512:(k + 1) * 512],
                        in_=attnTb[k * 128:(k + 1) * 128, t0:t0 + 512]), 16)
                for k in range(16):
                    sld.bump(s.dma_start(
                        out=xr_sb[:, k * 512:(k + 1) * 512],
                        in_=xresT[k * 128:(k + 1) * 128, t0:t0 + 512]), 16)
            LD_A = sld.n

            @blk.gpsimd
            def _(g, th=th):
                if th == 1:
                    g.wait_ge(mm.h, mm.n)
                for oc in range(2):
                    gld.bump(g.dma_start(
                        out=wo_sb[oc][:, :], in_=woP_ap(oc)), 16)
            GP_A = gld.n

            for oc in range(16):
                @blk.gpsimd
                def _(g, oc=oc, th=th):
                    if oc + 2 < 16:
                        g.wait_ge(mm.h, mm.n + 1)
                        gld.bump(g.dma_start(
                            out=wo_sb[oc % 2][:, :], in_=woP_ap(oc + 2)), 16)

                @blk.tensor
                def _(t, oc=oc, th=th, GP_A=GP_A, LD_A=LD_A):
                    if oc == 0:
                        t.wait_ge(sld.h, LD_A)
                    t.wait_ge(gld.h, GP_A + 16 * max(0, oc - 1))
                    if oc >= 2:
                        t.wait_ge(ev.h, ev.n)
                    last = None
                    for k in range(16):
                        last = t.matmul(ps_y[oc % 2][:, :],
                                        wo_sb[oc % 2][:, k * 128:(k + 1) * 128],
                                        at_sb[:, k * 512:(k + 1) * 512],
                                        start=(k == 0), stop=(k == 15))
                    mm.bump(last)

                @blk.vector
                def _(v, oc=oc):
                    v.wait_ge(mm.h, mm.n)
                    v.tensor_tensor(y_sb[:, oc * 512:(oc + 1) * 512],
                                    ps_y[oc % 2][:, :],
                                    xr_sb[:, oc * 512:(oc + 1) * 512], ALU.add)
                    ev.bump(v.tensor_copy(ybf[oc % 2][:, :],
                                          y_sb[:, oc * 512:(oc + 1) * 512]))

                if upto != "A0":
                    emit_stats(oc, ybf[oc % 2])

            if upto == "A0":
                for oc in range(16):
                    @blk.scalar
                    def _(sc, oc=oc, th=th):
                        sc.wait_ge(ev.h, ev.n)
                        st.bump(sc.dma_start(
                            out=x2T[oc * 128:(oc + 1) * 128, th * 512:th * 512 + 512],
                            in_=y_sb[:, oc * 512:(oc + 1) * 512]), 16)
                continue
            emit_ln(th, x1b, x1T)
            if upto == "A":
                continue

            @blk.sync
            def _(s, th=th):
                s.wait_ge(st.h, st.n)
                for buf in range(2):
                    sld.bump(s.dma_start(
                        out=wg_sb[buf][:, :], in_=wgP_ap(buf)), 16)
                    sld.bump(s.dma_start(
                        out=wu_sb[buf][:, :], in_=wuP_ap(buf)), 16)
            LD_B = sld.n

            for io in range(NIO):
                @blk.sync
                def _(s, io=io):
                    if io + 2 < NIO:
                        s.wait_ge(mm.h, mm.n + 2)
                        sld.bump(s.dma_start(
                            out=wg_sb[io % 2][:, :], in_=wgP_ap(io + 2)), 16)
                        sld.bump(s.dma_start(
                            out=wu_sb[io % 2][:, :], in_=wuP_ap(io + 2)), 16)

                @blk.tensor
                def _(t, io=io, th=th, LD_B=LD_B):
                    t.wait_ge(sld.h, LD_B + 32 * max(0, io - 1))
                    if io >= 2:
                        t.wait_ge(ev.h, ev.n)
                    last = None
                    for k in range(16):
                        last = t.matmul(ps_y[io % 2][:, :],
                                        wg_sb[io % 2][:, k * 128:(k + 1) * 128],
                                        x1b[:, k * 1024 + th * 512: k * 1024 + th * 512 + 512],
                                        start=(k == 0), stop=(k == 15))
                    mm.bump(last)
                    last = None
                    for k in range(16):
                        last = t.matmul(ps_u[io % 2][:, :],
                                        wu_sb[io % 2][:, k * 128:(k + 1) * 128],
                                        x1b[:, k * 1024 + th * 512: k * 1024 + th * 512 + 512],
                                        start=(k == 0), stop=(k == 15))
                    mm.bump(last)

                @blk.scalar
                def _(sc, io=io):
                    sc.wait_ge(mm.h, mm.n - 1)
                    sc.wait_ge(ev.h, ev.n)
                    ac.bump(sc.activation(sg_sb[:, :], ps_y[io % 2][:, :], AF.Silu))

                @blk.vector
                def _(v, io=io):
                    v.wait_ge(mm.h, mm.n)
                    v.wait_ge(ac.h, ac.n)
                    ev.bump(v.tensor_tensor(h_sb[:, io * 512:(io + 1) * 512],
                                            sg_sb[:, :], ps_u[io % 2][:, :], ALU.mult))

            if upto == "B":
                continue
            @blk.sync
            def _(s, th=th):
                s.wait_ge(st.h, st.n)
                for buf in range(2):
                    sld.bump(s.dma_start(
                        out=wd_sb[buf][:, :], in_=wdP_ap(buf)), 16)
                for k in range(16):
                    sld.bump(s.dma_start(
                        out=xr_sb[:, k * 512:(k + 1) * 512],
                        in_=x1T[k * 128:(k + 1) * 128, th * 512:th * 512 + 512]), 16)
            LD_B2 = sld.n

            for oc in range(16):
                @blk.sync
                def _(s, oc=oc):
                    if oc + 2 < 16:
                        s.wait_ge(mm.h, mm.n + 1)
                        sld.bump(s.dma_start(
                            out=wd_sb[oc % 2][:, :], in_=wdP_ap(oc + 2)), 16)

                @blk.tensor
                def _(t, oc=oc, LD_B2=LD_B2):
                    t.wait_ge(sld.h, LD_B2 + 16 * max(0, oc - 1))
                    if oc == 0:
                        t.wait_ge(ev.h, ev.n)
                    if oc >= 2:
                        t.wait_ge(ev.h, ev.n)
                    last = None
                    for io in range(NIO):
                        last = t.matmul(ps_y[oc % 2][:, :],
                                        wd_sb[oc % 2][:, io * 128:(io + 1) * 128],
                                        h_sb[:, io * 512:(io + 1) * 512],
                                        start=(io == 0), stop=(io == NIO - 1))
                    mm.bump(last)

                @blk.vector
                def _(v, oc=oc):
                    v.wait_ge(mm.h, mm.n)
                    v.tensor_tensor(y_sb[:, oc * 512:(oc + 1) * 512],
                                    ps_y[oc % 2][:, :],
                                    xr_sb[:, oc * 512:(oc + 1) * 512], ALU.add)
                    ev.bump(v.tensor_copy(ybf[oc % 2][:, :],
                                          y_sb[:, oc * 512:(oc + 1) * 512]))

                emit_stats(oc, ybf[oc % 2])

            emit_ln(th, x1b, x2T)

        @blk.scalar
        def _(sc):
            sc.wait_ge(st.h, st.n)

        sems.close()
    es.close()
    return st


def build_l0_ffn(upto="ALL", race=False):
    nc = bass.Bass("TRN2", target_bir_lowering=False, debug=False,
                   detect_race_conditions=race)
    woP = nc.dram_tensor("woP", [16, 16, 128, 128], F32, kind="ExternalInput")
    wgP = nc.dram_tensor("wgP", [NIO, 16, 128, 128], BF16, kind="ExternalInput")
    wuP = nc.dram_tensor("wuP", [NIO, 16, 128, 128], BF16, kind="ExternalInput")
    wdP = nc.dram_tensor("wdP", [16, NIO, 128, 128], BF16, kind="ExternalInput")
    io = {
        "attnTb": nc.dram_tensor("attnTb", [D, TOK], BF16, kind="ExternalInput"),
        "xresT": nc.dram_tensor("xresT", [D, TOK], F32, kind="ExternalInput"),
        "invn": nc.dram_tensor("invn", [128, 1], F32, kind="ExternalInput"),
        "x2T": nc.dram_tensor("x2T", [D, TOK], F32, kind="ExternalOutput"),
        "x1T": nc.dram_tensor("x1T", [D, TOK], F32),
        "rowsD": nc.dram_tensor("rowsD", [2, 512], F32),
        "woP_ap": lambda i: woP[i].rearrange("k p c -> p k c"),
        "wgP_ap": lambda i: wgP[i].rearrange("k p c -> p k c"),
        "wuP_ap": lambda i: wuP[i].rearrange("k p c -> p k c"),
        "wdP_ap": lambda i: wdP[i].rearrange("k p c -> p k c"),
    }
    with nc.Block() as blk:
        emit_l0_ffn(nc, blk, io, upto=upto)
    return nc


def host_prep_l0_ffn(inputs, attnT_full, WgT, WuT, WdT):
    x = inputs["L0_attn_hidden_states"]
    woT = np.ascontiguousarray(inputs["L0_attn_w_o"].T)
    woP = np.ascontiguousarray(
        woT.reshape(16, 128, 16, 128).transpose(2, 0, 1, 3))
    wg = np.ascontiguousarray(
        WgT.reshape(16, 128, NIO, 128).transpose(2, 0, 1, 3))
    wu = np.ascontiguousarray(
        WuT.reshape(16, 128, NIO, 128).transpose(2, 0, 1, 3))
    wd = np.ascontiguousarray(
        WdT.reshape(NIO, 128, 16, 128).transpose(2, 0, 1, 3))
    invn = np.full((128, 1), 1.0 / 2048.0, np.float32)
    maps = []
    for c in range(NCORES):
        b, half = c // 2, c % 2
        tc = slice(half * 1024, half * 1024 + 1024)
        maps.append({
            "attnTb": np.ascontiguousarray(attnT_full[b][:, tc]),
            "xresT": np.ascontiguousarray(x[b].T[:, tc]),
            "woP": woP, "wgP": wg, "wuP": wu, "wdP": wd, "invn": invn,
        })
    return maps



GC = 5504 // 8
DC = 2048 // 8
UPC = 8192 // 8
DNC = 2048 // 8


def build_wprep():
    nc = bass.Bass("TRN2", target_bir_lowering=False, debug=False,
                   detect_race_conditions=False)
    gidx = nc.dram_tensor("gidx", [D, GC], I32, kind="ExternalInput")
    uidx = nc.dram_tensor("uidx", [D, GC], I32, kind="ExternalInput")
    didx = nc.dram_tensor("didx", [5504, DC], I32, kind="ExternalInput")
    lutb = nc.dram_tensor("lutb", [128, 16], F32, kind="ExternalInput")
    h128 = nc.dram_tensor("h128", [128, 128], F32, kind="ExternalInput")
    wupT = nc.dram_tensor("wupT", [D, UPC], F32, kind="ExternalInput")
    wgtT = nc.dram_tensor("wgtT", [D, UPC], F32, kind="ExternalInput")
    wdnT = nc.dram_tensor("wdnT", [8192, DNC], F32, kind="ExternalInput")

    wg_o = nc.dram_tensor("wg_o", [D, GC], BF16, kind="ExternalOutput")
    wu_o = nc.dram_tensor("wu_o", [D, GC], BF16, kind="ExternalOutput")
    wd_o = nc.dram_tensor("wd_o", [5504, DC], BF16, kind="ExternalOutput")
    wupP = nc.dram_tensor("wupP", [D, UPC], BF16, kind="ExternalOutput")
    wgtP = nc.dram_tensor("wgtP", [D, UPC], BF16, kind="ExternalOutput")
    wdnP = nc.dram_tensor("wdnP", [8192, DNC], BF16, kind="ExternalOutput")

    zD = nc.dram_tensor("zD", [8192, UPC], BF16)

    es = ExitStack()
    sb = lambda name, shape, dt: es.enter_context(nc.sbuf_tensor(name, shape, dt))
    ps = lambda name, shape, dt: es.enter_context(nc.psum_tensor(name, shape, dt))

    idxb = [sb(f"idxb{i}", [128, 1024], BF16) for i in range(2)]
    accb = [sb(f"accb{i}", [128, GC], BF16) for i in range(2)]
    maskb = sb("maskb", [128, GC], BF16)
    lut_sb = sb("lut_sb", [128, 16], F32)
    h128b = sb("h128b", [128, 128], BF16)
    z_sb = sb("z_sb", [128, 16 * 1024], BF16)
    z2_sb = sb("z2_sb", [128, 64 * 256], BF16)
    ps_h = [ps(f"ps_h{i}", [128, 512], F32) for i in range(2)]

    with nc.Block() as blk:
        sems = ExitStack()
        gld = Ctr(nc, sems, "gld")
        mm = Ctr(nc, sems, "mm")
        ev = Ctr(nc, sems, "ev")
        st = Ctr(nc, sems, "st")
        bd = Ctr(nc, sems, "bd")

        @blk.gpsimd
        def _(g):
            gld.bump(g.dma_start(out=lut_sb[:, :], in_=lutb[:, :]), 16)
            gld.bump(g.dma_start(out=h128b[:, :], in_=h128[:, :]), 16)

        ghost_jobs = []
        for (idx_dr, out_dr, ntile, cols) in ((gidx, wg_o, 16, GC),
                                              (uidx, wu_o, 16, GC),
                                              (didx, wd_o, 43, DC)):
            for kt in range(ntile):
                ghost_jobs.append((idx_dr, out_dr, kt, cols))

        for ji, (idx_dr, out_dr, kt, cols) in enumerate(ghost_jobs):
            @blk.gpsimd
            def _(g, ji=ji, idx_dr=idx_dr, kt=kt, cols=cols):
                if ji >= 2:
                    g.wait_ge(ev.h, ji - 1)
                gld.bump(g.dma_start(
                    out=idxb[ji % 2][:, 0:cols],
                    in_=idx_dr[kt * 128:(kt + 1) * 128, :]), 16)

            @blk.vector
            def _(v, ji=ji, cols=cols):
                v.wait_ge(gld.h, 32 + 16 * (ji + 1))
                if ji >= 2:
                    v.wait_ge(st.h, 16 * (ji - 1))
                ib = idxb[ji % 2][:, 0:cols]
                ab = accb[ji % 2][:, 0:cols]
                mk = maskb[:, 0:cols]
                v.tensor_scalar(ab, ib, 0.0, lut_sb[:, 0:1], ALU.is_equal, ALU.mult)
                last = None
                for j in range(1, 16):
                    v.tensor_scalar(mk, ib, float(j), None, ALU.is_equal)
                    last = v.scalar_tensor_tensor(ab, mk, lut_sb[:, j:j + 1], ab,
                                                  ALU.mult, ALU.add)
                ev.bump(last)

            @blk.sync
            def _(s, ji=ji, out_dr=out_dr, kt=kt, cols=cols):
                s.wait_ge(ev.h, ji + 1)
                st.bump(s.dma_start(out=out_dr[kt * 128:(kt + 1) * 128, :],
                                    in_=accb[ji % 2][:, 0:cols]), 16)
        NGHOST = len(ghost_jobs)

        def emit_wprime(w_in, w_out, ntile, cols, nbf_stages):
            for kt in range(ntile):
                @blk.gpsimd
                def _(g, kt=kt, cols=cols, w_in=w_in, evn=ev.n):
                    g.wait_ge(ev.h, evn)
                    g.wait_ge(mm.h, mm.n)
                    gld.bump(g.dma_start(
                        out=idxb[kt % 2][:, 0:cols],
                        in_=w_in[kt * 128:(kt + 1) * 128, :]), 16)

                @blk.tensor
                def _(t, kt=kt, cols=cols):
                    t.wait_ge(gld.h, gld.n)
                    t.wait_ge(ev.h, ev.n)
                    nch = (cols + 511) // 512
                    last = None
                    for n in range(nch):
                        lo, hi = n * 512, min((n + 1) * 512, cols)
                        last = t.matmul(ps_h[n % 2][:, 0:hi - lo], h128b[:, :],
                                        idxb[kt % 2][:, lo:hi], start=True, stop=True)
                    mm.bump(last)

                @blk.vector
                def _(v, kt=kt, cols=cols, stn=st.n):
                    if kt == 0:
                        v.wait_ge(st.h, stn)
                    v.wait_ge(mm.h, mm.n)
                    nch = (cols + 511) // 512
                    lastv = None
                    for n in range(nch):
                        lo, hi = n * 512, min((n + 1) * 512, cols)
                        lastv = v.tensor_copy(z_sb[:, lo:hi], ps_h[n % 2][:, 0:hi - lo])
                    ev.bump(lastv)

                @blk.sync
                def _(s, kt=kt, cols=cols, bdn=bd.n):
                    if kt == 0:
                        s.wait_ge(bd.h, bdn)
                    s.wait_ge(ev.h, ev.n)
                    bd.bump(s.dma_start(out=zD[kt * 128:(kt + 1) * 128, 0:cols],
                                        in_=z_sb[:, 0:cols]), 16)

            @blk.sync
            def _(s, ntile=ntile, cols=cols, stn=st.n):
                s.wait_ge(st.h, stn)
                s.wait_ge(bd.h, bd.n)
                i = s.dma_start(
                    out=z2_sb[:, 0:ntile * cols],
                    in_=zD[0:ntile * 128, 0:cols].rearrange("(j p) o -> p j o", p=128))
                bd.bump(i, 16)

            @blk.vector
            def _(v, ntile=ntile, cols=cols, nbf=nbf_stages):
                v.wait_ge(bd.h, bd.n)
                hstep = 1
                for s_i in range(nbf):
                    ngrp = ntile // (2 * hstep)
                    apA = bass.AP(tensor=z2_sb, offset=0,
                                  ap=[[z2_sb.shape[1], 128],
                                      [2 * hstep * cols, ngrp], [1, hstep * cols]])
                    apB = bass.AP(tensor=z2_sb, offset=hstep * cols,
                                  ap=[[z2_sb.shape[1], 128],
                                      [2 * hstep * cols, ngrp], [1, hstep * cols]])
                    apA2 = bass.AP(tensor=z_sb, offset=0,
                                   ap=[[z_sb.shape[1], 128],
                                       [2 * hstep * cols, ngrp], [1, hstep * cols]])
                    apB2 = bass.AP(tensor=z_sb, offset=hstep * cols,
                                   ap=[[z_sb.shape[1], 128],
                                       [2 * hstep * cols, ngrp], [1, hstep * cols]])
                    srcA, srcB = (apA, apB) if s_i % 2 == 0 else (apA2, apB2)
                    dstA, dstB = (apA2, apB2) if s_i % 2 == 0 else (apA, apB)
                    v.tensor_tensor(dstA, srcA, srcB, ALU.add)
                    last = v.tensor_tensor(dstB, srcA, srcB, ALU.subtract)
                    hstep *= 2
                ev.bump(last)

            final_buf = z2_sb if (nbf_stages % 2 == 0) else z_sb

            @blk.sync
            def _(s, ntile=ntile, cols=cols, w_out=w_out, final_buf=final_buf):
                s.wait_ge(ev.h, ev.n)
                i = s.dma_start(
                    out=w_out[0:ntile * 128, 0:cols].rearrange("(j p) o -> p j o", p=128),
                    in_=bass.AP(tensor=final_buf, offset=0,
                                ap=[[final_buf.shape[1], 128], [cols, ntile], [1, cols]]))
                st.bump(i, 16)

        emit_wprime(wupT, wupP, 16, UPC, 4)
        emit_wprime(wgtT, wgtP, 16, UPC, 4)
        emit_wprime(wdnT, wdnP, 64, DNC, 6)

        @blk.sync
        def _(s):
            s.wait_ge(st.h, st.n)

        sems.close()
    es.close()
    return nc


def host_prep_wprep(inputs):
    gT = np.ascontiguousarray(inputs["L0_ffn_gate_idx"].T)
    uT = np.ascontiguousarray(inputs["L0_ffn_up_idx"].T)
    dT = np.ascontiguousarray(inputs["L0_ffn_down_idx"].T)
    upT = np.ascontiguousarray(inputs["L1_ffn_W_up"].T)
    gtT = np.ascontiguousarray(inputs["L1_ffn_W_gate"].T)
    dnT = np.ascontiguousarray(inputs["L1_ffn_W_down"].T)
    lut = np.ascontiguousarray(
        np.broadcast_to(inputs["L0_ffn_lut"][None, :], (128, 16))).astype(np.float32)
    H = np.array([[1.0]], np.float32)
    while H.shape[0] < 128:
        H = np.block([[H, H], [H, -H]])
    maps = []
    for c in range(NCORES):
        maps.append({
            "gidx": np.ascontiguousarray(gT[:, c * GC:(c + 1) * GC]),
            "uidx": np.ascontiguousarray(uT[:, c * GC:(c + 1) * GC]),
            "didx": np.ascontiguousarray(dT[:, c * DC:(c + 1) * DC]),
            "lutb": lut,
            "h128": np.ascontiguousarray(H),
            "wupT": np.ascontiguousarray(upT[:, c * UPC:(c + 1) * UPC]),
            "wgtT": np.ascontiguousarray(gtT[:, c * UPC:(c + 1) * UPC]),
            "wdnT": np.ascontiguousarray(dnT[:, c * DNC:(c + 1) * DNC]),
        })
    return maps



def emit_l1_ffn(nc, blk_outer, io, mkctr=None, pfx="L"):
    ZTb, xresT, invn = io["ZTb"], io["xresT"], io["invn"]
    woP_ap = io["l1woP_ap"]
    wupP_ap, wgtP_ap, wdnP_ap = io["wupP_ap"], io["wgtP_ap"], io["wdnP_ap"]
    su_col, wsu_col, wsg_col = io["su_col"], io["wsu_col"], io["wsg_col"]
    sudp_col, wsd_col = io["sudp_col"], io["wsd_col"]
    x4T, x3T, rowsD = io["x4T"], io["x3T"], io["rowsD"]
    dbg_h, dbg_y0 = io.get("dbg_h"), io.get("dbg_y0")

    es = ExitStack()
    sb = lambda name, shape, dt: es.enter_context(nc.sbuf_tensor(name, shape, dt))
    ps = lambda name, shape, dt: es.enter_context(nc.psum_tensor(name, shape, dt))

    xr_sb = sb("xr_sb", [128, 16 * 512], F32)
    y_sb = sb("y_sb", [128, 16 * 512], F32)
    ybf = [sb(f"ybf{i}", [128, 512], BF16) for i in range(2)]
    x3b = sb("x3b", [128, 16 * 1024], BF16)
    h_sb = sb("h_sb", [128, 32 * 512], BF16)
    at_sb = sb("at_sb", [128, 16 * 512], BF16)
    wo_sb = [sb(f"wo_sb{i}", [128, 16 * 128], BF16) for i in range(2)]
    wg_sb = [sb(f"wg_sb{i}", [128, 16 * 128], BF16) for i in range(2)]
    wu_sb = [sb(f"wu_sb{i}", [128, 16 * 128], BF16) for i in range(2)]
    wd_sb = [sb(f"wd_sb{i}", [128, 32 * 128], BF16) for i in range(2)]
    sg_sb = sb("sg_sb", [128, 512], BF16)
    sq_sb = sb("sq_sb", [128, 512], BF16)
    gsc_sb = sb("gsc_sb", [128, 512], BF16)
    inv_sb = sb("inv_sb", [128, 1], BF16)
    su_sb = sb("su_sb", [128, 16], F32)
    wsu_sb = sb("wsu_sb", [128, 64], F32)
    wsg_sb = sb("wsg_sb", [128, 64], F32)
    sudp_sb = sb("sudp_sb", [128, 64], F32)
    wsd_sb = sb("wsd_sb", [128, 16], F32)
    sup_sb = sb("sup_sb", [128, 64], F32)
    r1 = sb("r1", [1, 512], F32)
    r2 = sb("r2", [1, 512], F32)
    rtmp = sb("rtmp", [1, 512], F32)
    mbb = sb("mbb", [128, 512], F32)
    rbb = sb("rbb", [128, 512], F32)
    x1f = [sb(f"x1f{i}", [128, 512], F32) for i in range(2)]

    ps_y = [ps(f"ps_y{i}", [128, 512], F32) for i in range(2)]
    ps_u = [ps(f"ps_u{i}", [128, 512], F32) for i in range(2)]
    ps_m = ps("ps_m", [1, 512], F32)
    ps_v = ps("ps_v", [1, 512], F32)

    RS2048 = float(1.0 / np.sqrt(2048.0))
    RS8192 = float(1.0 / np.sqrt(8192.0))

    if True:
        blk = blk_outer
        sems = ExitStack()
        if mkctr is None:
            mkctr = lambda name: Ctr(nc, sems, pfx + name)
        gld = mkctr("gld")
        sld = mkctr("sld")
        mm = mkctr("mm")
        ev = mkctr("ev")
        ac = mkctr("ac")
        st = mkctr("st")
        bc = mkctr("bc")

        @blk.gpsimd
        def _(g):
            gld.bump(g.dma_start(out=inv_sb[:, :], in_=invn[:, :]), 16)
            for dst, srcd in ((su_sb, su_col), (wsu_sb, wsu_col), (wsg_sb, wsg_col),
                              (sudp_sb, sudp_col), (wsd_sb, wsd_col)):
                gld.bump(g.dma_start(out=dst[:, :], in_=srcd[:, :]), 16)

        @blk.vector
        def _(v):
            v.wait_ge(gld.h, gld.n)
            v.tensor_tensor(sup_sb[:, :], wsu_sb[:, :], sudp_sb[:, :], ALU.mult)
            v.tensor_scalar(sup_sb[:, :], sup_sb[:, :], RS2048, None, ALU.mult)
            v.tensor_scalar(wsg_sb[:, :], wsg_sb[:, :], RS2048, None, ALU.mult)
            ev.bump(v.tensor_scalar(wsd_sb[:, :], wsd_sb[:, :], RS8192, None,
                                    ALU.mult))
        SCALES_EV = ev.n

        def emit_stats(oc, src_bf):
            @blk.scalar
            def _(sc, oc=oc, src_bf=src_bf):
                sc.wait_ge(ev.h, ev.n)
                sc.wait_ge(mm.h, mm.n)
                ac.bump(sc.activation(sq_sb[:, :], src_bf[:, :], AF.Square))

            @blk.tensor
            def _(t, oc=oc, src_bf=src_bf):
                t.wait_ge(ev.h, ev.n)
                t.matmul(ps_m[:, :], inv_sb[:, :], src_bf[:, :],
                         start=(oc == 0), stop=(oc == 15))
                t.wait_ge(ac.h, ac.n)
                mm.bump(t.matmul(ps_v[:, :], inv_sb[:, :], sq_sb[:, :],
                                 start=(oc == 0), stop=(oc == 15)))

        def emit_ln(th, out_mode):
            @blk.vector
            def _(v):
                v.wait_ge(mm.h, mm.n)
                v.tensor_copy(r1[:, :], ps_m[:, :])
                v.tensor_copy(r2[:, :], ps_v[:, :])
                v.tensor_tensor(rtmp[0:1, :], r1[0:1, :], r1[0:1, :], ALU.mult)
                v.tensor_tensor(r2[0:1, :], r2[0:1, :], rtmp[0:1, :], ALU.subtract)
                ev.bump(v.tensor_scalar(r2[0:1, :], r2[0:1, :], 1e-5, None, ALU.add))

            @blk.scalar
            def _(sc):
                sc.wait_ge(ev.h, ev.n)
                ac.bump(sc.activation(r2[0:1, :], r2[0:1, :], AF.Sqrt))

            @blk.vector
            def _(v):
                v.wait_ge(ac.h, ac.n)
                ev.bump(v.reciprocal(r2[0:1, :], r2[0:1, :]))

            @blk.gpsimd
            def _(g):
                g.wait_ge(ev.h, ev.n)
                g.dma_start(out=rowsD[0:1, :], in_=r1[0:1, :]).then_inc(bc.h, 16)
                g.dma_start(out=rowsD[1:2, :], in_=r2[0:1, :]).then_inc(bc.h, 16)
                g.wait_ge(bc.h, bc.n + 32)
                bm = bass.AP(tensor=rowsD, offset=0, ap=[[0, 128], [1, 512]])
                br = bass.AP(tensor=rowsD, offset=512, ap=[[0, 128], [1, 512]])
                g.dma_start(out=mbb[:, :], in_=bm).then_inc(bc.h, 16)
                g.dma_start(out=rbb[:, :], in_=br).then_inc(bc.h, 16)
            bc.n += 64

            for oc in range(16):
                @blk.vector
                def _(v, oc=oc, th=th, out_mode=out_mode):
                    if oc == 0:
                        v.wait_ge(bc.h, bc.n)
                    if oc >= 2:
                        v.wait_ge(st.h, st.n - 16)
                    xf = x1f[oc % 2]
                    v.tensor_tensor(xf[:, :], y_sb[:, oc * 512:(oc + 1) * 512],
                                    mbb[:, :], ALU.subtract)
                    last = v.tensor_tensor(xf[:, :], xf[:, :], rbb[:, :], ALU.mult)
                    if out_mode == "x3":
                        last = v.tensor_scalar(
                            x3b[:, oc * 1024 + th * 512: oc * 1024 + th * 512 + 512],
                            xf[:, :], su_sb[:, oc:oc + 1], None, ALU.mult)
                    ev.bump(last)

                @blk.scalar
                def _(sc, oc=oc, th=th, out_mode=out_mode):
                    sc.wait_ge(ev.h, ev.n)
                    dstd = x3T if out_mode == "x3" else x4T
                    st.bump(sc.dma_start(
                        out=dstd[oc * 128:(oc + 1) * 128, th * 512:th * 512 + 512],
                        in_=x1f[oc % 2][:, :]), 16)

        for th in range(2):
            t0 = th * 512

            @blk.sync
            def _(s, th=th, t0=t0):
                if th == 1:
                    s.wait_ge(mm.h, mm.n)
                for k in range(16):
                    sld.bump(s.dma_start(
                        out=at_sb[:, k * 512:(k + 1) * 512],
                        in_=ZTb[k * 128:(k + 1) * 128, t0:t0 + 512]), 16)
                for k in range(16):
                    sld.bump(s.dma_start(
                        out=xr_sb[:, k * 512:(k + 1) * 512],
                        in_=xresT[k * 128:(k + 1) * 128, t0:t0 + 512]), 16)
            LD_A = sld.n

            @blk.gpsimd
            def _(g, th=th):
                if th == 1:
                    g.wait_ge(mm.h, mm.n)
                for oc in range(2):
                    gld.bump(g.dma_start(
                        out=wo_sb[oc][:, :], in_=woP_ap(oc)), 16)
            GP_A = gld.n

            for oc in range(16):
                @blk.gpsimd
                def _(g, oc=oc):
                    if oc + 2 < 16:
                        g.wait_ge(mm.h, mm.n + 1)
                        gld.bump(g.dma_start(
                            out=wo_sb[oc % 2][:, :], in_=woP_ap(oc + 2)), 16)

                @blk.tensor
                def _(t, oc=oc, GP_A=GP_A, LD_A=LD_A):
                    if oc == 0:
                        t.wait_ge(sld.h, LD_A)
                    t.wait_ge(gld.h, GP_A + 16 * max(0, oc - 1))
                    if oc >= 2:
                        t.wait_ge(ev.h, ev.n)
                    last = None
                    for k in range(16):
                        last = t.matmul(ps_y[oc % 2][:, :],
                                        wo_sb[oc % 2][:, k * 128:(k + 1) * 128],
                                        at_sb[:, k * 512:(k + 1) * 512],
                                        start=(k == 0), stop=(k == 15))
                    mm.bump(last)

                @blk.vector
                def _(v, oc=oc):
                    v.wait_ge(mm.h, mm.n)
                    v.tensor_tensor(y_sb[:, oc * 512:(oc + 1) * 512],
                                    ps_y[oc % 2][:, :],
                                    xr_sb[:, oc * 512:(oc + 1) * 512], ALU.add)
                    ev.bump(v.tensor_copy(ybf[oc % 2][:, :],
                                          y_sb[:, oc * 512:(oc + 1) * 512]))

                emit_stats(oc, ybf[oc % 2])

            emit_ln(th, "x3")

            for ioh in range(2):
                @blk.sync
                def _(s, ioh=ioh, th=th):
                    s.wait_ge(st.h, st.n)
                    for buf in range(2):
                        sld.bump(s.dma_start(
                            out=wg_sb[buf][:, :], in_=wgtP_ap(ioh * 32 + buf)), 16)
                        sld.bump(s.dma_start(
                            out=wu_sb[buf][:, :], in_=wupP_ap(ioh * 32 + buf)), 16)
                LD_B = sld.n

                for io in range(32):
                    gio = ioh * 32 + io

                    @blk.sync
                    def _(s, io=io, gio=gio):
                        if io + 2 < 32:
                            s.wait_ge(mm.h, mm.n + 2)
                            sld.bump(s.dma_start(
                                out=wg_sb[io % 2][:, :], in_=wgtP_ap(gio + 2)), 16)
                            sld.bump(s.dma_start(
                                out=wu_sb[io % 2][:, :], in_=wupP_ap(gio + 2)), 16)

                    @blk.tensor
                    def _(t, io=io, th=th, LD_B=LD_B):
                        t.wait_ge(sld.h, LD_B + 32 * max(0, io - 1))
                        t.wait_ge(ev.h, ev.n)
                        last = None
                        for k in range(16):
                            last = t.matmul(
                                ps_y[io % 2][:, :],
                                wg_sb[io % 2][:, k * 128:(k + 1) * 128],
                                x3b[:, k * 1024 + th * 512: k * 1024 + th * 512 + 512],
                                start=(k == 0), stop=(k == 15))
                        mm.bump(last)
                        last = None
                        for k in range(16):
                            last = t.matmul(
                                ps_u[io % 2][:, :],
                                wu_sb[io % 2][:, k * 128:(k + 1) * 128],
                                x3b[:, k * 1024 + th * 512: k * 1024 + th * 512 + 512],
                                start=(k == 0), stop=(k == 15))
                        mm.bump(last)

                    @blk.vector
                    def _(v, io=io, gio=gio):
                        v.wait_ge(mm.h, mm.n - 1)
                        v.wait_ge(ac.h, ac.n)
                        ev.bump(v.tensor_scalar(gsc_sb[:, :], ps_y[io % 2][:, :],
                                                wsg_sb[:, gio:gio + 1], None,
                                                ALU.mult))

                    @blk.scalar
                    def _(sc, io=io):
                        sc.wait_ge(ev.h, ev.n)
                        ac.bump(sc.activation(sg_sb[:, :], gsc_sb[:, :], AF.Silu))

                    @blk.vector
                    def _(v, io=io, gio=gio):
                        v.wait_ge(mm.h, mm.n)
                        v.wait_ge(ac.h, ac.n)
                        ev.bump(v.scalar_tensor_tensor(
                            h_sb[:, io * 512:(io + 1) * 512],
                            ps_u[io % 2][:, :], sup_sb[:, gio:gio + 1], sg_sb[:, :],
                            ALU.mult, ALU.mult))

                @blk.sync
                def _(s, ioh=ioh):
                    for buf in range(2):
                        sld.bump(s.dma_start(
                            out=wd_sb[buf][:, :], in_=wdnP_ap(buf, ioh)), 16)
                LD_D = sld.n

                for oc in range(16):
                    @blk.sync
                    def _(s, oc=oc, ioh=ioh):
                        if oc + 2 < 16:
                            s.wait_ge(mm.h, mm.n + 1)
                            sld.bump(s.dma_start(
                                out=wd_sb[oc % 2][:, :], in_=wdnP_ap(oc + 2, ioh)), 16)

                    @blk.tensor
                    def _(t, oc=oc, LD_D=LD_D):
                        t.wait_ge(sld.h, LD_D + 16 * max(0, oc - 1))
                        if oc == 0:
                            t.wait_ge(ev.h, ev.n)
                        if oc >= 2:
                            t.wait_ge(ev.h, ev.n)
                        last = None
                        for io in range(32):
                            last = t.matmul(ps_y[oc % 2][:, :],
                                            wd_sb[oc % 2][:, io * 128:(io + 1) * 128],
                                            h_sb[:, io * 512:(io + 1) * 512],
                                            start=(io == 0), stop=(io == 31))
                        mm.bump(last)

                    @blk.vector
                    def _(v, oc=oc, ioh=ioh):
                        v.wait_ge(mm.h, mm.n)
                        if ioh == 0:
                            ev.bump(v.tensor_copy(y_sb[:, oc * 512:(oc + 1) * 512],
                                                  ps_y[oc % 2][:, :]))
                        else:
                            ev.bump(v.tensor_tensor(y_sb[:, oc * 512:(oc + 1) * 512],
                                                    y_sb[:, oc * 512:(oc + 1) * 512],
                                                    ps_y[oc % 2][:, :], ALU.add))

            if dbg_h is not None and th == 0:
                @blk.sync
                def _(s):
                    s.wait_ge(ev.h, ev.n)
                    st.bump(s.dma_start(out=dbg_h[:, :], in_=h_sb[:, :]), 16)
                    st.bump(s.dma_start(out=dbg_y0[:, :], in_=y_sb[:, :]), 16)

            @blk.sync
            def _(s, th=th):
                s.wait_ge(st.h, st.n)
                for k in range(16):
                    sld.bump(s.dma_start(
                        out=xr_sb[:, k * 512:(k + 1) * 512],
                        in_=x3T[k * 128:(k + 1) * 128, th * 512:th * 512 + 512]), 16)
            LD_R = sld.n

            for oc in range(16):
                @blk.vector
                def _(v, oc=oc, LD_R=LD_R):
                    if oc == 0:
                        v.wait_ge(sld.h, LD_R)
                        v.wait_ge(st.h, st.n)
                    if oc >= 2:
                        v.wait_ge(mm.h, mm.n - 1)
                    v.scalar_tensor_tensor(
                        y_sb[:, oc * 512:(oc + 1) * 512],
                        y_sb[:, oc * 512:(oc + 1) * 512], wsd_sb[:, oc:oc + 1],
                        xr_sb[:, oc * 512:(oc + 1) * 512], ALU.mult, ALU.add)
                    ev.bump(v.tensor_copy(ybf[oc % 2][:, :],
                                          y_sb[:, oc * 512:(oc + 1) * 512]))

                emit_stats(oc, ybf[oc % 2])

            emit_ln(th, "x4")

        @blk.scalar
        def _(sc):
            sc.wait_ge(st.h, st.n)

        sems.close()
    es.close()
    return st


def build_l1_ffn(dbg=False):
    nc = bass.Bass("TRN2", target_bir_lowering=False, debug=False,
                   detect_race_conditions=False)
    woP = nc.dram_tensor("woP", [16, 16, 128, 128], F32, kind="ExternalInput")
    wupP = nc.dram_tensor("wupP", [64, 16, 128, 128], BF16, kind="ExternalInput")
    wgtP = nc.dram_tensor("wgtP", [64, 16, 128, 128], BF16, kind="ExternalInput")
    wdnP = nc.dram_tensor("wdnP", [16, 64, 128, 128], BF16, kind="ExternalInput")
    io = {
        "ZTb": nc.dram_tensor("ZTb", [D, TOK], BF16, kind="ExternalInput"),
        "xresT": nc.dram_tensor("xresT", [D, TOK], F32, kind="ExternalInput"),
        "su_col": nc.dram_tensor("su_col", [128, 16], F32, kind="ExternalInput"),
        "wsu_col": nc.dram_tensor("wsu_col", [128, 64], F32, kind="ExternalInput"),
        "wsg_col": nc.dram_tensor("wsg_col", [128, 64], F32, kind="ExternalInput"),
        "sudp_col": nc.dram_tensor("sudp_col", [128, 64], F32, kind="ExternalInput"),
        "wsd_col": nc.dram_tensor("wsd_col", [128, 16], F32, kind="ExternalInput"),
        "invn": nc.dram_tensor("invn", [128, 1], F32, kind="ExternalInput"),
        "x4T": nc.dram_tensor("x4T", [D, TOK], F32, kind="ExternalOutput"),
        "x3T": nc.dram_tensor("x3T", [D, TOK], F32, kind="ExternalOutput"),
        "dbg_h": nc.dram_tensor("dbg_h", [128, 32 * 512], BF16,
                                kind="ExternalOutput" if dbg else "Internal"),
        "dbg_y0": nc.dram_tensor("dbg_y0", [128, 16 * 512], F32,
                                 kind="ExternalOutput" if dbg else "Internal"),
        "rowsD": nc.dram_tensor("rowsD", [2, 512], F32),
        "l1woP_ap": lambda i: woP[i].rearrange("k p c -> p k c"),
        "wupP_ap": lambda i: wupP[i].rearrange("k p c -> p k c"),
        "wgtP_ap": lambda i: wgtP[i].rearrange("k p c -> p k c"),
        "wdnP_ap": lambda oc, ioh: wdnP[oc, ioh * 32:(ioh + 1) * 32].rearrange(
            "k p c -> p k c"),
    }
    with nc.Block() as blk:
        emit_l1_ffn(nc, blk, io)
    return nc


def host_prep_l1_ffn(inputs, ZT_full, x2_full):
    woT = np.ascontiguousarray(inputs["L1_attn_Wo"].T)
    woP = np.ascontiguousarray(woT.reshape(16, 128, 16, 128).transpose(2, 0, 1, 3))
    invn = np.full((128, 1), 1.0 / 2048.0, np.float32)
    su_col = np.ascontiguousarray(
        inputs["L1_ffn_SU_ug"].reshape(16, 128).T).astype(np.float32)
    wsu_col = np.ascontiguousarray(
        inputs["L1_ffn_Wscale_ug"][:8192].reshape(64, 128).T).astype(np.float32)
    wsg_col = np.ascontiguousarray(
        inputs["L1_ffn_Wscale_ug"][8192:].reshape(64, 128).T).astype(np.float32)
    sudp_col = np.ascontiguousarray(
        inputs["L1_ffn_SU_dp"].reshape(64, 128).T).astype(np.float32)
    wsd_col = np.ascontiguousarray(
        inputs["L1_ffn_Wscale_dp"].reshape(16, 128).T).astype(np.float32)
    maps = []
    for c in range(NCORES):
        b, half = c // 2, c % 2
        tc = slice(half * 1024, half * 1024 + 1024)
        maps.append({
            "ZTb": np.ascontiguousarray(ZT_full[b][:, tc]),
            "xresT": np.ascontiguousarray(x2_full[b][:, tc]),
            "woP": woP, "invn": invn,
            "su_col": su_col, "wsu_col": wsu_col, "wsg_col": wsg_col,
            "sudp_col": sudp_col, "wsd_col": wsd_col,
        })
    return maps


def panelize_l1_w(wupP_full, wgtP_full, wdnP_full):
    up = np.ascontiguousarray(
        wupP_full.reshape(16, 128, 64, 128).transpose(2, 0, 1, 3))
    gt = np.ascontiguousarray(
        wgtP_full.reshape(16, 128, 64, 128).transpose(2, 0, 1, 3))
    dn = np.ascontiguousarray(
        wdnP_full.reshape(64, 128, 16, 128).transpose(2, 0, 1, 3))
    return up, gt, dn



def emit_l1_attn(nc, blk_outer, io, mkctr=None, nch=128, SCAN_PARTS=5, pfx="S"):
    x2Tb, wqT, wkT, wvT = io["x2Tb"], io["l1wqT"], io["l1wkT"], io["l1wvT"]
    wilrT, bilr, w1in, pen = io["wilrT"], io["bilr"], io["w1in"], io["pen"]
    ident, trimask, Zraw, ilrD = io["ident"], io["trimask"], io["Zraw"], io["ilrD"]

    es = ExitStack()
    sb = lambda name, shape, dt: es.enter_context(nc.sbuf_tensor(pfx + name, shape, dt))
    ps = lambda name, shape, dt: es.enter_context(nc.psum_tensor(pfx + name, shape, dt))

    xb_t = sb("xb_t", [128, 8 * S], BF16)
    xc_t = sb("xc_t", [128, 8 * S], BF16)
    xa_t = sb("xa_t", [128, 8 * S], BF16)
    alias = sb("alias", [128, 16 * S], BF16)
    ilr_sb = sb("ilr_sb", [32, S], F32)
    penb = sb("penb", [32, S], F32)
    s_sb = sb("s_sb", [128, S], F32)
    w1i_sb = sb("w1i_sb", [128, 512], F32)
    id_f = sb("id_f", [128, 128], F32)
    id_b = sb("id_b", [128, 128], BF16)
    tri_b = sb("tri_b", [128, 128], BF16)
    bilr_sb = sb("bilr_sb", [16, 1], F32)
    wq_sb = sb("wq_sb", [128, 16 * 128], BF16)
    wk_sb = sb("wk_sb", [128, 16 * 128], BF16)
    wv_sb = sb("wv_sb", [128, 16 * 128], BF16)
    wi_sb = sb("wi_sb", [128, 16 * 16], BF16)
    w1sb = sb("w1sb", [128, 512], BF16)
    z1sb = sb("z1sb", [128, 256], BF16)
    z1n = sb("z1n", [128, 256], BF16)
    xbn = sb("xbn", [128, 256], BF16)
    at_s = sb("at_s", [128, 128], BF16)
    gbuf = sb("gbuf", [128, 256], BF16)
    zbuf = [sb(f"zbuf{i}", [128, 128], BF16) for i in range(2)]

    ps_w1 = ps("ps_w1", [128, 512], F32)
    ps_z1 = ps("ps_z1", [128, 512], F32)
    ps_zb = ps("ps_zb", [128, 512], F32)
    ps_at = ps("ps_at", [128, 512], F32)
    ps_az = ps("ps_az", [128, 512], F32)
    ps_t1 = ps("ps_t1", [128, 512], BF16)
    ps_t2 = ps("ps_t2", [128, 512], BF16)

    XCp = alias
    XBp_OFF = 8 * S

    if True:
        blk = blk_outer
        sems = ExitStack()
        if mkctr is None:
            mkctr = lambda name: Ctr(nc, sems, pfx + name)
        gld = mkctr("gld")
        mm = mkctr("mm")
        ev = mkctr("ev")
        ac = mkctr("ac")
        st = mkctr("st")
        bc = mkctr("bc")

        @blk.gpsimd
        def _(g):
            for k in range(16):
                gld.bump(g.dma_start(out=alias[:, k * S:(k + 1) * S],
                                     in_=x2Tb[k * 128:(k + 1) * 128, :]), 16)
            gld.bump(g.dma_start(out=w1i_sb[:, :], in_=w1in[:, :]), 16)
            gld.bump(g.dma_start(out=id_f[:, :], in_=ident[:, :]), 16)
            gld.bump(g.dma_start(out=id_b[:, :], in_=ident[:, :]), 16)
            gld.bump(g.dma_start(out=tri_b[:, :], in_=trimask[:, :]), 16)
            gld.bump(g.dma_start(out=bilr_sb[:, :], in_=bilr[:, :]), 16)
            pb = bass.AP(tensor=pen, offset=0, ap=[[0, 32], [1, S]])
            gld.bump(g.dma_start(out=penb[:, :], in_=pb), 16)
            for k in range(16):
                gld.bump(g.dma_start(out=wi_sb[:, k * 16:(k + 1) * 16],
                                     in_=wilrT[k * 128:(k + 1) * 128, :]), 16)
        LD0 = gld.n

        for p in range(8):
            @blk.gpsimd
            def _(g, p=p):
                if p > 0:
                    g.wait_ge(mm.h, mm.n)
                for w_sb, w_dr in ((wq_sb, wqT), (wk_sb, wkT), (wv_sb, wvT)):
                    for k in range(16):
                        gld.bump(g.dma_start(
                            out=w_sb[:, k * 128:(k + 1) * 128],
                            in_=w_dr[k * 128:(k + 1) * 128,
                                     p * 128:(p + 1) * 128]), 16)

            for wi, (w_sb, dst) in enumerate(((wq_sb, xc_t), (wk_sb, xb_t),
                                              (wv_sb, xa_t))):
                for nh2 in range(2):
                    @blk.tensor
                    def _(t, p=p, wi=wi, w_sb=w_sb, nh2=nh2):
                        t.wait_ge(gld.h, gld.n)
                        if not (p == 0 and wi == 0 and nh2 == 0):
                            t.wait_ge(ev.h, ev.n)
                        last = None
                        for k in range(16):
                            for n in range(2):
                                c0 = k * S + nh2 * 1024 + n * 512
                                last = t.matmul((ps_z1 if n == 0 else ps_zb)[:, 0:512],
                                                w_sb[:, k * 128:(k + 1) * 128],
                                                alias[:, c0:c0 + 512],
                                                start=(k == 0), stop=(k == 15))
                        mm.bump(last)

                    @blk.vector
                    def _(v, p=p, dst=dst, nh2=nh2):
                        v.wait_ge(mm.h, mm.n)
                        c0 = p * S + nh2 * 1024
                        v.tensor_copy(dst[:, c0:c0 + 512], ps_z1[:, 0:512])
                        ev.bump(v.tensor_copy(dst[:, c0 + 512:c0 + 1024],
                                              ps_zb[:, 0:512]))

        for ch4 in range(4):
            @blk.tensor
            def _(t, ch4=ch4):
                t.wait_ge(ev.h, ev.n)
                t.wait_ge(ac.h, ac.n)
                last = None
                for k in range(16):
                    c0 = k * S + ch4 * 512
                    last = t.matmul(ps_z1[0:16, 0:512],
                                    wi_sb[:, k * 16:(k + 1) * 16],
                                    alias[:, c0:c0 + 512],
                                    start=(k == 0), stop=(k == 15))
                mm.bump(last)

            @blk.scalar
            def _(sc, ch4=ch4):
                sc.wait_ge(mm.h, mm.n)
                ac.bump(sc.activation(ilr_sb[0:16, ch4 * 512:(ch4 + 1) * 512],
                                      ps_z1[0:16, 0:512], AF.Sigmoid,
                                      bias=bilr_sb[0:16, 0:1]))

        @blk.vector
        def _(v):
            v.wait_ge(ac.h, ac.n)
            ev.bump(v.tensor_tensor(ilr_sb[0:16, :], ilr_sb[0:16, :],
                                    penb[0:16, :], ALU.mult))

        @blk.gpsimd
        def _(g):
            g.wait_ge(ev.h, ev.n)
            bc.bump(g.dma_start(out=ilrD[:, :], in_=ilr_sb[0:16, :]), 16)

        for p in range(8):
            @blk.gpsimd
            def _(g, p=p):
                g.wait_ge(bc.h, 16)
                if p == 0:
                    g.wait_ge(ev.h, ev.n)
                else:
                    g.wait_ge(ev.h, ev.n)
                b0 = bass.AP(tensor=ilrD, offset=(2 * p) * S, ap=[[0, 64], [1, S]])
                b1 = bass.AP(tensor=ilrD, offset=(2 * p + 1) * S, ap=[[0, 64], [1, S]])
                bc.bump(g.dma_start(out=s_sb[0:64, :], in_=b0), 16)
                bc.bump(g.dma_start(out=s_sb[64:128, :], in_=b1), 16)

            @blk.vector
            def _(v, p=p):
                v.wait_ge(bc.h, bc.n)
                v.tensor_tensor(XCp[:, p * S:(p + 1) * S], xc_t[:, p * S:(p + 1) * S],
                                s_sb[:, :], ALU.mult)
                ev.bump(v.scalar_tensor_tensor(
                    alias[:, XBp_OFF + p * S: XBp_OFF + (p + 1) * S],
                    xb_t[:, p * S:(p + 1) * S], -1.0, s_sb[:, :],
                    ALU.mult, ALU.mult))

        @blk.vector
        def _(v):
            v.wait_ge(ev.h, ev.n)
            v.memset(gbuf[:, :], 0.0)
            ev.bump(v.memset(z1sb[:, :], 0.0))

        @blk.tensor
        def _(t):
            t.wait_ge(gld.h, LD0)
            t.wait_ge(ev.h, ev.n)
            mm.bump(t.matmul(ps_w1[:, :], id_f[:, :], w1i_sb[:, :],
                             start=True, stop=False, skip_group_check=True))

        def head_coords(p, s):
            return (p % 4) * 32, (p // 4) * 32 + s * 16

        for ch in range(nch):
            @blk.vector
            def _(v, ch=ch):
                v.wait_ge(mm.h, mm.n)
                v.tensor_copy(w1sb[:, :], ps_w1[:, :])
                last = None
                for g4 in range(2):
                    srcg = bass.AP(tensor=alias, offset=XBp_OFF + (g4 * 4) * S + ch * 16,
                                   ap=[[16 * S, 128], [S, 4], [1, 16]])
                    dstp = bass.AP(tensor=gbuf, offset=g4 * 128,
                                   ap=[[256, 128], [32, 4], [1, 16]])
                    last = v.tensor_copy(dstp, srcg)
                ev.bump(last)
            ev_w1 = ev.n

            @blk.tensor
            def _(t, ch=ch, ev_w1=ev_w1):
                t.wait_ge(ev.h, ev_w1)
                for _ in range(8):
                    t.nop()
                last = None
                for s2 in range(2):
                    for _ in range(8):
                        t.nop()
                    for p in range(8):
                        lhsT = w1sb[s2 * 64:(s2 + 1) * 64, p * 64:(p + 1) * 64]
                        t.matmul(ps_z1[s2 * 64:(s2 + 1) * 64, p * 32:p * 32 + 16],
                                 lhsT,
                                 xb_t[s2 * 64:(s2 + 1) * 64, p * S + ch * 16:
                                      p * S + ch * 16 + 16],
                                 start=True, stop=True, skip_group_check=True,
                                 tile_position=(s2 * 64, s2 * 64))
                        last = t.matmul(
                            ps_zb[s2 * 64:(s2 + 1) * 64, p * 32:p * 32 + 16],
                            lhsT,
                            xc_t[s2 * 64:(s2 + 1) * 64, p * S + ch * 16:
                                 p * S + ch * 16 + 16],
                            start=True, stop=True, skip_group_check=True,
                            tile_position=(s2 * 64, s2 * 64))
                mm.bump(last)
            mm_12 = mm.n

            @blk.vector
            def _(v, ch=ch, mm_12=mm_12):
                v.wait_ge(mm.h, mm_12)
                srcz = bass.AP(tensor=ps_z1, offset=0,
                               ap=[[512, 128], [32, 8], [1, 16]])
                srca = bass.AP(tensor=xa_t, offset=ch * 16,
                               ap=[[8 * S, 128], [S, 8], [1, 16]])
                dstz = bass.AP(tensor=z1sb, offset=0,
                               ap=[[256, 128], [32, 8], [1, 16]])
                ev.bump(v.tensor_tensor(dstz, srcz, srca, ALU.subtract))
            ev_z1 = ev.n

            if SCAN_PARTS >= 3:
                pass
            if SCAN_PARTS >= 3:
                @blk.tensor
                def _(t, ch=ch):
                    last = None
                    for s2 in range(2):
                        for _ in range(8):
                            t.nop()
                        for p in range(8):
                            pb_, cb = head_coords(p, s2)
                            last = t.matmul(
                                ps_at[pb_:pb_ + 16, cb:cb + 16],
                                xb_t[s2 * 64:(s2 + 1) * 64, p * S + ch * 16:
                                     p * S + ch * 16 + 16],
                                XCp[s2 * 64:(s2 + 1) * 64, p * S + ch * 16:
                                    p * S + ch * 16 + 16],
                                start=True, stop=True, skip_group_check=True,
                                tile_position=(s2 * 64, pb_))
                    mm.bump(last)
                mm_3 = mm.n

                if SCAN_PARTS >= 3.5:
                    @blk.vector
                    def _(v, ch=ch, mm_3=mm_3):
                        v.wait_ge(mm.h, mm_3)
                        last = None
                        for pb_ in (0, 32, 64, 96):
                            last = v.tensor_tensor(at_s[pb_:pb_ + 16, 0:64],
                                                   ps_at[pb_:pb_ + 16, 0:64],
                                                   tri_b[pb_:pb_ + 16, 0:64], ALU.mult)
                        ev.bump(last)
                ev_mk = ev.n

            if SCAN_PARTS >= 2:
                @blk.tensor
                def _(t, ch=ch, ev_z1=ev_z1):
                    t.wait_ge(ev.h, ev_z1)
                    for _ in range(8):
                        t.nop()
                    t.transpose(ps_t2[:, 0:128], gbuf[:, 0:128], id_b[:, :])
                    t.transpose(ps_t2[:, 128:256], gbuf[:, 128:256], id_b[:, :])
                    t.transpose(ps_t1[:, 0:128], z1sb[:, 0:128], id_b[:, :])
                    last = t.transpose(ps_t1[:, 128:256], z1sb[:, 128:256], id_b[:, :])
                    mm.bump(last)
                mm_tr = mm.n

                @blk.vector
                def _(v, ch=ch, mm_tr=mm_tr):
                    v.wait_ge(mm.h, mm_tr)
                    v.tensor_copy(xbn[:, :], ps_t2[:, 0:256])
                    ev.bump(v.tensor_copy(z1n[:, :], ps_t1[:, 0:256]))
                ev_tr = ev.n

            if SCAN_PARTS >= 4:
                @blk.tensor
                def _(t, ch=ch, ev_tr=ev_tr, ev_mk=ev_mk):
                    t.wait_ge(ev.h, max(ev_tr, ev_mk))
                    for _ in range(8):
                        t.nop()
                    last = None
                    for po in range(4):
                        for _ in range(8):
                            t.nop()
                        for g4 in range(2):
                            p = g4 * 4 + po
                            for s2 in range(2):
                                pb_, cb = head_coords(p, s2)
                                last = t.matmul(
                                    ps_az[s2 * 64:(s2 + 1) * 64, p * 32:p * 32 + 16],
                                z1n[po * 32:po * 32 + 16,
                                    g4 * 128 + s2 * 64:g4 * 128 + (s2 + 1) * 64],
                                at_s[pb_:pb_ + 16, cb:cb + 16],
                                start=True, stop=True, skip_group_check=True,
                                tile_position=(po * 32, s2 * 64))
                    mm.bump(last)
                mm_4 = mm.n

            if SCAN_PARTS >= 5:
                @blk.tensor
                def _(t, ch=ch):
                    last = None
                    for po in range(4):
                        for _ in range(8):
                            t.nop()
                        for g4 in range(2):
                            p = g4 * 4 + po
                            for s2 in range(2):
                                last = t.matmul(
                                    ps_w1[s2 * 64:(s2 + 1) * 64, p * 64:(p + 1) * 64],
                                xbn[po * 32:po * 32 + 16,
                                    g4 * 128 + s2 * 64:g4 * 128 + (s2 + 1) * 64],
                                z1n[po * 32:po * 32 + 16,
                                    g4 * 128 + s2 * 64:g4 * 128 + (s2 + 1) * 64],
                                start=False, stop=False, skip_group_check=True,
                                tile_position=(po * 32, s2 * 64))
                    mm.bump(last)

            mm_pre_evac = mm.n

            @blk.vector
            def _(v, ch=ch, mm_pre_evac=mm_pre_evac):
                v.wait_ge(mm.h, mm_pre_evac)
                if ch >= 2:
                    v.wait_ge(st.h, 16 * (ch - 1))
                srcz = bass.AP(tensor=ps_zb, offset=0,
                               ap=[[512, 128], [32, 8], [1, 16]])
                srca2 = bass.AP(tensor=ps_az, offset=0,
                                ap=[[512, 128], [32, 8], [1, 16]])
                dstz = bass.AP(tensor=zbuf[ch % 2], offset=0,
                               ap=[[128, 128], [16, 8], [1, 16]])
                v.tensor_copy(dstz, srcz)
                if SCAN_PARTS >= 4:
                    ev.bump(v.tensor_tensor(dstz, dstz, srca2, ALU.add))
                else:
                    ev.bump(v.tensor_copy(dstz, srcz))
            @blk.sync
            def _(s, ch=ch):
                s.wait_ge(ev.h, ev.n)
                st.bump(s.dma_start(out=Zraw[:, ch * 128:(ch + 1) * 128],
                                    in_=zbuf[ch % 2][:, :]), 16)

        @blk.sync
        def _(s):
            s.wait_ge(st.h, st.n)

        sems.close()
    es.close()
    return st


def build_l1_attn(nch=128, SCAN_PARTS=5):
    nc = bass.Bass("TRN2", target_bir_lowering=False, debug=False,
                   detect_race_conditions=False)
    io = {
        "x2Tb": nc.dram_tensor("x2Tb", [D, S], F32, kind="ExternalInput"),
        "l1wqT": nc.dram_tensor("wqT", [D, 1024], F32, kind="ExternalInput"),
        "l1wkT": nc.dram_tensor("wkT", [D, 1024], F32, kind="ExternalInput"),
        "l1wvT": nc.dram_tensor("wvT", [D, 1024], F32, kind="ExternalInput"),
        "wilrT": nc.dram_tensor("wilrT", [D, 16], F32, kind="ExternalInput"),
        "bilr": nc.dram_tensor("bilr", [16, 1], F32, kind="ExternalInput"),
        "w1in": nc.dram_tensor("w1in", [128, 512], F32, kind="ExternalInput"),
        "pen": nc.dram_tensor("pen", [1, S], F32, kind="ExternalInput"),
        "ident": nc.dram_tensor("ident", [128, 128], F32, kind="ExternalInput"),
        "trimask": nc.dram_tensor("trimask", [128, 128], F32, kind="ExternalInput"),
        "Zraw": nc.dram_tensor("Zraw", [128, 128 * 128], BF16, kind="ExternalOutput"),
        "ilrD": nc.dram_tensor("ilrD", [16, S], F32),
    }
    with nc.Block() as blk:
        emit_l1_attn(nc, blk, io, nch=nch, SCAN_PARTS=SCAN_PARTS)
    return nc


def host_prep_l1_attn(inputs, x2_full):
    wqT = np.ascontiguousarray(inputs["L1_attn_Wq"].T)
    wkT = np.ascontiguousarray(inputs["L1_attn_Wk"].T)
    wvT = np.ascontiguousarray(inputs["L1_attn_Wv"].T)
    wiT = np.ascontiguousarray(inputs["L1_attn_W_ilr"].T)
    bilr = inputs["L1_attn_b_ilr"].astype(np.float32)
    W1 = inputs["L1_attn_W1"].astype(np.float32)
    pen = (1.0 / (np.arange(S) % 16 + 1.0) / 64.0).astype(np.float32)[None, :]
    ident = np.eye(128, dtype=np.float32)
    tm = np.zeros((128, 128), np.float32)
    for p in range(8):
        for s2 in range(2):
            pb_, cb = (p % 4) * 32, (p // 4) * 32 + s2 * 16
            for d_ in range(16):
                for c_ in range(16):
                    if d_ <= c_:
                        tm[pb_ + d_, cb + c_] = -1.0
    maps = []
    for c in range(NCORES):
        b, half = c // 2, c % 2
        g0 = 16 * half
        cols = slice(g0 * 64, g0 * 64 + 1024)
        w1v = np.zeros((128, 512), np.float32)
        for p in range(8):
            for s2 in range(2):
                w1v[s2 * 64:(s2 + 1) * 64, p * 64:(p + 1) * 64] = W1[g0 + 2 * p + s2]
        maps.append({
            "x2Tb": np.ascontiguousarray(x2_full[b]),
            "wqT": np.ascontiguousarray(wqT[:, cols]),
            "wkT": np.ascontiguousarray(wkT[:, cols]),
            "wvT": np.ascontiguousarray(wvT[:, cols]),
            "wilrT": np.ascontiguousarray(wiT[:, g0:g0 + 16]),
            "bilr": np.ascontiguousarray(bilr[g0:g0 + 16].reshape(16, 1)),
            "w1in": w1v,
            "pen": pen, "ident": ident, "trimask": tm,
        })
    return maps


def zraw_to_zt(zraw):
    z = np.asarray(zraw).reshape(128, 128, 8, 16)
    return np.ascontiguousarray(z.transpose(2, 0, 1, 3).reshape(1024, 2048))





from concourse.bass_core_rust import IndirectOffsetOnAxis

MAGICF = 1597000001
POLL_ITERS = 20000
GCW = 2816
NBAR = 3


def build_fused():
    nc = bass.Bass("TRN2", target_bir_lowering=False, debug=False,
                   detect_race_conditions=False)

    xTb = nc.dram_tensor("xTb", [D, S], F32, kind="ExternalInput")
    wqT = nc.dram_tensor("wqT", [D, 1024], F32, kind="ExternalInput")
    wkT = nc.dram_tensor("wkT", [D, 1024], F32, kind="ExternalInput")
    wvT = nc.dram_tensor("wvT", [D, 1024], F32, kind="ExternalInput")
    triu = nc.dram_tensor("triu", [128, 128], F32, kind="ExternalInput")
    onesc = nc.dram_tensor("onesc", [128, 128], F32, kind="ExternalInput")
    xresT = nc.dram_tensor("xresT", [D, TOK], F32, kind="ExternalInput")
    woP = nc.dram_tensor("woP", [16, 16, 128, 128], F32, kind="ExternalInput")
    invn = nc.dram_tensor("invn", [128, 1], F32, kind="ExternalInput")
    gidxF = nc.dram_tensor("gidxF", [D, GCW], I32, kind="ExternalInput")
    uidxF = nc.dram_tensor("uidxF", [D, GCW], I32, kind="ExternalInput")
    didxF = nc.dram_tensor("didxF", [GCW, D], I32, kind="ExternalInput")
    lutb = nc.dram_tensor("lutb", [128, 16], F32, kind="ExternalInput")
    h128 = nc.dram_tensor("h128", [128, 128], F32, kind="ExternalInput")
    wupTF = nc.dram_tensor("wupTF", [D, 4096], F32, kind="ExternalInput")
    wgtTF = nc.dram_tensor("wgtTF", [D, 4096], F32, kind="ExternalInput")
    wdnTF = nc.dram_tensor("wdnTF", [8192, 1024], F32, kind="ExternalInput")
    zwqT = nc.dram_tensor("zwqT", [D, 1024], F32, kind="ExternalInput")
    zwkT = nc.dram_tensor("zwkT", [D, 1024], F32, kind="ExternalInput")
    zwvT = nc.dram_tensor("zwvT", [D, 1024], F32, kind="ExternalInput")
    wilrT = nc.dram_tensor("wilrT", [D, 16], F32, kind="ExternalInput")
    bilr = nc.dram_tensor("bilr", [16, 1], F32, kind="ExternalInput")
    w1in = nc.dram_tensor("w1in", [128, 512], F32, kind="ExternalInput")
    pen = nc.dram_tensor("pen", [1, S], F32, kind="ExternalInput")
    ident = nc.dram_tensor("ident", [128, 128], F32, kind="ExternalInput")
    trimask = nc.dram_tensor("trimask", [128, 128], F32, kind="ExternalInput")
    l1woP = nc.dram_tensor("l1woP", [16, 16, 128, 128], F32, kind="ExternalInput")
    su_col = nc.dram_tensor("su_col", [128, 16], F32, kind="ExternalInput")
    wsu_col = nc.dram_tensor("wsu_col", [128, 64], F32, kind="ExternalInput")
    wsg_col = nc.dram_tensor("wsg_col", [128, 64], F32, kind="ExternalInput")
    sudp_col = nc.dram_tensor("sudp_col", [128, 64], F32, kind="ExternalInput")
    wsd_col = nc.dram_tensor("wsd_col", [128, 16], F32, kind="ExternalInput")
    ix_g = nc.dram_tensor("ix_g", [128, 16], U32, kind="ExternalInput")
    ix_d = nc.dram_tensor("ix_d", [128, 22], U32, kind="ExternalInput")
    ix_up = nc.dram_tensor("ix_up", [128, 64], U32, kind="ExternalInput")
    ix_dn = nc.dram_tensor("ix_dn", [128, 256], U32, kind="ExternalInput")
    ix_att = nc.dram_tensor("ix_att", [128, 16], U32, kind="ExternalInput")
    gxi = nc.dram_tensor("gxi", [128, 16], U32, kind="ExternalInput")
    ix_flg = nc.dram_tensor("ix_flg", [128, NBAR], U32, kind="ExternalInput")
    flgv = nc.dram_tensor("flgv", [128, NBAR], I32, kind="ExternalInput")

    attnT = nc.dram_tensor("attnT", [1024, S], BF16)
    attnTb = nc.dram_tensor("attnTb", [D, TOK], BF16)
    x2T = nc.dram_tensor("x2T", [D, TOK], F32)
    x1T = nc.dram_tensor("x1T", [D, TOK], F32)
    rowsD = nc.dram_tensor("rowsD", [2, 512], F32)
    x2Tb = nc.dram_tensor("x2Tb", [D, S], F32)
    Zraw = nc.dram_tensor("Zraw", [128, 128 * 128], BF16)
    ilrD = nc.dram_tensor("ilrD", [16, S], F32)
    ZTb = nc.dram_tensor("ZTb", [D, TOK], BF16)
    x3T = nc.dram_tensor("x3T", [D, TOK], F32)
    zDf = nc.dram_tensor("zDf", [8192, 1024], BF16)
    dumD = nc.dram_tensor("dumD", [2, 16], F32)
    x4T = nc.dram_tensor("x4T", [D, TOK], F32, kind="ExternalOutput")

    pool_g = nc.dram_tensor("pool_g", [4096, GCW], BF16, addr_space="Shared")
    pool_u = nc.dram_tensor("pool_u", [4096, GCW], BF16, addr_space="Shared")
    pool_d = nc.dram_tensor("pool_d", [5504, 2048], BF16, addr_space="Shared")
    pool_up = nc.dram_tensor("pool_up", [16384, 1024], BF16, addr_space="Shared")
    pool_gt = nc.dram_tensor("pool_gt", [16384, 1024], BF16, addr_space="Shared")
    pool_dn = nc.dram_tensor("pool_dn", [65536, 256], BF16, addr_space="Shared")
    pool_att = nc.dram_tensor("pool_att", [4096, 1024], BF16, addr_space="Shared")
    pool_x2 = nc.dram_tensor("pool_x2", [4096, 1024], F32, addr_space="Shared")
    pool_z = nc.dram_tensor("pool_z", [4096, 1024], BF16, addr_space="Shared")
    pool_flg = nc.dram_tensor("pool_flg", [64, 1], I32, addr_space="Shared")

    def wgP_pool_ap(pool):
        def f(i):
            if i <= 21:
                off = i * 128
            else:
                off = 2048 * GCW + (i - 21) * 128
            return bass.AP(tensor=pool, offset=off,
                           ap=[[GCW, 128], [128 * GCW, 16], [1, 128]])
        return f

    def wdP_pool_ap(oc):
        return bass.AP(tensor=pool_d, offset=oc * 128,
                       ap=[[2048, 128], [128 * 2048, NIO], [1, 128]])

    def wupP_pool_ap(pool):
        def f(gio):
            par, gi = gio // 32, gio % 32
            chunk, o8 = gi // 8, gi % 8
            off = par * 8192 * 1024 + chunk * 1024 + o8 * 128
            return bass.AP(tensor=pool, offset=off,
                           ap=[[4096, 128], [128 * 4096, 16], [1, 128]])
        return f

    def wdnP_pool_ap(oc, ioh):
        par, lc = oc // 8, oc % 8
        chunk, cb = lc // 2, (lc % 2) * 128
        off = ioh * 32 * 128 * 2048 + (par * 4 + chunk) * 256 + cb
        return bass.AP(tensor=pool_dn, offset=off,
                       ap=[[2048, 128], [128 * 2048, 32], [1, 128]])

    es0 = ExitStack()
    sbg = lambda name, shape, dt: es0.enter_context(nc.sbuf_tensor(name, shape, dt))
    ixg_sb = sbg("ixg_sb", [128, 16], U32)
    ixd_sb = sbg("ixd_sb", [128, 22], U32)
    ixup_sb = sbg("ixup_sb", [128, 64], U32)
    ixdn_sb = sbg("ixdn_sb", [128, 256], U32)
    ixatt_sb = sbg("ixatt_sb", [128, 16], U32)
    gx_sb = sbg("gx_sb", [128, 16], U32)
    ixflg_sb = sbg("ixflg_sb", [128, NBAR], U32)
    flgv_sb = sbg("flgv_sb", [128, NBAR], I32)
    flg4 = sbg("flg4", [4, 1], I32)
    dum = sbg("dum", [2, 16], F32)
    dum2 = sbg("dum2", [2, 16], F32)

    with nc.Block() as blk:
        sems0 = ExitStack()
        phc = sems0.enter_context(nc.semaphore("phc"))
        phs = sems0.enter_context(nc.semaphore("phs"))
        phg = sems0.enter_context(nc.semaphore("phg"))
        gpb = Ctr(nc, sems0, "gpb")
        plf = [sems0.enter_context(nc.semaphore(f"plf{b}")) for b in range(NBAR)]

        fence_n = [0]

        def fence():
            fence_n[0] += 1
            k = fence_n[0]

            @blk.tensor
            def _(t, k=k):
                t.nop().then_inc(phc, 1)
                t.wait_ge(phc, 3 * k)
                t.wait_ge(phs, 16 * k)
                t.wait_ge(phg, 16 * k)

            @blk.vector
            def _(v, k=k):
                v.tensor_copy(dum2[0:1, 0:8], dum[0:1, 0:8]).then_inc(phc, 1)
                v.wait_ge(phc, 3 * k)
                v.wait_ge(phs, 16 * k)
                v.wait_ge(phg, 16 * k)

            @blk.scalar
            def _(sc, k=k):
                sc.activation(dum2[0:1, 8:16], dum[0:1, 8:16],
                              AF.Copy).then_inc(phc, 1)
                sc.wait_ge(phc, 3 * k)
                sc.wait_ge(phs, 16 * k)
                sc.wait_ge(phg, 16 * k)

            @blk.sync
            def _(s, k=k):
                s.dma_start(out=dum[0:1, 0:8], in_=dumD[0:1, 0:8]).then_inc(phs, 16)
                s.wait_ge(phc, 3 * k)
                s.wait_ge(phs, 16 * k)
                s.wait_ge(phg, 16 * k)

            @blk.gpsimd
            def _(g, k=k):
                g.dma_start(out=dum[1:2, 0:8], in_=dumD[1:2, 0:8],
                            allow_slow_non_contiguous=True).then_inc(phg, 16)
                g.wait_ge(phc, 3 * k)
                g.wait_ge(phs, 16 * k)
                g.wait_ge(phg, 16 * k)

        @blk.gpsimd
        def _(g):
            for dst, srcd in ((ixg_sb, ix_g), (ixd_sb, ix_d), (ixup_sb, ix_up),
                              (ixdn_sb, ix_dn), (ixatt_sb, ix_att), (gx_sb, gxi),
                              (ixflg_sb, ix_flg), (flgv_sb, flgv)):
                gpb.bump(g.dma_start(out=dst[:, :], in_=srcd[:, :]), 16)
        IX_LD = gpb.n

        esW = ExitStack()
        sbW = lambda name, shape, dt: esW.enter_context(nc.sbuf_tensor(name, shape, dt))
        psW = lambda name, shape, dt: esW.enter_context(nc.psum_tensor(name, shape, dt))
        idxb = [sbW(f"Widxb{i}", [128, GCW], BF16) for i in range(2)]
        accb = [sbW(f"Waccb{i}", [128, GCW], BF16) for i in range(2)]
        maskb = sbW("Wmaskb", [128, GCW], BF16)
        lut_sb = sbW("Wlut_sb", [128, 16], F32)
        h128b = sbW("Wh128b", [128, 128], BF16)
        z_sb = sbW("Wz_sb", [128, 16 * 1024], BF16)
        z2_sb = sbW("Wz2_sb", [128, 64 * 256], BF16)
        ps_h = [psW(f"Wps_h{i}", [128, 512], F32) for i in range(2)]

        semsW = ExitStack()
        gld = Ctr(nc, semsW, "Wgld")
        mm = Ctr(nc, semsW, "Wmm")
        ev = Ctr(nc, semsW, "Wev")
        gsc = Ctr(nc, semsW, "Wgsc")
        bd = Ctr(nc, semsW, "Wbd")

        @blk.gpsimd
        def _(g):
            g.wait_ge(gpb.h, IX_LD)
            gld.bump(g.dma_start(out=lut_sb[:, :], in_=lutb[:, :]), 16)
            gld.bump(g.dma_start(out=h128b[:, :], in_=h128[:, :]), 16)

        ghost_jobs = []
        for (idx_dr, pool, ixsb, ntile, cols) in (
                (gidxF, pool_g, ixg_sb, 16, GCW),
                (uidxF, pool_u, ixg_sb, 16, GCW),
                (didxF, pool_d, ixd_sb, 22, 2048)):
            for kt in range(ntile):
                ghost_jobs.append((idx_dr, pool, ixsb, kt, cols))

        for ji, (idx_dr, pool, ixsb, kt, cols) in enumerate(ghost_jobs):
            @blk.gpsimd
            def _(g, ji=ji, idx_dr=idx_dr, kt=kt, cols=cols):
                if ji >= 2:
                    g.wait_ge(ev.h, ji - 1)
                gld.bump(g.dma_start(
                    out=idxb[ji % 2][:, 0:cols],
                    in_=idx_dr[kt * 128:(kt + 1) * 128, :]), 16)

            @blk.vector
            def _(v, ji=ji, cols=cols):
                v.wait_ge(gld.h, 32 + 16 * (ji + 1))
                if ji >= 2:
                    v.wait_ge(gsc.h, 16 * (ji - 1))
                ib = idxb[ji % 2][:, 0:cols]
                ab = accb[ji % 2][:, 0:cols]
                mk = maskb[:, 0:cols]
                v.tensor_scalar(ab, ib, 0.0, lut_sb[:, 0:1], ALU.is_equal, ALU.mult)
                last = None
                for j in range(1, 16):
                    v.tensor_scalar(mk, ib, float(j), None, ALU.is_equal)
                    last = v.scalar_tensor_tensor(ab, mk, lut_sb[:, j:j + 1], ab,
                                                  ALU.mult, ALU.add)
                ev.bump(last)

            @blk.gpsimd
            def _(g, ji=ji, pool=pool, ixsb=ixsb, kt=kt, cols=cols):
                g.wait_ge(ev.h, ji + 1)
                gsc.bump(g.indirect_dma_start(
                    pool[:, :],
                    IndirectOffsetOnAxis(ap=ixsb[:, kt:kt + 1], axis=0),
                    accb[ji % 2][:, 0:cols], None), 16)

        def emit_wprime_f(w_in, c0, cols, ntile, nbf_stages, pool, ixsb, ixc0):
            for kt in range(ntile):
                @blk.gpsimd
                def _(g, kt=kt, cols=cols, w_in=w_in, c0=c0, evn=ev.n):
                    g.wait_ge(ev.h, evn)
                    g.wait_ge(mm.h, mm.n)
                    gld.bump(g.dma_start(
                        out=idxb[kt % 2][:, 0:cols],
                        in_=w_in[kt * 128:(kt + 1) * 128, c0:c0 + cols]), 16)

                @blk.tensor
                def _(t, kt=kt, cols=cols):
                    t.wait_ge(gld.h, gld.n)
                    t.wait_ge(ev.h, ev.n)
                    nch = (cols + 511) // 512
                    last = None
                    for n in range(nch):
                        lo, hi = n * 512, min((n + 1) * 512, cols)
                        last = t.matmul(ps_h[n % 2][:, 0:hi - lo], h128b[:, :],
                                        idxb[kt % 2][:, lo:hi], start=True, stop=True)
                    mm.bump(last)

                @blk.vector
                def _(v, kt=kt, cols=cols, gscn=gsc.n):
                    if kt == 0:
                        v.wait_ge(gsc.h, gscn)
                    v.wait_ge(mm.h, mm.n)
                    nch = (cols + 511) // 512
                    lastv = None
                    for n in range(nch):
                        lo, hi = n * 512, min((n + 1) * 512, cols)
                        lastv = v.tensor_copy(z_sb[:, lo:hi], ps_h[n % 2][:, 0:hi - lo])
                    ev.bump(lastv)

                @blk.sync
                def _(s, kt=kt, cols=cols, bdn=bd.n):
                    if kt == 0:
                        s.wait_ge(bd.h, bdn)
                    s.wait_ge(ev.h, ev.n)
                    bd.bump(s.dma_start(out=zDf[kt * 128:(kt + 1) * 128, 0:cols],
                                        in_=z_sb[:, 0:cols]), 16)

            @blk.sync
            def _(s, ntile=ntile, cols=cols, gscn=gsc.n):
                s.wait_ge(gsc.h, gscn)
                s.wait_ge(bd.h, bd.n)
                i = s.dma_start(
                    out=z2_sb[:, 0:ntile * cols],
                    in_=zDf[0:ntile * 128, 0:cols].rearrange("(j p) o -> p j o", p=128))
                bd.bump(i, 16)

            @blk.vector
            def _(v, ntile=ntile, cols=cols, nbf=nbf_stages):
                v.wait_ge(bd.h, bd.n)
                hstep = 1
                last = None
                for s_i in range(nbf):
                    ngrp = ntile // (2 * hstep)
                    apA = bass.AP(tensor=z2_sb, offset=0,
                                  ap=[[z2_sb.shape[1], 128],
                                      [2 * hstep * cols, ngrp], [1, hstep * cols]])
                    apB = bass.AP(tensor=z2_sb, offset=hstep * cols,
                                  ap=[[z2_sb.shape[1], 128],
                                      [2 * hstep * cols, ngrp], [1, hstep * cols]])
                    apA2 = bass.AP(tensor=z_sb, offset=0,
                                   ap=[[z_sb.shape[1], 128],
                                       [2 * hstep * cols, ngrp], [1, hstep * cols]])
                    apB2 = bass.AP(tensor=z_sb, offset=hstep * cols,
                                   ap=[[z_sb.shape[1], 128],
                                       [2 * hstep * cols, ngrp], [1, hstep * cols]])
                    srcA, srcB = (apA, apB) if s_i % 2 == 0 else (apA2, apB2)
                    dstA, dstB = (apA2, apB2) if s_i % 2 == 0 else (apA, apB)
                    v.tensor_tensor(dstA, srcA, srcB, ALU.add)
                    last = v.tensor_tensor(dstB, srcA, srcB, ALU.subtract)
                    hstep *= 2
                ev.bump(last)

            final_buf = z2_sb if (nbf_stages % 2 == 0) else z_sb

            @blk.gpsimd
            def _(g, ntile=ntile, cols=cols, pool=pool, ixsb=ixsb, ixc0=ixc0,
                  final_buf=final_buf):
                g.wait_ge(ev.h, ev.n)
                for j in range(ntile):
                    gsc.bump(g.indirect_dma_start(
                        pool[:, :],
                        IndirectOffsetOnAxis(ap=ixsb[:, ixc0 + j:ixc0 + j + 1], axis=0),
                        final_buf[:, j * cols:(j + 1) * cols], None), 16)

        for ch in range(4):
            emit_wprime_f(wupTF, ch * 1024, 1024, 16, 4, pool_up, ixup_sb, ch * 16)
        for ch in range(4):
            emit_wprime_f(wgtTF, ch * 1024, 1024, 16, 4, pool_gt, ixup_sb, ch * 16)
        for ch in range(4):
            emit_wprime_f(wdnTF, ch * 256, 256, 64, 6, pool_dn, ixdn_sb, ch * 64)

        @blk.gpsimd
        def _(g):
            g.wait_ge(gsc.h, gsc.n)

        semsW.close()
        esW.close()
        fence()

        ioA = {"xTb": xTb, "wqT": wqT, "wkT": wkT, "wvT": wvT,
               "triu": triu, "onesc": onesc, "attnT": attnT}
        emit_l0_attn(nc, blk, ioA)
        fence()

        esG1 = ExitStack()
        astg = [esG1.enter_context(nc.sbuf_tensor(f"G1a{i}", [128, 2048], BF16))
                for i in range(2)]
        gst = [esG1.enter_context(nc.sbuf_tensor(f"G1g{i}", [128, 1024], BF16))
               for i in range(2)]

        @blk.gpsimd
        def _(g):
            base = gpb.n
            for t in range(8):
                if t >= 2:
                    g.wait_ge(gpb.h, base + 48 * (t - 1))
                gpb.bump(g.dma_start(out=astg[t % 2][:, :],
                                     in_=attnT[t * 128:(t + 1) * 128, :]), 16)
                g.wait_ge(gpb.h, gpb.n)
                for th2 in range(2):
                    gpb.bump(g.indirect_dma_start(
                        pool_att[:, :],
                        IndirectOffsetOnAxis(
                            ap=ixatt_sb[:, 2 * t + th2:2 * t + th2 + 1], axis=0),
                        astg[t % 2][:, th2 * 1024:(th2 + 1) * 1024], None), 16)
            g.wait_ge(gpb.h, gpb.n)
            gpb.bump(g.indirect_dma_start(
                pool_flg[:, :],
                IndirectOffsetOnAxis(ap=ixflg_sb[0:2, 0:1], axis=0),
                flgv_sb[0:2, 0:1], None), 16)
            g.wait_ge(gpb.h, gpb.n)

        emit_poll(nc, blk, plf[0], pool_flg, flg4, 0)

        @blk.gpsimd
        def _(g):
            base = gpb.n
            for k in range(16):
                if k >= 2:
                    g.wait_ge(gpb.h, base + 32 * (k - 1))
                gpb.bump(g.indirect_dma_start(
                    gst[k % 2][:, :], None,
                    pool_att[:, :],
                    IndirectOffsetOnAxis(ap=gx_sb[:, k:k + 1], axis=0)), 16)
                g.wait_ge(gpb.h, gpb.n)
                gpb.bump(g.dma_start(out=attnTb[k * 128:(k + 1) * 128, :],
                                     in_=gst[k % 2][:, :]), 16)
            g.wait_ge(gpb.h, gpb.n)

        esG1.close()
        fence()

        ioF = {"attnTb": attnTb, "xresT": xresT, "invn": invn,
               "x2T": x2T, "x1T": x1T, "rowsD": rowsD,
               "woP_ap": lambda i: woP[i].rearrange("k p c -> p k c"),
               "wgP_ap": wgP_pool_ap(pool_g),
               "wuP_ap": wgP_pool_ap(pool_u),
               "wdP_ap": wdP_pool_ap}
        emit_l0_ffn(nc, blk, ioF)
        fence()

        esG2 = ExitStack()
        xstg = [esG2.enter_context(nc.sbuf_tensor(f"G2a{i}", [128, 1024], F32))
                for i in range(2)]
        x2g = [esG2.enter_context(nc.sbuf_tensor(f"G2b{i}", [128, 2048], F32))
               for i in range(2)]

        @blk.gpsimd
        def _(g):
            base = gpb.n
            for k in range(16):
                if k >= 2:
                    g.wait_ge(gpb.h, base + 32 * (k - 1))
                gpb.bump(g.dma_start(out=xstg[k % 2][:, :],
                                     in_=x2T[k * 128:(k + 1) * 128, :]), 16)
                g.wait_ge(gpb.h, gpb.n)
                gpb.bump(g.indirect_dma_start(
                    pool_x2[:, :],
                    IndirectOffsetOnAxis(ap=gx_sb[:, k:k + 1], axis=0),
                    xstg[k % 2][:, :], None), 16)
            g.wait_ge(gpb.h, gpb.n)
            gpb.bump(g.indirect_dma_start(
                pool_flg[:, :],
                IndirectOffsetOnAxis(ap=ixflg_sb[0:2, 1:2], axis=0),
                flgv_sb[0:2, 1:2], None), 16)
            g.wait_ge(gpb.h, gpb.n)

        emit_poll(nc, blk, plf[1], pool_flg, flg4, 1)

        @blk.gpsimd
        def _(g):
            base = gpb.n
            for k in range(16):
                if k >= 2:
                    g.wait_ge(gpb.h, base + 48 * (k - 1))
                for t2 in range(2):
                    src = bass.AP(tensor=pool_x2,
                                  offset=(t2 * 2048 + k * 128) * 1024,
                                  ap=[[1024, 128], [1, 1024]])
                    gpb.bump(g.dma_start(
                        out=x2g[k % 2][:, t2 * 1024:(t2 + 1) * 1024], in_=src), 16)
                g.wait_ge(gpb.h, gpb.n)
                gpb.bump(g.dma_start(out=x2Tb[k * 128:(k + 1) * 128, :],
                                     in_=x2g[k % 2][:, :]), 16)
            g.wait_ge(gpb.h, gpb.n)

        esG2.close()
        fence()

        ioS = {"x2Tb": x2Tb, "l1wqT": zwqT, "l1wkT": zwkT, "l1wvT": zwvT,
               "wilrT": wilrT, "bilr": bilr, "w1in": w1in, "pen": pen,
               "ident": ident, "trimask": trimask, "Zraw": Zraw, "ilrD": ilrD}
        emit_l1_attn(nc, blk, ioS)
        fence()

        esG3 = ExitStack()
        zstg = [esG3.enter_context(nc.sbuf_tensor(f"G3a{i}", [128, 2048], BF16))
                for i in range(2)]
        zgt = [esG3.enter_context(nc.sbuf_tensor(f"G3g{i}", [128, 1024], BF16))
               for i in range(2)]

        @blk.gpsimd
        def _(g):
            base = gpb.n
            for p in range(8):
                if p >= 2:
                    g.wait_ge(gpb.h, base + 48 * (p - 1))
                for chh in range(2):
                    src = bass.AP(tensor=Zraw, offset=chh * 64 * 128 + p * 16,
                                  ap=[[16384, 128], [128, 64], [1, 16]])
                    gpb.bump(g.dma_start(
                        out=zstg[p % 2][:, chh * 1024:(chh + 1) * 1024],
                        in_=src), 16)
                g.wait_ge(gpb.h, gpb.n)
                for th2 in range(2):
                    gpb.bump(g.indirect_dma_start(
                        pool_z[:, :],
                        IndirectOffsetOnAxis(
                            ap=ixatt_sb[:, 2 * p + th2:2 * p + th2 + 1], axis=0),
                        zstg[p % 2][:, th2 * 1024:(th2 + 1) * 1024], None), 16)
            g.wait_ge(gpb.h, gpb.n)
            gpb.bump(g.indirect_dma_start(
                pool_flg[:, :],
                IndirectOffsetOnAxis(ap=ixflg_sb[0:2, 2:3], axis=0),
                flgv_sb[0:2, 2:3], None), 16)
            g.wait_ge(gpb.h, gpb.n)

        emit_poll(nc, blk, plf[2], pool_flg, flg4, 2)

        @blk.gpsimd
        def _(g):
            base = gpb.n
            for k in range(16):
                if k >= 2:
                    g.wait_ge(gpb.h, base + 32 * (k - 1))
                gpb.bump(g.indirect_dma_start(
                    zgt[k % 2][:, :], None,
                    pool_z[:, :],
                    IndirectOffsetOnAxis(ap=gx_sb[:, k:k + 1], axis=0)), 16)
                g.wait_ge(gpb.h, gpb.n)
                gpb.bump(g.dma_start(out=ZTb[k * 128:(k + 1) * 128, :],
                                     in_=zgt[k % 2][:, :]), 16)
            g.wait_ge(gpb.h, gpb.n)

        esG3.close()
        fence()

        ioL = {"ZTb": ZTb, "xresT": x2T, "invn": invn,
               "su_col": su_col, "wsu_col": wsu_col, "wsg_col": wsg_col,
               "sudp_col": sudp_col, "wsd_col": wsd_col,
               "x4T": x4T, "x3T": x3T, "rowsD": rowsD,
               "l1woP_ap": lambda i: l1woP[i].rearrange("k p c -> p k c"),
               "wupP_ap": wupP_pool_ap(pool_up),
               "wgtP_ap": wupP_pool_ap(pool_gt),
               "wdnP_ap": wdnP_pool_ap}
        emit_l1_ffn(nc, blk, ioL)

        sems0.close()
    es0.close()
    return nc


def emit_poll(nc, blk, sem, pool_flg, flg4, bar):
    @blk.gpsimd
    def _(g):
        want = MAGICF + bar
        cnt = nc.alloc_register(mybir.EngineType.Pool, f"pl{bar}_cnt")
        iters = nc.alloc_register(mybir.EngineType.Pool, f"pl{bar}_it")
        wm = nc.alloc_register(mybir.EngineType.Pool, f"pl{bar}_wm")
        t1 = nc.alloc_register(mybir.EngineType.Pool, f"pl{bar}_t1")
        t2 = nc.alloc_register(mybir.EngineType.Pool, f"pl{bar}_t2")
        c = nc.alloc_register(mybir.EngineType.Pool, f"pl{bar}_c")
        g.reg_mov(cnt, 0)
        g.reg_mov(iters, 0)
        g.reg_mov(wm, 0)
        g.reg_mov(c, 1)
        with g.While(c):
            g.reg_add(iters, iters, 1)
            g.reg_add(wm, wm, 16)
            g.dma_start(out=flg4[:, :], in_=pool_flg[bar * 8:bar * 8 + 4, 0:1],
                        allow_slow_non_contiguous=True).then_inc(sem, 16)
            g.wait_ge(sem, wm)
            g.reg_mov(cnt, 0)
            for r in range(4):
                g.load(t1, flg4[r:r + 1, 0:1])
                g.reg_alu(t1, t1, want, ALU.is_equal)
                g.reg_alu(cnt, cnt, t1, ALU.add)
            g.reg_alu(t1, cnt, 4, ALU.is_lt)
            g.reg_alu(t2, iters, POLL_ITERS, ALU.is_lt)
            g.reg_alu(c, t1, t2, ALU.bitwise_and)




def host_prep_fused(inputs):
    mA = host_prep_l0_attn(inputs)
    x = inputs["L0_attn_hidden_states"]
    woT = np.ascontiguousarray(inputs["L0_attn_w_o"].T)
    woP = np.ascontiguousarray(woT.reshape(16, 128, 16, 128).transpose(2, 0, 1, 3))
    invn = np.full((128, 1), 1.0 / 2048.0, np.float32)
    gT = np.ascontiguousarray(inputs["L0_ffn_gate_idx"].T)
    uT = np.ascontiguousarray(inputs["L0_ffn_up_idx"].T)
    dTc = np.ascontiguousarray(inputs["L0_ffn_down_idx"].T)
    upT = np.ascontiguousarray(inputs["L1_ffn_W_up"].T)
    gtT = np.ascontiguousarray(inputs["L1_ffn_W_gate"].T)
    dnT = np.ascontiguousarray(inputs["L1_ffn_W_down"].T)
    lut = np.ascontiguousarray(
        np.broadcast_to(inputs["L0_ffn_lut"][None, :], (128, 16))).astype(np.float32)
    H = np.array([[1.0]], np.float32)
    while H.shape[0] < 128:
        H = np.block([[H, H], [H, -H]])
    H = np.ascontiguousarray(H)
    zwqT = np.ascontiguousarray(inputs["L1_attn_Wq"].T)
    zwkT = np.ascontiguousarray(inputs["L1_attn_Wk"].T)
    zwvT = np.ascontiguousarray(inputs["L1_attn_Wv"].T)
    wiT = np.ascontiguousarray(inputs["L1_attn_W_ilr"].T)
    bilr_np = inputs["L1_attn_b_ilr"].astype(np.float32)
    W1 = inputs["L1_attn_W1"].astype(np.float32)
    pen_np = (1.0 / (np.arange(S) % 16 + 1.0) / 64.0).astype(np.float32)[None, :]
    ident = np.eye(128, dtype=np.float32)
    tm = np.zeros((128, 128), np.float32)
    for p in range(8):
        for s2 in range(2):
            pb_, cb = (p % 4) * 32, (p // 4) * 32 + s2 * 16
            for d_ in range(16):
                for c_ in range(16):
                    if d_ <= c_:
                        tm[pb_ + d_, cb + c_] = -1.0
    l1woT = np.ascontiguousarray(inputs["L1_attn_Wo"].T)
    l1woP = np.ascontiguousarray(
        l1woT.reshape(16, 128, 16, 128).transpose(2, 0, 1, 3))
    su_col = np.ascontiguousarray(
        inputs["L1_ffn_SU_ug"].reshape(16, 128).T).astype(np.float32)
    wsu_col = np.ascontiguousarray(
        inputs["L1_ffn_Wscale_ug"][:8192].reshape(64, 128).T).astype(np.float32)
    wsg_col = np.ascontiguousarray(
        inputs["L1_ffn_Wscale_ug"][8192:].reshape(64, 128).T).astype(np.float32)
    sudp_col = np.ascontiguousarray(
        inputs["L1_ffn_SU_dp"].reshape(64, 128).T).astype(np.float32)
    wsd_col = np.ascontiguousarray(
        inputs["L1_ffn_Wscale_dp"].reshape(16, 128).T).astype(np.float32)

    pv = np.arange(128, dtype=np.uint32)
    maps = []
    for c in range(NCORES):
        b, par = c // 2, c % 2
        tc = slice(par * 1024, par * 1024 + 1024)
        gsl = slice(0, GCW) if par == 0 else slice(5504 - GCW, 5504)
        g0 = 16 * par
        zcols = slice(g0 * 64, g0 * 64 + 1024)
        w1v = np.zeros((128, 512), np.float32)
        for p in range(8):
            for s2 in range(2):
                w1v[s2 * 64:(s2 + 1) * 64, p * 64:(p + 1) * 64] = W1[g0 + 2 * p + s2]
        ixg = (par * 2048 + np.arange(16, dtype=np.uint32)[None, :] * 128
               + pv[:, None]).astype(np.uint32)
        ixd = (par * (5504 - GCW) + np.arange(22, dtype=np.uint32)[None, :] * 128
               + pv[:, None]).astype(np.uint32)
        ixup = np.zeros((128, 64), np.uint32)
        for ch in range(4):
            for j in range(16):
                ixup[:, ch * 16 + j] = par * 8192 + (j * 128 + pv) * 4 + ch
        ixdn = np.zeros((128, 256), np.uint32)
        for ch in range(4):
            for j in range(64):
                ixdn[:, ch * 64 + j] = (j * 128 + pv) * 8 + par * 4 + ch
        ixatt = np.zeros((128, 16), np.uint32)
        for t in range(8):
            for th2 in range(2):
                ixatt[:, 2 * t + th2] = th2 * 2048 + par * 1024 + t * 128 + pv
        gx = (par * 2048 + np.arange(16, dtype=np.uint32)[None, :] * 128
              + pv[:, None]).astype(np.uint32)
        ixflg = np.zeros((128, NBAR), np.uint32)
        for bb in range(NBAR):
            ixflg[0:2, bb] = bb * 8 + par * 2 + np.arange(2)
        flgv_np = np.zeros((128, NBAR), np.int32)
        for bb in range(NBAR):
            flgv_np[:, bb] = MAGICF + bb

        m = dict(mA[c])
        m.update({
            "xresT": np.ascontiguousarray(x[b].T[:, tc]),
            "woP": woP, "invn": invn,
            "gidxF": np.ascontiguousarray(gT[:, gsl]),
            "uidxF": np.ascontiguousarray(uT[:, gsl]),
            "didxF": np.ascontiguousarray(dTc[gsl, :]),
            "lutb": lut, "h128": H,
            "wupTF": np.ascontiguousarray(upT[:, par * 4096:(par + 1) * 4096]),
            "wgtTF": np.ascontiguousarray(gtT[:, par * 4096:(par + 1) * 4096]),
            "wdnTF": np.ascontiguousarray(dnT[:, par * 1024:(par + 1) * 1024]),
            "zwqT": np.ascontiguousarray(zwqT[:, zcols]),
            "zwkT": np.ascontiguousarray(zwkT[:, zcols]),
            "zwvT": np.ascontiguousarray(zwvT[:, zcols]),
            "wilrT": np.ascontiguousarray(wiT[:, g0:g0 + 16]),
            "bilr": np.ascontiguousarray(bilr_np[g0:g0 + 16].reshape(16, 1)),
            "w1in": w1v, "pen": pen_np, "ident": ident, "trimask": tm,
            "l1woP": l1woP,
            "su_col": su_col, "wsu_col": wsu_col, "wsg_col": wsg_col,
            "sudp_col": sudp_col, "wsd_col": wsd_col,
            "ix_g": ixg, "ix_d": ixd, "ix_up": ixup, "ix_dn": ixdn,
            "ix_att": ixatt, "gxi": gx, "ix_flg": ixflg, "flgv": flgv_np,
        })
        maps.append(m)
    return maps



_NC_CACHE = {}


def _get_nc(name, builder):
    if name not in _NC_CACHE:
        _NC_CACHE[name] = builder()
    return _NC_CACHE[name]


def kernel_unfused(**inputs):
    inputs = {k: np.asarray(v) for k, v in inputs.items()}
    cores = list(range(NCORES))

    nc = _get_nc("wprep", build_wprep)
    res = run_bass_kernel_spmd(nc, host_prep_wprep(inputs), core_ids=cores)
    WgT = np.concatenate([np.asarray(r["wg_o"]) for r in res.results], axis=1)
    WuT = np.concatenate([np.asarray(r["wu_o"]) for r in res.results], axis=1)
    WdT = np.concatenate([np.asarray(r["wd_o"]) for r in res.results], axis=1)
    upP = np.concatenate([np.asarray(r["wupP"]) for r in res.results], axis=1)
    gtP = np.concatenate([np.asarray(r["wgtP"]) for r in res.results], axis=1)
    dnP = np.concatenate([np.asarray(r["wdnP"]) for r in res.results], axis=1)

    nc = _get_nc("l0a", build_l0_attn)
    res = run_bass_kernel_spmd(nc, host_prep_l0_attn(inputs), core_ids=cores)
    attnT_full = [np.concatenate([np.asarray(res.results[2 * b]["attnT"]),
                                  np.asarray(res.results[2 * b + 1]["attnT"])], axis=0)
                  for b in range(4)]

    nc = _get_nc("l0f", build_l0_ffn)
    res = run_bass_kernel_spmd(
        nc, host_prep_l0_ffn(inputs, attnT_full, WgT, WuT, WdT), core_ids=cores)
    x2_full = [np.concatenate([np.asarray(res.results[2 * b]["x2T"]),
                               np.asarray(res.results[2 * b + 1]["x2T"])], axis=1)
               for b in range(4)]

    nc = _get_nc("l1a", build_l1_attn)
    res = run_bass_kernel_spmd(nc, host_prep_l1_attn(inputs, x2_full), core_ids=cores)
    ZT_full = [np.concatenate([zraw_to_zt(np.asarray(res.results[2 * b]["Zraw"])),
                               zraw_to_zt(np.asarray(res.results[2 * b + 1]["Zraw"]))],
                              axis=0)
               for b in range(4)]

    nc = _get_nc("l1f", build_l1_ffn)
    maps = host_prep_l1_ffn(inputs, ZT_full, x2_full)
    up, gt, dn = panelize_l1_w(upP, gtP, dnP)
    for m in maps:
        m["wupP"] = up
        m["wgtP"] = gt
        m["wdnP"] = dn
    res = run_bass_kernel_spmd(nc, maps, core_ids=cores)

    out = np.empty((4, S, D), np.float32)
    for b in range(4):
        x4 = np.concatenate([np.asarray(res.results[2 * b]["x4T"]),
                             np.asarray(res.results[2 * b + 1]["x4T"])], axis=1)
        out[b] = x4.T
    return out


def kernel(**inputs):
    inputs = {k: np.asarray(v) for k, v in inputs.items()}
    nc = _get_nc("fused", build_fused)
    maps = host_prep_fused(inputs)
    res = run_bass_kernel_spmd(nc, maps, core_ids=list(range(NCORES)))
    out = np.empty((4, S, D), np.float32)
    for b in range(4):
        x4 = np.concatenate([np.asarray(res.results[2 * b]["x4T"]),
                             np.asarray(res.results[2 * b + 1]["x4T"])], axis=1)
        out[b] = x4.T
    return out


# version 03 (final)
# speedup vs baseline: 1.1589x; 1.1589x over previous
import numpy as np
import ml_dtypes
from contextlib import ExitStack

import concourse.bass as bass
import concourse.mybir as mybir
from concourse.bass_utils import run_bass_kernel_spmd

F32 = mybir.dt.float32
BF16 = mybir.dt.bfloat16
I32 = mybir.dt.int32
U32 = mybir.dt.uint32
AF = mybir.ActivationFunctionType
ALU = mybir.AluOpType

B, S, D = 4, 2048, 2048
NCORES = 8
TOK = 1024
RSQRT_HD = float(1.0 / np.sqrt(128.0))


class Ctr:

    def __init__(self, nc, es, name):
        self.h = es.enter_context(nc.semaphore(name))
        self.n = 0

    def bump(self, inst, k=1):
        inst.then_inc(self.h, k)
        self.n += k
        return self.n


def bfcast(x):
    return np.ascontiguousarray(x).astype(ml_dtypes.bfloat16)



def emit_l0_attn(nc, blk_outer, io, pfx="a"):
    xTb, wqT, wkT, wvT = io["xTb"], io["wqT"], io["wkT"], io["wvT"]
    triu, onesc, attnT = io["triu"], io["onesc"], io["attnT"]
    dbg_q = dbg_k = dbg_vn = dbg_rc = None

    es = ExitStack()
    sb = lambda name, shape, dt: es.enter_context(nc.sbuf_tensor(pfx + name, shape, dt))
    ps = lambda name, shape, dt: es.enter_context(nc.psum_tensor(pfx + name, shape, dt))

    xT = sb("xT", [128, 16 * S], BF16)
    wq = sb("wq", [128, 16 * 1024], BF16)
    wk = sb("wk", [128, 16 * 1024], BF16)
    wv = sb("wv", [128, 16 * 1024], BF16)
    triu_b = sb("triu_b", [128, 128], BF16)
    ones_b = sb("ones_b", [128, 128], BF16)
    qT = sb("qT", [128, S], BF16)
    kT = sb("kT", [128, S], BF16)
    vT = sb("vT", [128, S], BF16)
    vn = sb("vn", [128, S], BF16)
    exps = [sb(f"exps{i}", [128, 1024], BF16) for i in range(2)]
    recip = sb("recip", [128, 1024], F32)
    att_o = sb("att_o", [128, S], BF16)

    ps_a = ps("ps_a", [128, 1024], F32)
    ps_b = ps("ps_b", [128, 1024], F32)
    ps_den = ps("ps_den", [128, 1024], F32)
    ps_att = ps("ps_att", [128, 1024], F32)

    if True:
        blk = blk_outer
        sems = ExitStack()
        g_ld = Ctr(nc, sems, pfx + "g_ld")
        mm = Ctr(nc, sems, pfx + "mm")
        ev = Ctr(nc, sems, pfx + "ev")
        ex = Ctr(nc, sems, pfx + "ex")
        mk = Ctr(nc, sems, pfx + "mk")
        tr = Ctr(nc, sems, pfx + "tr")
        st = Ctr(nc, sems, pfx + "st")

        @blk.gpsimd
        def _(g):
            for k in range(16):
                i = g.dma_start(out=xT[:, k * S:(k + 1) * S],
                                in_=xTb[k * 128:(k + 1) * 128, :])
                g_ld.bump(i, 16)
            for w_sb, w_dr in ((wq, wqT), (wk, wkT), (wv, wvT)):
                for k in range(16):
                    i = g.dma_start(out=w_sb[:, k * 1024:(k + 1) * 1024],
                                    in_=w_dr[k * 128:(k + 1) * 128, :])
                    g_ld.bump(i, 16)
            g_ld.bump(g.dma_start(out=triu_b[:, :], in_=triu[:, :]), 16)
            g_ld.bump(g.dma_start(out=ones_b[:, :], in_=onesc[:, :]), 16)

        LD_ALL = g_ld.n

        for h in range(8):
            ev_base = ev.n

            @blk.tensor
            def _(t, h=h, ev_base=ev_base):
                if h == 0:
                    t.wait_ge(g_ld.h, LD_ALL)
                for pi, w_sb in enumerate((wq, wk, wv)):
                    if pi > 0:
                        t.wait_ge(ev.h, ev_base + pi)
                    elif h > 0:
                        t.wait_ge(ev.h, ev_base)
                    last = None
                    for k in range(16):
                        lhsT = w_sb[:, k * 1024 + h * 128: k * 1024 + (h + 1) * 128]
                        for n in range(4):
                            dst = (ps_a if n < 2 else ps_b)[:, (n % 2) * 512:(n % 2 + 1) * 512]
                            last = t.matmul(dst, lhsT,
                                            xT[:, k * S + n * 512: k * S + (n + 1) * 512],
                                            start=(k == 0), stop=(k == 15))
                    mm.bump(last)

            mm_pbase = mm.n - 3

            @blk.vector
            def _(v, h=h, mm_pbase=mm_pbase):
                for i, dst in enumerate((qT, kT, vT)):
                    v.wait_ge(mm.h, mm_pbase + i + 1)
                    if dst is vT:
                        v.wait_ge(tr.h, tr.n)
                    if i == 0 and h > 0:
                        v.wait_ge(st.h, st.n)
                    v.tensor_copy(dst[:, 0:1024], ps_a[:, :])
                    ev.bump(v.tensor_copy(dst[:, 1024:2048], ps_b[:, :]))

            @blk.sync
            def _(s):
                s.wait_ge(ev.h, ev.n)
                for kt in range(16):
                    i = s.dma_start_transpose(vn[:, kt * 128:(kt + 1) * 128],
                                              vT[:, kt * 128:(kt + 1) * 128])
                    tr.bump(i, 16)

            for qh in range(2):
                kts = list(range(0, 8 * (qh + 1)))
                K = len(kts)
                q0 = qh * 1024
                mm_base = mm.n
                ex_base = ex.n
                mk_base = mk.n
                ev_evac = ev.n
                diag = {kt: kt * 128 - q0 for kt in kts if q0 <= kt * 128 < q0 + 1024}

                def emit_scores(t, i, kt):
                    pss = (ps_a, ps_b)[i % 2]
                    co = max(0, kt * 128 - q0)
                    last = None
                    for n in range(2):
                        lo, hi = max(n * 512, co), (n + 1) * 512
                        if lo >= hi:
                            continue
                        last = t.matmul(pss[:, lo:hi],
                                        kT[:, kt * 128:(kt + 1) * 128],
                                        qT[:, q0 + lo:q0 + hi],
                                        start=True, stop=True)
                    mm.bump(last)

                @blk.tensor
                def _(t, kts=kts, K=K, q0=q0, mm_base=mm_base, ex_base=ex_base,
                      mk_base=mk_base, ev_evac=ev_evac, diag=diag, qh=qh, h=h):
                    if qh == 0 and h == 0:
                        t.wait_ge(tr.h, tr.n)
                    elif qh == 0:
                        t.wait_ge(tr.h, tr.n)
                    t.wait_ge(ev.h, ev_evac)
                    emit_scores(t, 0, kts[0])
                    if K > 1:
                        emit_scores(t, 1, kts[1])
                    ndiag_seen = 0
                    for i, kt in enumerate(kts):
                        t.wait_ge(ex.h, ex_base + i + 1)
                        if kt in diag:
                            ndiag_seen += 1
                            t.wait_ge(mk.h, mk_base + ndiag_seen)
                        first, last_i = (i == 0), (i == K - 1)
                        co = max(0, kt * 128 - q0)
                        last = None
                        for n in range(2):
                            lo, hi = max(n * 512, co), (n + 1) * 512
                            if lo >= hi:
                                continue
                            t.matmul(ps_den[:, lo:hi], ones_b[:, :],
                                     exps[i % 2][:, lo:hi], start=first, stop=last_i)
                            last = t.matmul(ps_att[:, lo:hi],
                                            vn[:, kt * 128:(kt + 1) * 128],
                                            exps[i % 2][:, lo:hi], start=first, stop=last_i)
                        mm.bump(last)
                        if i + 2 < K:
                            emit_scores(t, i + 2, kts[i + 2])

                @blk.scalar
                def _(sc, kts=kts, K=K, q0=q0, mm_base=mm_base, mk_base=mk_base,
                      diag=diag):
                    ndiag_seen = 0
                    for i, kt in enumerate(kts):
                        sc.wait_ge(mm.h, mm_base + (i + 1 if i < 2 else 2 * i))
                        if i >= 2:
                            sc.wait_ge(mm.h, mm_base + 3 + 2 * (i - 2))
                            if kts[i - 2] in diag:
                                ndiag_seen += 1
                                sc.wait_ge(mk.h, mk_base + ndiag_seen)
                        co = max(0, kt * 128 - q0)
                        pss = (ps_a, ps_b)[i % 2]
                        a = sc.activation(exps[i % 2][:, co:1024], pss[:, co:1024],
                                          AF.Exp, scale=RSQRT_HD)
                        ex.bump(a)

                @blk.vector
                def _(v, kts=kts, q0=q0, ex_base=ex_base, diag=diag):
                    for kt, co in diag.items():
                        i = kts.index(kt)
                        v.wait_ge(ex.h, ex_base + i + 1)
                        m = v.tensor_tensor(exps[i % 2][:, co:co + 128],
                                            exps[i % 2][:, co:co + 128],
                                            triu_b[:, :], ALU.mult)
                        mk.bump(m)

                @blk.vector
                def _(v, q0=q0):
                    v.wait_ge(mm.h, mm.n)
                    v.reciprocal(recip[:, :], ps_den[:, :])
                    e = v.tensor_tensor(att_o[:, q0:q0 + 1024], ps_att[:, :],
                                        recip[:, :], ALU.mult)
                    ev.bump(e)

            @blk.sync
            def _(s, h=h):
                s.wait_ge(ev.h, ev.n)
                i = s.dma_start(out=attnT[h * 128:(h + 1) * 128, :],
                                in_=att_o[:, :])
                st.bump(i, 16)
                if h == 0 and dbg_q is not None:
                    st.bump(s.dma_start(out=dbg_q[:, :], in_=qT[:, :]), 16)
                    st.bump(s.dma_start(out=dbg_k[:, :], in_=kT[:, :]), 16)
                    st.bump(s.dma_start(out=dbg_vn[:, :], in_=vn[:, :]), 16)
                    st.bump(s.dma_start(out=dbg_rc[:, :], in_=recip[:, :]), 16)

        @blk.sync
        def _(s):
            s.wait_ge(st.h, st.n)

        sems.close()
    es.close()
    return st


def build_l0_attn():
    nc = bass.Bass("TRN2", target_bir_lowering=False, debug=False,
                   detect_race_conditions=False)
    io = {
        "xTb": nc.dram_tensor("xTb", [D, S], F32, kind="ExternalInput"),
        "wqT": nc.dram_tensor("wqT", [D, 1024], F32, kind="ExternalInput"),
        "wkT": nc.dram_tensor("wkT", [D, 1024], F32, kind="ExternalInput"),
        "wvT": nc.dram_tensor("wvT", [D, 1024], F32, kind="ExternalInput"),
        "triu": nc.dram_tensor("triu", [128, 128], F32, kind="ExternalInput"),
        "onesc": nc.dram_tensor("onesc", [128, 128], F32, kind="ExternalInput"),
        "attnT": nc.dram_tensor("attnT", [1024, S], BF16, kind="ExternalOutput"),
    }
    with nc.Block() as blk:
        emit_l0_attn(nc, blk, io)
    return nc


def host_prep_l0_attn(inputs):
    x = inputs["L0_attn_hidden_states"]
    w_qkvT = np.ascontiguousarray(inputs["L0_attn_w_qkv"].T)
    triu = np.ascontiguousarray(np.triu(np.ones((128, 128), np.float32)))
    ones = np.ones((128, 128), np.float32)
    maps = []
    for c in range(NCORES):
        b, half = c // 2, c % 2
        lo = 8 * half * 128
        maps.append({
            "xTb": np.ascontiguousarray(x[b].T),
            "wqT": np.ascontiguousarray(w_qkvT[:, lo:lo + 1024]),
            "wkT": np.ascontiguousarray(w_qkvT[:, 2048 + lo:2048 + lo + 1024]),
            "wvT": np.ascontiguousarray(w_qkvT[:, 4096 + lo:4096 + lo + 1024]),
            "triu": triu,
            "onesc": ones,
        })
    return maps





NIO = 43


def emit_l0_ffn(nc, blk_outer, io, mkctr=None, upto="ALL", pfx="f"):
    attnTb, xresT, invn = io["attnTb"], io["xresT"], io["invn"]
    woP_ap, wgP_ap, wuP_ap, wdP_ap = (io["woP_ap"], io["wgP_ap"],
                                      io["wuP_ap"], io["wdP_ap"])
    x2T, x1T, rowsD = io["x2T"], io["x1T"], io["rowsD"]

    es = ExitStack()
    sb = lambda name, shape, dt: es.enter_context(nc.sbuf_tensor(pfx + name, shape, dt))
    ps = lambda name, shape, dt: es.enter_context(nc.psum_tensor(pfx + name, shape, dt))

    xr_sb = sb("xr_sb", [128, 16 * 512], F32)
    y_sb = sb("y_sb", [128, 16 * 512], F32)
    ybf = [sb(f"ybf{i}", [128, 512], BF16) for i in range(2)]
    x1b = sb("x1b", [128, 16 * 1024], BF16)
    wo_sb = [sb(f"wo_sb{i}", [128, 16 * 128], BF16) for i in range(2)]
    wg_sb = [sb(f"wg_sb{i}", [128, 16 * 128], BF16) for i in range(2)]
    wu_sb = [sb(f"wu_sb{i}", [128, 16 * 128], BF16) for i in range(2)]
    wd_sb = [sb(f"wd_sb{i}", [128, NIO * 128], BF16) for i in range(2)]
    h_sb = sb("h_sb", [128, NIO * 512], BF16)
    at_sb = h_sb
    sg_sb = sb("sg_sb", [128, 512], BF16)
    sq_sb = sb("sq_sb", [128, 512], BF16)
    inv_sb = sb("inv_sb", [128, 1], BF16)
    r1 = sb("r1", [1, 512], F32)
    r2 = sb("r2", [1, 512], F32)
    rtmp = sb("rtmp", [1, 512], F32)
    mb = sb("mb", [128, 512], F32)
    rbb = sb("rbb", [128, 512], F32)
    x1f = [sb(f"x1f{i}", [128, 512], F32) for i in range(2)]

    ps_y = [ps(f"ps_y{i}", [128, 512], F32) for i in range(2)]
    ps_u = [ps(f"ps_u{i}", [128, 512], F32) for i in range(2)]
    ps_m = ps("ps_m", [1, 512], F32)
    ps_v = ps("ps_v", [1, 512], F32)

    if True:
        blk = blk_outer
        sems = ExitStack()
        if mkctr is None:
            mkctr = lambda name: Ctr(nc, sems, pfx + name)
        gld = mkctr("gld")
        sld = mkctr("sld")
        mm = mkctr("mm")
        ev = mkctr("ev")
        ac = mkctr("ac")
        st = mkctr("st")
        bc = mkctr("bc")

        @blk.gpsimd
        def _(g):
            gld.bump(g.dma_start(out=inv_sb[:, :], in_=invn[:, :]), 16)

        def emit_stats(oc, src_bf):
            @blk.scalar
            def _(sc, oc=oc, src_bf=src_bf):
                sc.wait_ge(ev.h, ev.n)
                sc.wait_ge(mm.h, mm.n)
                ac.bump(sc.activation(sq_sb[:, :], src_bf[:, :], AF.Square))

            @blk.tensor
            def _(t, oc=oc, src_bf=src_bf):
                t.wait_ge(ev.h, ev.n)
                t.matmul(ps_m[:, :], inv_sb[:, :], src_bf[:, :],
                         start=(oc == 0), stop=(oc == 15))
                t.wait_ge(ac.h, ac.n)
                mm.bump(t.matmul(ps_v[:, :], inv_sb[:, :], sq_sb[:, :],
                                 start=(oc == 0), stop=(oc == 15)))

        def emit_ln(th, out_bf_full, out_dram):
            @blk.vector
            def _(v):
                v.wait_ge(mm.h, mm.n)
                v.tensor_copy(r1[:, :], ps_m[:, :])
                v.tensor_copy(r2[:, :], ps_v[:, :])
                v.tensor_tensor(rtmp[0:1, :], r1[0:1, :], r1[0:1, :], ALU.mult)
                v.tensor_tensor(r2[0:1, :], r2[0:1, :], rtmp[0:1, :],
                                ALU.subtract)
                ev.bump(v.tensor_scalar(r2[0:1, :], r2[0:1, :], 1e-5, None,
                                        ALU.add))

            @blk.scalar
            def _(sc):
                sc.wait_ge(ev.h, ev.n)
                ac.bump(sc.activation(r2[0:1, :], r2[0:1, :], AF.Sqrt))

            @blk.vector
            def _(v):
                v.wait_ge(ac.h, ac.n)
                ev.bump(v.reciprocal(r2[0:1, :], r2[0:1, :]))

            @blk.gpsimd
            def _(g):
                g.wait_ge(ev.h, ev.n)
                g.dma_start(out=rowsD[0:1, :], in_=r1[0:1, :]).then_inc(bc.h, 16)
                g.dma_start(out=rowsD[1:2, :], in_=r2[0:1, :]).then_inc(bc.h, 16)
                g.wait_ge(bc.h, bc.n + 32)
                bm = bass.AP(tensor=rowsD, offset=0, ap=[[0, 128], [1, 512]])
                br = bass.AP(tensor=rowsD, offset=512, ap=[[0, 128], [1, 512]])
                g.dma_start(out=mb[:, :], in_=bm).then_inc(bc.h, 16)
                g.dma_start(out=rbb[:, :], in_=br).then_inc(bc.h, 16)
            bc.n += 64

            for oc in range(16):
                @blk.vector
                def _(v, oc=oc, th=th):
                    if oc == 0:
                        v.wait_ge(bc.h, bc.n)
                    if oc >= 2:
                        v.wait_ge(st.h, st.n - 16)
                    xf = x1f[oc % 2]
                    v.tensor_tensor(xf[:, :], y_sb[:, oc * 512:(oc + 1) * 512],
                                    mb[:, :], ALU.subtract)
                    v.tensor_tensor(xf[:, :], xf[:, :], rbb[:, :], ALU.mult)
                    ev.bump(v.tensor_copy(
                        out_bf_full[:, oc * 1024 + th * 512: oc * 1024 + th * 512 + 512],
                        xf[:, :]))

                @blk.scalar
                def _(sc, oc=oc, th=th):
                    sc.wait_ge(ev.h, ev.n)
                    st.bump(sc.dma_start(
                        out=out_dram[oc * 128:(oc + 1) * 128, th * 512:th * 512 + 512],
                        in_=x1f[oc % 2][:, :]), 16)

        for th in range(2):
            t0 = th * 512

            @blk.sync
            def _(s, th=th, t0=t0):
                if th == 1:
                    s.wait_ge(mm.h, mm.n)
                for k in range(16):
                    sld.bump(s.dma_start(
                        out=at_sb[:, k * 512:(k + 1) * 512],
                        in_=attnTb[k * 128:(k + 1) * 128, t0:t0 + 512]), 16)
                for k in range(16):
                    sld.bump(s.dma_start(
                        out=xr_sb[:, k * 512:(k + 1) * 512],
                        in_=xresT[k * 128:(k + 1) * 128, t0:t0 + 512]), 16)
            LD_A = sld.n

            @blk.gpsimd
            def _(g, th=th):
                if th == 1:
                    g.wait_ge(mm.h, mm.n)
                for oc in range(2):
                    gld.bump(g.dma_start(
                        out=wo_sb[oc][:, :], in_=woP_ap(oc)), 16)
            GP_A = gld.n

            for oc in range(16):
                @blk.gpsimd
                def _(g, oc=oc, th=th):
                    if oc + 2 < 16:
                        g.wait_ge(mm.h, mm.n + 1)
                        gld.bump(g.dma_start(
                            out=wo_sb[oc % 2][:, :], in_=woP_ap(oc + 2)), 16)

                @blk.tensor
                def _(t, oc=oc, th=th, GP_A=GP_A, LD_A=LD_A):
                    if oc == 0:
                        t.wait_ge(sld.h, LD_A)
                    t.wait_ge(gld.h, GP_A + 16 * max(0, oc - 1))
                    if oc >= 2:
                        t.wait_ge(ev.h, ev.n)
                    last = None
                    for k in range(16):
                        last = t.matmul(ps_y[oc % 2][:, :],
                                        wo_sb[oc % 2][:, k * 128:(k + 1) * 128],
                                        at_sb[:, k * 512:(k + 1) * 512],
                                        start=(k == 0), stop=(k == 15))
                    mm.bump(last)

                @blk.vector
                def _(v, oc=oc):
                    v.wait_ge(mm.h, mm.n)
                    v.tensor_tensor(y_sb[:, oc * 512:(oc + 1) * 512],
                                    ps_y[oc % 2][:, :],
                                    xr_sb[:, oc * 512:(oc + 1) * 512], ALU.add)
                    ev.bump(v.tensor_copy(ybf[oc % 2][:, :],
                                          y_sb[:, oc * 512:(oc + 1) * 512]))

                if upto != "A0":
                    emit_stats(oc, ybf[oc % 2])

            if upto == "A0":
                for oc in range(16):
                    @blk.scalar
                    def _(sc, oc=oc, th=th):
                        sc.wait_ge(ev.h, ev.n)
                        st.bump(sc.dma_start(
                            out=x2T[oc * 128:(oc + 1) * 128, th * 512:th * 512 + 512],
                            in_=y_sb[:, oc * 512:(oc + 1) * 512]), 16)
                continue
            emit_ln(th, x1b, x1T)
            if upto == "A":
                continue

            @blk.sync
            def _(s, th=th):
                s.wait_ge(st.h, st.n)
                for buf in range(2):
                    sld.bump(s.dma_start(
                        out=wg_sb[buf][:, :], in_=wgP_ap(buf)), 16)
                    sld.bump(s.dma_start(
                        out=wu_sb[buf][:, :], in_=wuP_ap(buf)), 16)
            LD_B = sld.n

            for io in range(NIO):
                @blk.sync
                def _(s, io=io):
                    if io + 2 < NIO:
                        s.wait_ge(mm.h, mm.n + 2)
                        sld.bump(s.dma_start(
                            out=wg_sb[io % 2][:, :], in_=wgP_ap(io + 2)), 16)
                        sld.bump(s.dma_start(
                            out=wu_sb[io % 2][:, :], in_=wuP_ap(io + 2)), 16)

                @blk.tensor
                def _(t, io=io, th=th, LD_B=LD_B):
                    t.wait_ge(sld.h, LD_B + 32 * max(0, io - 1))
                    if io >= 2:
                        t.wait_ge(ev.h, ev.n)
                    last = None
                    for k in range(16):
                        last = t.matmul(ps_y[io % 2][:, :],
                                        wg_sb[io % 2][:, k * 128:(k + 1) * 128],
                                        x1b[:, k * 1024 + th * 512: k * 1024 + th * 512 + 512],
                                        start=(k == 0), stop=(k == 15))
                    mm.bump(last)
                    last = None
                    for k in range(16):
                        last = t.matmul(ps_u[io % 2][:, :],
                                        wu_sb[io % 2][:, k * 128:(k + 1) * 128],
                                        x1b[:, k * 1024 + th * 512: k * 1024 + th * 512 + 512],
                                        start=(k == 0), stop=(k == 15))
                    mm.bump(last)

                @blk.scalar
                def _(sc, io=io):
                    sc.wait_ge(mm.h, mm.n - 1)
                    sc.wait_ge(ev.h, ev.n)
                    ac.bump(sc.activation(sg_sb[:, :], ps_y[io % 2][:, :], AF.Silu))

                @blk.vector
                def _(v, io=io):
                    v.wait_ge(mm.h, mm.n)
                    v.wait_ge(ac.h, ac.n)
                    ev.bump(v.tensor_tensor(h_sb[:, io * 512:(io + 1) * 512],
                                            sg_sb[:, :], ps_u[io % 2][:, :], ALU.mult))

            if upto == "B":
                continue
            @blk.sync
            def _(s, th=th):
                s.wait_ge(st.h, st.n)
                for buf in range(2):
                    sld.bump(s.dma_start(
                        out=wd_sb[buf][:, :], in_=wdP_ap(buf)), 16)
                for k in range(16):
                    sld.bump(s.dma_start(
                        out=xr_sb[:, k * 512:(k + 1) * 512],
                        in_=x1T[k * 128:(k + 1) * 128, th * 512:th * 512 + 512]), 16)
            LD_B2 = sld.n

            for oc in range(16):
                @blk.sync
                def _(s, oc=oc):
                    if oc + 2 < 16:
                        s.wait_ge(mm.h, mm.n + 1)
                        sld.bump(s.dma_start(
                            out=wd_sb[oc % 2][:, :], in_=wdP_ap(oc + 2)), 16)

                @blk.tensor
                def _(t, oc=oc, LD_B2=LD_B2):
                    t.wait_ge(sld.h, LD_B2 + 16 * max(0, oc - 1))
                    if oc == 0:
                        t.wait_ge(ev.h, ev.n)
                    if oc >= 2:
                        t.wait_ge(ev.h, ev.n)
                    last = None
                    for io in range(NIO):
                        last = t.matmul(ps_y[oc % 2][:, :],
                                        wd_sb[oc % 2][:, io * 128:(io + 1) * 128],
                                        h_sb[:, io * 512:(io + 1) * 512],
                                        start=(io == 0), stop=(io == NIO - 1))
                    mm.bump(last)

                @blk.vector
                def _(v, oc=oc):
                    v.wait_ge(mm.h, mm.n)
                    v.tensor_tensor(y_sb[:, oc * 512:(oc + 1) * 512],
                                    ps_y[oc % 2][:, :],
                                    xr_sb[:, oc * 512:(oc + 1) * 512], ALU.add)
                    ev.bump(v.tensor_copy(ybf[oc % 2][:, :],
                                          y_sb[:, oc * 512:(oc + 1) * 512]))

                emit_stats(oc, ybf[oc % 2])

            emit_ln(th, x1b, x2T)

        @blk.scalar
        def _(sc):
            sc.wait_ge(st.h, st.n)

        sems.close()
    es.close()
    return st


def build_l0_ffn(upto="ALL", race=False):
    nc = bass.Bass("TRN2", target_bir_lowering=False, debug=False,
                   detect_race_conditions=race)
    woP = nc.dram_tensor("woP", [16, 16, 128, 128], F32, kind="ExternalInput")
    wgP = nc.dram_tensor("wgP", [NIO, 16, 128, 128], BF16, kind="ExternalInput")
    wuP = nc.dram_tensor("wuP", [NIO, 16, 128, 128], BF16, kind="ExternalInput")
    wdP = nc.dram_tensor("wdP", [16, NIO, 128, 128], BF16, kind="ExternalInput")
    io = {
        "attnTb": nc.dram_tensor("attnTb", [D, TOK], BF16, kind="ExternalInput"),
        "xresT": nc.dram_tensor("xresT", [D, TOK], F32, kind="ExternalInput"),
        "invn": nc.dram_tensor("invn", [128, 1], F32, kind="ExternalInput"),
        "x2T": nc.dram_tensor("x2T", [D, TOK], F32, kind="ExternalOutput"),
        "x1T": nc.dram_tensor("x1T", [D, TOK], F32),
        "rowsD": nc.dram_tensor("rowsD", [2, 512], F32),
        "woP_ap": lambda i: woP[i].rearrange("k p c -> p k c"),
        "wgP_ap": lambda i: wgP[i].rearrange("k p c -> p k c"),
        "wuP_ap": lambda i: wuP[i].rearrange("k p c -> p k c"),
        "wdP_ap": lambda i: wdP[i].rearrange("k p c -> p k c"),
    }
    with nc.Block() as blk:
        emit_l0_ffn(nc, blk, io, upto=upto)
    return nc


def host_prep_l0_ffn(inputs, attnT_full, WgT, WuT, WdT):
    x = inputs["L0_attn_hidden_states"]
    woT = np.ascontiguousarray(inputs["L0_attn_w_o"].T)
    woP = np.ascontiguousarray(
        woT.reshape(16, 128, 16, 128).transpose(2, 0, 1, 3))
    wg = np.ascontiguousarray(
        WgT.reshape(16, 128, NIO, 128).transpose(2, 0, 1, 3))
    wu = np.ascontiguousarray(
        WuT.reshape(16, 128, NIO, 128).transpose(2, 0, 1, 3))
    wd = np.ascontiguousarray(
        WdT.reshape(NIO, 128, 16, 128).transpose(2, 0, 1, 3))
    invn = np.full((128, 1), 1.0 / 2048.0, np.float32)
    maps = []
    for c in range(NCORES):
        b, half = c // 2, c % 2
        tc = slice(half * 1024, half * 1024 + 1024)
        maps.append({
            "attnTb": np.ascontiguousarray(attnT_full[b][:, tc]),
            "xresT": np.ascontiguousarray(x[b].T[:, tc]),
            "woP": woP, "wgP": wg, "wuP": wu, "wdP": wd, "invn": invn,
        })
    return maps



GC = 5504 // 8
DC = 2048 // 8
UPC = 8192 // 8
DNC = 2048 // 8


def build_wprep():
    nc = bass.Bass("TRN2", target_bir_lowering=False, debug=False,
                   detect_race_conditions=False)
    gidx = nc.dram_tensor("gidx", [D, GC], I32, kind="ExternalInput")
    uidx = nc.dram_tensor("uidx", [D, GC], I32, kind="ExternalInput")
    didx = nc.dram_tensor("didx", [5504, DC], I32, kind="ExternalInput")
    lutb = nc.dram_tensor("lutb", [128, 16], F32, kind="ExternalInput")
    h128 = nc.dram_tensor("h128", [128, 128], F32, kind="ExternalInput")
    wupT = nc.dram_tensor("wupT", [D, UPC], F32, kind="ExternalInput")
    wgtT = nc.dram_tensor("wgtT", [D, UPC], F32, kind="ExternalInput")
    wdnT = nc.dram_tensor("wdnT", [8192, DNC], F32, kind="ExternalInput")

    wg_o = nc.dram_tensor("wg_o", [D, GC], BF16, kind="ExternalOutput")
    wu_o = nc.dram_tensor("wu_o", [D, GC], BF16, kind="ExternalOutput")
    wd_o = nc.dram_tensor("wd_o", [5504, DC], BF16, kind="ExternalOutput")
    wupP = nc.dram_tensor("wupP", [D, UPC], BF16, kind="ExternalOutput")
    wgtP = nc.dram_tensor("wgtP", [D, UPC], BF16, kind="ExternalOutput")
    wdnP = nc.dram_tensor("wdnP", [8192, DNC], BF16, kind="ExternalOutput")

    zD = nc.dram_tensor("zD", [8192, UPC], BF16)

    es = ExitStack()
    sb = lambda name, shape, dt: es.enter_context(nc.sbuf_tensor(name, shape, dt))
    ps = lambda name, shape, dt: es.enter_context(nc.psum_tensor(name, shape, dt))

    idxb = [sb(f"idxb{i}", [128, 1024], BF16) for i in range(2)]
    accb = [sb(f"accb{i}", [128, GC], BF16) for i in range(2)]
    maskb = sb("maskb", [128, GC], BF16)
    lut_sb = sb("lut_sb", [128, 16], F32)
    h128b = sb("h128b", [128, 128], BF16)
    z_sb = sb("z_sb", [128, 16 * 1024], BF16)
    z2_sb = sb("z2_sb", [128, 64 * 256], BF16)
    ps_h = [ps(f"ps_h{i}", [128, 512], F32) for i in range(2)]

    with nc.Block() as blk:
        sems = ExitStack()
        gld = Ctr(nc, sems, "gld")
        mm = Ctr(nc, sems, "mm")
        ev = Ctr(nc, sems, "ev")
        st = Ctr(nc, sems, "st")
        bd = Ctr(nc, sems, "bd")

        @blk.gpsimd
        def _(g):
            gld.bump(g.dma_start(out=lut_sb[:, :], in_=lutb[:, :]), 16)
            gld.bump(g.dma_start(out=h128b[:, :], in_=h128[:, :]), 16)

        ghost_jobs = []
        for (idx_dr, out_dr, ntile, cols) in ((gidx, wg_o, 16, GC),
                                              (uidx, wu_o, 16, GC),
                                              (didx, wd_o, 43, DC)):
            for kt in range(ntile):
                ghost_jobs.append((idx_dr, out_dr, kt, cols))

        for ji, (idx_dr, out_dr, kt, cols) in enumerate(ghost_jobs):
            @blk.gpsimd
            def _(g, ji=ji, idx_dr=idx_dr, kt=kt, cols=cols):
                if ji >= 2:
                    g.wait_ge(ev.h, ji - 1)
                gld.bump(g.dma_start(
                    out=idxb[ji % 2][:, 0:cols],
                    in_=idx_dr[kt * 128:(kt + 1) * 128, :]), 16)

            @blk.vector
            def _(v, ji=ji, cols=cols):
                v.wait_ge(gld.h, 32 + 16 * (ji + 1))
                if ji >= 2:
                    v.wait_ge(st.h, 16 * (ji - 1))
                ib = idxb[ji % 2][:, 0:cols]
                ab = accb[ji % 2][:, 0:cols]
                mk = maskb[:, 0:cols]
                v.tensor_scalar(ab, ib, 0.0, lut_sb[:, 0:1], ALU.is_equal, ALU.mult)
                last = None
                for j in range(1, 16):
                    v.tensor_scalar(mk, ib, float(j), None, ALU.is_equal)
                    last = v.scalar_tensor_tensor(ab, mk, lut_sb[:, j:j + 1], ab,
                                                  ALU.mult, ALU.add)
                ev.bump(last)

            @blk.sync
            def _(s, ji=ji, out_dr=out_dr, kt=kt, cols=cols):
                s.wait_ge(ev.h, ji + 1)
                st.bump(s.dma_start(out=out_dr[kt * 128:(kt + 1) * 128, :],
                                    in_=accb[ji % 2][:, 0:cols]), 16)
        NGHOST = len(ghost_jobs)

        def emit_wprime(w_in, w_out, ntile, cols, nbf_stages):
            for kt in range(ntile):
                @blk.gpsimd
                def _(g, kt=kt, cols=cols, w_in=w_in, evn=ev.n):
                    g.wait_ge(ev.h, evn)
                    g.wait_ge(mm.h, mm.n)
                    gld.bump(g.dma_start(
                        out=idxb[kt % 2][:, 0:cols],
                        in_=w_in[kt * 128:(kt + 1) * 128, :]), 16)

                @blk.tensor
                def _(t, kt=kt, cols=cols):
                    t.wait_ge(gld.h, gld.n)
                    t.wait_ge(ev.h, ev.n)
                    nch = (cols + 511) // 512
                    last = None
                    for n in range(nch):
                        lo, hi = n * 512, min((n + 1) * 512, cols)
                        last = t.matmul(ps_h[n % 2][:, 0:hi - lo], h128b[:, :],
                                        idxb[kt % 2][:, lo:hi], start=True, stop=True)
                    mm.bump(last)

                @blk.vector
                def _(v, kt=kt, cols=cols, stn=st.n):
                    if kt == 0:
                        v.wait_ge(st.h, stn)
                    v.wait_ge(mm.h, mm.n)
                    nch = (cols + 511) // 512
                    lastv = None
                    for n in range(nch):
                        lo, hi = n * 512, min((n + 1) * 512, cols)
                        lastv = v.tensor_copy(z_sb[:, lo:hi], ps_h[n % 2][:, 0:hi - lo])
                    ev.bump(lastv)

                @blk.sync
                def _(s, kt=kt, cols=cols, bdn=bd.n):
                    if kt == 0:
                        s.wait_ge(bd.h, bdn)
                    s.wait_ge(ev.h, ev.n)
                    bd.bump(s.dma_start(out=zD[kt * 128:(kt + 1) * 128, 0:cols],
                                        in_=z_sb[:, 0:cols]), 16)

            @blk.sync
            def _(s, ntile=ntile, cols=cols, stn=st.n):
                s.wait_ge(st.h, stn)
                s.wait_ge(bd.h, bd.n)
                i = s.dma_start(
                    out=z2_sb[:, 0:ntile * cols],
                    in_=zD[0:ntile * 128, 0:cols].rearrange("(j p) o -> p j o", p=128))
                bd.bump(i, 16)

            @blk.vector
            def _(v, ntile=ntile, cols=cols, nbf=nbf_stages):
                v.wait_ge(bd.h, bd.n)
                hstep = 1
                for s_i in range(nbf):
                    ngrp = ntile // (2 * hstep)
                    apA = bass.AP(tensor=z2_sb, offset=0,
                                  ap=[[z2_sb.shape[1], 128],
                                      [2 * hstep * cols, ngrp], [1, hstep * cols]])
                    apB = bass.AP(tensor=z2_sb, offset=hstep * cols,
                                  ap=[[z2_sb.shape[1], 128],
                                      [2 * hstep * cols, ngrp], [1, hstep * cols]])
                    apA2 = bass.AP(tensor=z_sb, offset=0,
                                   ap=[[z_sb.shape[1], 128],
                                       [2 * hstep * cols, ngrp], [1, hstep * cols]])
                    apB2 = bass.AP(tensor=z_sb, offset=hstep * cols,
                                   ap=[[z_sb.shape[1], 128],
                                       [2 * hstep * cols, ngrp], [1, hstep * cols]])
                    srcA, srcB = (apA, apB) if s_i % 2 == 0 else (apA2, apB2)
                    dstA, dstB = (apA2, apB2) if s_i % 2 == 0 else (apA, apB)
                    v.tensor_tensor(dstA, srcA, srcB, ALU.add)
                    last = v.tensor_tensor(dstB, srcA, srcB, ALU.subtract)
                    hstep *= 2
                ev.bump(last)

            final_buf = z2_sb if (nbf_stages % 2 == 0) else z_sb

            @blk.sync
            def _(s, ntile=ntile, cols=cols, w_out=w_out, final_buf=final_buf):
                s.wait_ge(ev.h, ev.n)
                i = s.dma_start(
                    out=w_out[0:ntile * 128, 0:cols].rearrange("(j p) o -> p j o", p=128),
                    in_=bass.AP(tensor=final_buf, offset=0,
                                ap=[[final_buf.shape[1], 128], [cols, ntile], [1, cols]]))
                st.bump(i, 16)

        emit_wprime(wupT, wupP, 16, UPC, 4)
        emit_wprime(wgtT, wgtP, 16, UPC, 4)
        emit_wprime(wdnT, wdnP, 64, DNC, 6)

        @blk.sync
        def _(s):
            s.wait_ge(st.h, st.n)

        sems.close()
    es.close()
    return nc


def host_prep_wprep(inputs):
    gT = np.ascontiguousarray(inputs["L0_ffn_gate_idx"].T)
    uT = np.ascontiguousarray(inputs["L0_ffn_up_idx"].T)
    dT = np.ascontiguousarray(inputs["L0_ffn_down_idx"].T)
    upT = np.ascontiguousarray(inputs["L1_ffn_W_up"].T)
    gtT = np.ascontiguousarray(inputs["L1_ffn_W_gate"].T)
    dnT = np.ascontiguousarray(inputs["L1_ffn_W_down"].T)
    lut = np.ascontiguousarray(
        np.broadcast_to(inputs["L0_ffn_lut"][None, :], (128, 16))).astype(np.float32)
    H = np.array([[1.0]], np.float32)
    while H.shape[0] < 128:
        H = np.block([[H, H], [H, -H]])
    maps = []
    for c in range(NCORES):
        maps.append({
            "gidx": np.ascontiguousarray(gT[:, c * GC:(c + 1) * GC]),
            "uidx": np.ascontiguousarray(uT[:, c * GC:(c + 1) * GC]),
            "didx": np.ascontiguousarray(dT[:, c * DC:(c + 1) * DC]),
            "lutb": lut,
            "h128": np.ascontiguousarray(H),
            "wupT": np.ascontiguousarray(upT[:, c * UPC:(c + 1) * UPC]),
            "wgtT": np.ascontiguousarray(gtT[:, c * UPC:(c + 1) * UPC]),
            "wdnT": np.ascontiguousarray(dnT[:, c * DNC:(c + 1) * DNC]),
        })
    return maps



def emit_l1_ffn(nc, blk_outer, io, mkctr=None, pfx="L"):
    ZTb, xresT, invn = io["ZTb"], io["xresT"], io["invn"]
    h128d = io["h128"]
    woP_ap = io["l1woP_ap"]
    wupP_ap, wgtP_ap, wdnP_ap = io["wupP_ap"], io["wgtP_ap"], io["wdnP_ap"]
    su_col, wsu_col, wsg_col = io["su_col"], io["wsu_col"], io["wsg_col"]
    sudp_col, wsd_col = io["sudp_col"], io["wsd_col"]
    x4T, x3T, rowsD = io["x4T"], io["x3T"], io["rowsD"]
    dbg_h, dbg_y0 = io.get("dbg_h"), io.get("dbg_y0")

    es = ExitStack()
    sb = lambda name, shape, dt: es.enter_context(nc.sbuf_tensor(name, shape, dt))
    ps = lambda name, shape, dt: es.enter_context(nc.psum_tensor(name, shape, dt))

    xr_sb = sb("xr_sb", [128, 16 * 512], F32)
    y_sb = sb("y_sb", [128, 16 * 512], F32)
    ybf = [sb(f"ybf{i}", [128, 512], BF16) for i in range(2)]
    x3b = sb("x3b", [128, 16 * 1024], BF16)
    h_sb = sb("h_sb", [128, 32 * 512], BF16)
    at_sb = sb("at_sb", [128, 16 * 512], BF16)
    wo_sb = [sb(f"wo_sb{i}", [128, 16 * 128], BF16) for i in range(2)]
    wg_sb = [sb(f"wg_sb{i}", [128, 16 * 128], BF16) for i in range(2)]
    wu_sb = [sb(f"wu_sb{i}", [128, 16 * 128], BF16) for i in range(2)]
    wd_sb = [sb(f"wd_sb{i}", [128, 32 * 128], BF16) for i in range(2)]
    sg_sb = sb("sg_sb", [128, 512], BF16)
    sq_sb = sb("sq_sb", [128, 512], BF16)
    gsc_sb = sb("gsc_sb", [128, 512], BF16)
    inv_sb = sb("inv_sb", [128, 1], BF16)
    su_sb = sb("su_sb", [128, 16], F32)
    h128L = sb("h128L", [128, 128], BF16)
    wsu_sb = sb("wsu_sb", [128, 64], F32)
    wsg_sb = sb("wsg_sb", [128, 64], F32)
    sudp_sb = sb("sudp_sb", [128, 64], F32)
    wsd_sb = sb("wsd_sb", [128, 16], F32)
    sup_sb = sb("sup_sb", [128, 64], F32)
    r1 = sb("r1", [1, 512], F32)
    r2 = sb("r2", [1, 512], F32)
    rtmp = sb("rtmp", [1, 512], F32)
    mbb = sb("mbb", [128, 512], F32)
    rbb = sb("rbb", [128, 512], F32)
    x1f = [sb(f"x1f{i}", [128, 512], F32) for i in range(2)]

    ps_y = [ps(f"ps_y{i}", [128, 512], F32) for i in range(2)]
    ps_u = [ps(f"ps_u{i}", [128, 512], F32) for i in range(2)]
    ps_m = ps("ps_m", [1, 512], F32)
    ps_v = ps("ps_v", [1, 512], F32)

    RS2048 = float(1.0 / np.sqrt(2048.0))
    RS8192 = float(1.0 / np.sqrt(8192.0))

    if True:
        blk = blk_outer
        sems = ExitStack()
        if mkctr is None:
            mkctr = lambda name: Ctr(nc, sems, pfx + name)
        gld = mkctr("gld")
        sld = mkctr("sld")
        mm = mkctr("mm")
        ev = mkctr("ev")
        ac = mkctr("ac")
        st = mkctr("st")
        bc = mkctr("bc")

        @blk.gpsimd
        def _(g):
            gld.bump(g.dma_start(out=inv_sb[:, :], in_=invn[:, :]), 16)
            gld.bump(g.dma_start(out=h128L[:, :], in_=h128d[:, :]), 16)
            for dst, srcd in ((su_sb, su_col), (wsu_sb, wsu_col), (wsg_sb, wsg_col),
                              (sudp_sb, sudp_col), (wsd_sb, wsd_col)):
                gld.bump(g.dma_start(out=dst[:, :], in_=srcd[:, :]), 16)

        @blk.vector
        def _(v):
            v.wait_ge(gld.h, gld.n)
            v.tensor_tensor(sup_sb[:, :], wsu_sb[:, :], sudp_sb[:, :], ALU.mult)
            v.tensor_scalar(sup_sb[:, :], sup_sb[:, :], RS2048, None, ALU.mult)
            v.tensor_scalar(wsg_sb[:, :], wsg_sb[:, :], RS2048, None, ALU.mult)
            ev.bump(v.tensor_scalar(wsd_sb[:, :], wsd_sb[:, :], RS8192, None,
                                    ALU.mult))
        SCALES_EV = ev.n

        def emit_stats(oc, src_bf):
            @blk.scalar
            def _(sc, oc=oc, src_bf=src_bf):
                sc.wait_ge(ev.h, ev.n)
                sc.wait_ge(mm.h, mm.n)
                ac.bump(sc.activation(sq_sb[:, :], src_bf[:, :], AF.Square))

            @blk.tensor
            def _(t, oc=oc, src_bf=src_bf):
                t.wait_ge(ev.h, ev.n)
                t.matmul(ps_m[:, :], inv_sb[:, :], src_bf[:, :],
                         start=(oc == 0), stop=(oc == 15))
                t.wait_ge(ac.h, ac.n)
                mm.bump(t.matmul(ps_v[:, :], inv_sb[:, :], sq_sb[:, :],
                                 start=(oc == 0), stop=(oc == 15)))

        def emit_ln(th, out_mode):
            @blk.vector
            def _(v):
                v.wait_ge(mm.h, mm.n)
                v.tensor_copy(r1[:, :], ps_m[:, :])
                v.tensor_copy(r2[:, :], ps_v[:, :])
                v.tensor_tensor(rtmp[0:1, :], r1[0:1, :], r1[0:1, :], ALU.mult)
                v.tensor_tensor(r2[0:1, :], r2[0:1, :], rtmp[0:1, :], ALU.subtract)
                ev.bump(v.tensor_scalar(r2[0:1, :], r2[0:1, :], 1e-5, None, ALU.add))

            @blk.scalar
            def _(sc):
                sc.wait_ge(ev.h, ev.n)
                ac.bump(sc.activation(r2[0:1, :], r2[0:1, :], AF.Sqrt))

            @blk.vector
            def _(v):
                v.wait_ge(ac.h, ac.n)
                ev.bump(v.reciprocal(r2[0:1, :], r2[0:1, :]))

            @blk.gpsimd
            def _(g):
                g.wait_ge(ev.h, ev.n)
                g.dma_start(out=rowsD[0:1, :], in_=r1[0:1, :]).then_inc(bc.h, 16)
                g.dma_start(out=rowsD[1:2, :], in_=r2[0:1, :]).then_inc(bc.h, 16)
                g.wait_ge(bc.h, bc.n + 32)
                bm = bass.AP(tensor=rowsD, offset=0, ap=[[0, 128], [1, 512]])
                br = bass.AP(tensor=rowsD, offset=512, ap=[[0, 128], [1, 512]])
                g.dma_start(out=mbb[:, :], in_=bm).then_inc(bc.h, 16)
                g.dma_start(out=rbb[:, :], in_=br).then_inc(bc.h, 16)
            bc.n += 64

            for oc in range(16):
                @blk.vector
                def _(v, oc=oc, th=th, out_mode=out_mode):
                    if oc == 0:
                        v.wait_ge(bc.h, bc.n)
                    if oc >= 2:
                        v.wait_ge(st.h, st.n - 16)
                    xf = x1f[oc % 2]
                    v.tensor_tensor(xf[:, :], y_sb[:, oc * 512:(oc + 1) * 512],
                                    mbb[:, :], ALU.subtract)
                    last = v.tensor_tensor(xf[:, :], xf[:, :], rbb[:, :], ALU.mult)
                    if out_mode == "x3":
                        last = v.tensor_scalar(
                            x3b[:, oc * 1024 + th * 512: oc * 1024 + th * 512 + 512],
                            xf[:, :], su_sb[:, oc:oc + 1], None, ALU.mult)
                    ev.bump(last)

                @blk.scalar
                def _(sc, oc=oc, th=th, out_mode=out_mode):
                    sc.wait_ge(ev.h, ev.n)
                    dstd = x3T if out_mode == "x3" else x4T
                    st.bump(sc.dma_start(
                        out=dstd[oc * 128:(oc + 1) * 128, th * 512:th * 512 + 512],
                        in_=x1f[oc % 2][:, :]), 16)

        for th in range(2):
            t0 = th * 512

            @blk.sync
            def _(s, th=th, t0=t0):
                if th == 1:
                    s.wait_ge(mm.h, mm.n)
                for k in range(16):
                    sld.bump(s.dma_start(
                        out=at_sb[:, k * 512:(k + 1) * 512],
                        in_=ZTb[k * 128:(k + 1) * 128, t0:t0 + 512]), 16)
                for k in range(16):
                    sld.bump(s.dma_start(
                        out=xr_sb[:, k * 512:(k + 1) * 512],
                        in_=xresT[k * 128:(k + 1) * 128, t0:t0 + 512]), 16)
            LD_A = sld.n

            @blk.gpsimd
            def _(g, th=th):
                if th == 1:
                    g.wait_ge(mm.h, mm.n)
                for oc in range(2):
                    gld.bump(g.dma_start(
                        out=wo_sb[oc][:, :], in_=woP_ap(oc)), 16)
            GP_A = gld.n

            for oc in range(16):
                @blk.gpsimd
                def _(g, oc=oc):
                    if oc + 2 < 16:
                        g.wait_ge(mm.h, mm.n + 1)
                        gld.bump(g.dma_start(
                            out=wo_sb[oc % 2][:, :], in_=woP_ap(oc + 2)), 16)

                @blk.tensor
                def _(t, oc=oc, GP_A=GP_A, LD_A=LD_A):
                    if oc == 0:
                        t.wait_ge(sld.h, LD_A)
                    t.wait_ge(gld.h, GP_A + 16 * max(0, oc - 1))
                    if oc >= 2:
                        t.wait_ge(ev.h, ev.n)
                    last = None
                    for k in range(16):
                        last = t.matmul(ps_y[oc % 2][:, :],
                                        wo_sb[oc % 2][:, k * 128:(k + 1) * 128],
                                        at_sb[:, k * 512:(k + 1) * 512],
                                        start=(k == 0), stop=(k == 15))
                    mm.bump(last)

                @blk.vector
                def _(v, oc=oc):
                    v.wait_ge(mm.h, mm.n)
                    v.tensor_tensor(y_sb[:, oc * 512:(oc + 1) * 512],
                                    ps_y[oc % 2][:, :],
                                    xr_sb[:, oc * 512:(oc + 1) * 512], ALU.add)
                    ev.bump(v.tensor_copy(ybf[oc % 2][:, :],
                                          y_sb[:, oc * 512:(oc + 1) * 512]))

                emit_stats(oc, ybf[oc % 2])

            emit_ln(th, "x3")

            for k in range(16):
                @blk.tensor
                def _(t, k=k, th=th):
                    t.wait_ge(ev.h, ev.n)
                    c0 = k * 1024 + th * 512
                    mm.bump(t.matmul(ps_y[k % 2][:, :], h128L[:, :],
                                     x3b[:, c0:c0 + 512], start=True, stop=True))

                @blk.vector
                def _(v, k=k, th=th):
                    v.wait_ge(mm.h, mm.n)
                    c0 = k * 1024 + th * 512
                    ev.bump(v.tensor_copy(x3b[:, c0:c0 + 512], ps_y[k % 2][:, :]))

            @blk.vector
            def _(v, th=th):
                v.wait_ge(ev.h, ev.n)
                rl = x3b.shape[1]
                cur = th
                last = None
                h = 1
                for s_i in range(4):
                    ngrp = 16 // (2 * h)
                    dsth = 1 - cur
                    for j in range(h):
                        apSA = bass.AP(tensor=x3b, offset=j * 1024 + cur * 512,
                                       ap=[[rl, 128], [2 * h * 1024, ngrp], [1, 512]])
                        apSB = bass.AP(tensor=x3b,
                                       offset=(j + h) * 1024 + cur * 512,
                                       ap=[[rl, 128], [2 * h * 1024, ngrp], [1, 512]])
                        apDA = bass.AP(tensor=x3b, offset=j * 1024 + dsth * 512,
                                       ap=[[rl, 128], [2 * h * 1024, ngrp], [1, 512]])
                        apDB = bass.AP(tensor=x3b,
                                       offset=(j + h) * 1024 + dsth * 512,
                                       ap=[[rl, 128], [2 * h * 1024, ngrp], [1, 512]])
                        v.tensor_tensor(apDA, apSA, apSB, ALU.add)
                        last = v.tensor_tensor(apDB, apSA, apSB, ALU.subtract)
                    cur = dsth
                    h *= 2
                assert cur == th
                ev.bump(last)

            @blk.tensor
            def _(t):
                t.wait_ge(ev.h, ev.n)

            for ioh in range(2):
                @blk.gpsimd
                def _(g, ioh=ioh, th=th):
                    g.wait_ge(st.h, st.n)
                    g.wait_ge(mm.h, mm.n)
                    for buf in range(2):
                        gld.bump(g.dma_start(
                            out=wg_sb[buf][:, :], in_=wgtP_ap(ioh * 32 + buf)), 16)
                        gld.bump(g.dma_start(
                            out=wu_sb[buf][:, :], in_=wupP_ap(ioh * 32 + buf)), 16)
                LD_B = gld.n

                for io in range(32):
                    gio = ioh * 32 + io

                    @blk.gpsimd
                    def _(g, io=io, gio=gio):
                        if io + 2 < 32:
                            g.wait_ge(mm.h, mm.n + 2)
                            gld.bump(g.dma_start(
                                out=wg_sb[io % 2][:, :], in_=wgtP_ap(gio + 2)), 16)
                            gld.bump(g.dma_start(
                                out=wu_sb[io % 2][:, :], in_=wupP_ap(gio + 2)), 16)

                    @blk.tensor
                    def _(t, io=io, th=th, LD_B=LD_B):
                        t.wait_ge(gld.h, LD_B + 32 * max(0, io - 1))
                        t.wait_ge(ev.h, ev.n)
                        last = None
                        for k in range(16):
                            last = t.matmul(
                                ps_y[io % 2][:, :],
                                wg_sb[io % 2][:, k * 128:(k + 1) * 128],
                                x3b[:, k * 1024 + th * 512: k * 1024 + th * 512 + 512],
                                start=(k == 0), stop=(k == 15))
                        mm.bump(last)
                        last = None
                        for k in range(16):
                            last = t.matmul(
                                ps_u[io % 2][:, :],
                                wu_sb[io % 2][:, k * 128:(k + 1) * 128],
                                x3b[:, k * 1024 + th * 512: k * 1024 + th * 512 + 512],
                                start=(k == 0), stop=(k == 15))
                        mm.bump(last)

                    @blk.vector
                    def _(v, io=io, gio=gio):
                        v.wait_ge(mm.h, mm.n - 1)
                        v.wait_ge(ac.h, ac.n)
                        ev.bump(v.tensor_scalar(gsc_sb[:, :], ps_y[io % 2][:, :],
                                                wsg_sb[:, gio:gio + 1], None,
                                                ALU.mult))

                    @blk.scalar
                    def _(sc, io=io):
                        sc.wait_ge(ev.h, ev.n)
                        ac.bump(sc.activation(sg_sb[:, :], gsc_sb[:, :], AF.Silu))

                    @blk.vector
                    def _(v, io=io, gio=gio):
                        v.wait_ge(mm.h, mm.n)
                        v.wait_ge(ac.h, ac.n)
                        ev.bump(v.scalar_tensor_tensor(
                            h_sb[:, io * 512:(io + 1) * 512],
                            ps_u[io % 2][:, :], sup_sb[:, gio:gio + 1], sg_sb[:, :],
                            ALU.mult, ALU.mult))

                @blk.sync
                def _(s, ioh=ioh):
                    s.wait_ge(mm.h, mm.n)
                    for buf in range(2):
                        sld.bump(s.dma_start(
                            out=wd_sb[buf][:, :], in_=wdnP_ap(buf, ioh)), 16)
                LD_D = sld.n

                for oc in range(16):
                    @blk.sync
                    def _(s, oc=oc, ioh=ioh):
                        if oc + 2 < 16:
                            s.wait_ge(mm.h, mm.n + 1)
                            sld.bump(s.dma_start(
                                out=wd_sb[oc % 2][:, :], in_=wdnP_ap(oc + 2, ioh)), 16)

                    @blk.tensor
                    def _(t, oc=oc, LD_D=LD_D):
                        t.wait_ge(sld.h, LD_D + 16 * max(0, oc - 1))
                        if oc == 0:
                            t.wait_ge(ev.h, ev.n)
                        if oc >= 2:
                            t.wait_ge(ev.h, ev.n)
                        last = None
                        for io in range(32):
                            last = t.matmul(ps_y[oc % 2][:, :],
                                            wd_sb[oc % 2][:, io * 128:(io + 1) * 128],
                                            h_sb[:, io * 512:(io + 1) * 512],
                                            start=(io == 0), stop=(io == 31))
                        mm.bump(last)

                    @blk.vector
                    def _(v, oc=oc, ioh=ioh):
                        v.wait_ge(mm.h, mm.n)
                        if ioh == 0:
                            ev.bump(v.tensor_copy(y_sb[:, oc * 512:(oc + 1) * 512],
                                                  ps_y[oc % 2][:, :]))
                        else:
                            ev.bump(v.tensor_tensor(y_sb[:, oc * 512:(oc + 1) * 512],
                                                    y_sb[:, oc * 512:(oc + 1) * 512],
                                                    ps_y[oc % 2][:, :], ALU.add))

            if dbg_h is not None and th == 0:
                @blk.sync
                def _(s):
                    s.wait_ge(ev.h, ev.n)
                    st.bump(s.dma_start(out=dbg_h[:, :], in_=h_sb[:, :]), 16)
                    st.bump(s.dma_start(out=dbg_y0[:, :], in_=y_sb[:, :]), 16)

            @blk.sync
            def _(s, th=th):
                s.wait_ge(st.h, st.n)
                for k in range(16):
                    sld.bump(s.dma_start(
                        out=xr_sb[:, k * 512:(k + 1) * 512],
                        in_=x3T[k * 128:(k + 1) * 128, th * 512:th * 512 + 512]), 16)
            LD_R = sld.n

            for oc in range(16):
                @blk.vector
                def _(v, oc=oc, LD_R=LD_R):
                    if oc == 0:
                        v.wait_ge(sld.h, LD_R)
                        v.wait_ge(st.h, st.n)
                    if oc >= 2:
                        v.wait_ge(mm.h, mm.n - 1)
                    v.scalar_tensor_tensor(
                        y_sb[:, oc * 512:(oc + 1) * 512],
                        y_sb[:, oc * 512:(oc + 1) * 512], wsd_sb[:, oc:oc + 1],
                        xr_sb[:, oc * 512:(oc + 1) * 512], ALU.mult, ALU.add)
                    ev.bump(v.tensor_copy(ybf[oc % 2][:, :],
                                          y_sb[:, oc * 512:(oc + 1) * 512]))

                emit_stats(oc, ybf[oc % 2])

            emit_ln(th, "x4")

        @blk.scalar
        def _(sc):
            sc.wait_ge(st.h, st.n)

        sems.close()
    es.close()
    return st


def build_l1_ffn(dbg=False):
    nc = bass.Bass("TRN2", target_bir_lowering=False, debug=False,
                   detect_race_conditions=False)
    woP = nc.dram_tensor("woP", [16, 16, 128, 128], F32, kind="ExternalInput")
    wupP = nc.dram_tensor("wupP", [64, 16, 128, 128], BF16, kind="ExternalInput")
    wgtP = nc.dram_tensor("wgtP", [64, 16, 128, 128], BF16, kind="ExternalInput")
    wdnP = nc.dram_tensor("wdnP", [16, 64, 128, 128], BF16, kind="ExternalInput")
    io = {
        "ZTb": nc.dram_tensor("ZTb", [D, TOK], BF16, kind="ExternalInput"),
        "xresT": nc.dram_tensor("xresT", [D, TOK], F32, kind="ExternalInput"),
        "su_col": nc.dram_tensor("su_col", [128, 16], F32, kind="ExternalInput"),
        "wsu_col": nc.dram_tensor("wsu_col", [128, 64], F32, kind="ExternalInput"),
        "wsg_col": nc.dram_tensor("wsg_col", [128, 64], F32, kind="ExternalInput"),
        "sudp_col": nc.dram_tensor("sudp_col", [128, 64], F32, kind="ExternalInput"),
        "wsd_col": nc.dram_tensor("wsd_col", [128, 16], F32, kind="ExternalInput"),
        "invn": nc.dram_tensor("invn", [128, 1], F32, kind="ExternalInput"),
        "h128": nc.dram_tensor("h128", [128, 128], F32, kind="ExternalInput"),
        "x4T": nc.dram_tensor("x4T", [D, TOK], F32, kind="ExternalOutput"),
        "x3T": nc.dram_tensor("x3T", [D, TOK], F32, kind="ExternalOutput"),
        "dbg_h": nc.dram_tensor("dbg_h", [128, 32 * 512], BF16,
                                kind="ExternalOutput" if dbg else "Internal"),
        "dbg_y0": nc.dram_tensor("dbg_y0", [128, 16 * 512], F32,
                                 kind="ExternalOutput" if dbg else "Internal"),
        "rowsD": nc.dram_tensor("rowsD", [2, 512], F32),
        "l1woP_ap": lambda i: woP[i].rearrange("k p c -> p k c"),
        "wupP_ap": lambda i: wupP[i].rearrange("k p c -> p k c"),
        "wgtP_ap": lambda i: wgtP[i].rearrange("k p c -> p k c"),
        "wdnP_ap": lambda oc, ioh: wdnP[oc, ioh * 32:(ioh + 1) * 32].rearrange(
            "k p c -> p k c"),
    }
    with nc.Block() as blk:
        emit_l1_ffn(nc, blk, io)
    return nc


def host_prep_l1_ffn(inputs, ZT_full, x2_full):
    woT = np.ascontiguousarray(inputs["L1_attn_Wo"].T)
    woP = np.ascontiguousarray(woT.reshape(16, 128, 16, 128).transpose(2, 0, 1, 3))
    invn = np.full((128, 1), 1.0 / 2048.0, np.float32)
    su_col = np.ascontiguousarray(
        inputs["L1_ffn_SU_ug"].reshape(16, 128).T).astype(np.float32)
    wsu_col = np.ascontiguousarray(
        inputs["L1_ffn_Wscale_ug"][:8192].reshape(64, 128).T).astype(np.float32)
    wsg_col = np.ascontiguousarray(
        inputs["L1_ffn_Wscale_ug"][8192:].reshape(64, 128).T).astype(np.float32)
    sudp_col = np.ascontiguousarray(
        inputs["L1_ffn_SU_dp"].reshape(64, 128).T).astype(np.float32)
    wsd_col = np.ascontiguousarray(
        inputs["L1_ffn_Wscale_dp"].reshape(16, 128).T).astype(np.float32)
    maps = []
    for c in range(NCORES):
        b, half = c // 2, c % 2
        tc = slice(half * 1024, half * 1024 + 1024)
        maps.append({
            "ZTb": np.ascontiguousarray(ZT_full[b][:, tc]),
            "xresT": np.ascontiguousarray(x2_full[b][:, tc]),
            "woP": woP, "invn": invn,
            "su_col": su_col, "wsu_col": wsu_col, "wsg_col": wsg_col,
            "sudp_col": sudp_col, "wsd_col": wsd_col,
        })
    return maps


def panelize_l1_w(wupP_full, wgtP_full, wdnP_full):
    up = np.ascontiguousarray(
        wupP_full.reshape(16, 128, 64, 128).transpose(2, 0, 1, 3))
    gt = np.ascontiguousarray(
        wgtP_full.reshape(16, 128, 64, 128).transpose(2, 0, 1, 3))
    dn = np.ascontiguousarray(
        wdnP_full.reshape(64, 128, 16, 128).transpose(2, 0, 1, 3))
    return up, gt, dn



def emit_l1_attn(nc, blk_outer, io, mkctr=None, nch=128, SCAN_PARTS=5, pfx="S"):
    x2Tb, wqT, wkT, wvT = io["x2Tb"], io["l1wqT"], io["l1wkT"], io["l1wvT"]
    wilrT, bilr, w1in, pen = io["wilrT"], io["bilr"], io["w1in"], io["pen"]
    ident, trimask, Zraw, ilrD = io["ident"], io["trimask"], io["Zraw"], io["ilrD"]

    es = ExitStack()
    sb = lambda name, shape, dt: es.enter_context(nc.sbuf_tensor(pfx + name, shape, dt))
    ps = lambda name, shape, dt: es.enter_context(nc.psum_tensor(pfx + name, shape, dt))

    xb_t = sb("xb_t", [128, 8 * S], BF16)
    xc_t = sb("xc_t", [128, 8 * S], BF16)
    xa_t = sb("xa_t", [128, 8 * S], BF16)
    alias = sb("alias", [128, 16 * S], BF16)
    ilr_sb = sb("ilr_sb", [32, S], F32)
    penb = sb("penb", [32, S], F32)
    s_sb = sb("s_sb", [128, S], F32)
    w1i_sb = sb("w1i_sb", [128, 512], F32)
    id_f = sb("id_f", [128, 128], F32)
    id_b = sb("id_b", [128, 128], BF16)
    tri_b = sb("tri_b", [128, 128], BF16)
    bilr_sb = sb("bilr_sb", [16, 1], F32)
    wq_sb = sb("wq_sb", [128, 16 * 128], BF16)
    wk_sb = sb("wk_sb", [128, 16 * 128], BF16)
    wv_sb = sb("wv_sb", [128, 16 * 128], BF16)
    wi_sb = sb("wi_sb", [128, 16 * 16], BF16)
    w1sb = sb("w1sb", [128, 512], BF16)
    z1sb = sb("z1sb", [128, 256], BF16)
    z1n = sb("z1n", [128, 256], BF16)
    xbn = sb("xbn", [128, 256], BF16)
    at_s = sb("at_s", [128, 128], BF16)
    gbuf = sb("gbuf", [128, 256], BF16)
    zbuf = [sb(f"zbuf{i}", [128, 128], BF16) for i in range(2)]

    ps_w1 = ps("ps_w1", [128, 512], F32)
    ps_z1 = ps("ps_z1", [128, 512], F32)
    ps_zb = ps("ps_zb", [128, 512], F32)
    ps_at = ps("ps_at", [128, 512], F32)
    ps_az = ps("ps_az", [128, 512], F32)
    ps_t1 = ps("ps_t1", [128, 512], BF16)
    ps_t2 = ps("ps_t2", [128, 512], BF16)

    XCp = alias
    XBp_OFF = 8 * S

    if True:
        blk = blk_outer
        sems = ExitStack()
        if mkctr is None:
            mkctr = lambda name: Ctr(nc, sems, pfx + name)
        gld = mkctr("gld")
        mm = mkctr("mm")
        ev = mkctr("ev")
        ac = mkctr("ac")
        st = mkctr("st")
        bc = mkctr("bc")

        @blk.gpsimd
        def _(g):
            for k in range(16):
                gld.bump(g.dma_start(out=alias[:, k * S:(k + 1) * S],
                                     in_=x2Tb[k * 128:(k + 1) * 128, :]), 16)
            gld.bump(g.dma_start(out=w1i_sb[:, :], in_=w1in[:, :]), 16)
            gld.bump(g.dma_start(out=id_f[:, :], in_=ident[:, :]), 16)
            gld.bump(g.dma_start(out=id_b[:, :], in_=ident[:, :]), 16)
            gld.bump(g.dma_start(out=tri_b[:, :], in_=trimask[:, :]), 16)
            gld.bump(g.dma_start(out=bilr_sb[:, :], in_=bilr[:, :]), 16)
            pb = bass.AP(tensor=pen, offset=0, ap=[[0, 32], [1, S]])
            gld.bump(g.dma_start(out=penb[:, :], in_=pb), 16)
            for k in range(16):
                gld.bump(g.dma_start(out=wi_sb[:, k * 16:(k + 1) * 16],
                                     in_=wilrT[k * 128:(k + 1) * 128, :]), 16)
        LD0 = gld.n

        for p in range(8):
            @blk.gpsimd
            def _(g, p=p):
                if p > 0:
                    g.wait_ge(mm.h, mm.n)
                for w_sb, w_dr in ((wq_sb, wqT), (wk_sb, wkT), (wv_sb, wvT)):
                    for k in range(16):
                        gld.bump(g.dma_start(
                            out=w_sb[:, k * 128:(k + 1) * 128],
                            in_=w_dr[k * 128:(k + 1) * 128,
                                     p * 128:(p + 1) * 128]), 16)

            for wi, (w_sb, dst) in enumerate(((wq_sb, xc_t), (wk_sb, xb_t),
                                              (wv_sb, xa_t))):
                for nh2 in range(2):
                    @blk.tensor
                    def _(t, p=p, wi=wi, w_sb=w_sb, nh2=nh2):
                        t.wait_ge(gld.h, gld.n)
                        if not (p == 0 and wi == 0 and nh2 == 0):
                            t.wait_ge(ev.h, ev.n)
                        last = None
                        for k in range(16):
                            for n in range(2):
                                c0 = k * S + nh2 * 1024 + n * 512
                                last = t.matmul((ps_z1 if n == 0 else ps_zb)[:, 0:512],
                                                w_sb[:, k * 128:(k + 1) * 128],
                                                alias[:, c0:c0 + 512],
                                                start=(k == 0), stop=(k == 15))
                        mm.bump(last)

                    @blk.vector
                    def _(v, p=p, dst=dst, nh2=nh2):
                        v.wait_ge(mm.h, mm.n)
                        c0 = p * S + nh2 * 1024
                        v.tensor_copy(dst[:, c0:c0 + 512], ps_z1[:, 0:512])
                        ev.bump(v.tensor_copy(dst[:, c0 + 512:c0 + 1024],
                                              ps_zb[:, 0:512]))

        for ch4 in range(4):
            @blk.tensor
            def _(t, ch4=ch4):
                t.wait_ge(ev.h, ev.n)
                t.wait_ge(ac.h, ac.n)
                last = None
                for k in range(16):
                    c0 = k * S + ch4 * 512
                    last = t.matmul(ps_z1[0:16, 0:512],
                                    wi_sb[:, k * 16:(k + 1) * 16],
                                    alias[:, c0:c0 + 512],
                                    start=(k == 0), stop=(k == 15))
                mm.bump(last)

            @blk.scalar
            def _(sc, ch4=ch4):
                sc.wait_ge(mm.h, mm.n)
                ac.bump(sc.activation(ilr_sb[0:16, ch4 * 512:(ch4 + 1) * 512],
                                      ps_z1[0:16, 0:512], AF.Sigmoid,
                                      bias=bilr_sb[0:16, 0:1]))

        @blk.vector
        def _(v):
            v.wait_ge(ac.h, ac.n)
            ev.bump(v.tensor_tensor(ilr_sb[0:16, :], ilr_sb[0:16, :],
                                    penb[0:16, :], ALU.mult))

        @blk.gpsimd
        def _(g):
            g.wait_ge(ev.h, ev.n)
            bc.bump(g.dma_start(out=ilrD[:, :], in_=ilr_sb[0:16, :]), 16)

        for p in range(8):
            @blk.gpsimd
            def _(g, p=p):
                g.wait_ge(bc.h, 16)
                if p == 0:
                    g.wait_ge(ev.h, ev.n)
                else:
                    g.wait_ge(ev.h, ev.n)
                b0 = bass.AP(tensor=ilrD, offset=(2 * p) * S, ap=[[0, 64], [1, S]])
                b1 = bass.AP(tensor=ilrD, offset=(2 * p + 1) * S, ap=[[0, 64], [1, S]])
                bc.bump(g.dma_start(out=s_sb[0:64, :], in_=b0), 16)
                bc.bump(g.dma_start(out=s_sb[64:128, :], in_=b1), 16)

            @blk.vector
            def _(v, p=p):
                v.wait_ge(bc.h, bc.n)
                v.tensor_tensor(XCp[:, p * S:(p + 1) * S], xc_t[:, p * S:(p + 1) * S],
                                s_sb[:, :], ALU.mult)
                ev.bump(v.scalar_tensor_tensor(
                    alias[:, XBp_OFF + p * S: XBp_OFF + (p + 1) * S],
                    xb_t[:, p * S:(p + 1) * S], -1.0, s_sb[:, :],
                    ALU.mult, ALU.mult))

        @blk.vector
        def _(v):
            v.wait_ge(ev.h, ev.n)
            v.memset(gbuf[:, :], 0.0)
            ev.bump(v.memset(z1sb[:, :], 0.0))

        @blk.tensor
        def _(t):
            t.wait_ge(gld.h, LD0)
            t.wait_ge(ev.h, ev.n)
            mm.bump(t.matmul(ps_w1[:, :], id_f[:, :], w1i_sb[:, :],
                             start=True, stop=False, skip_group_check=True))

        def head_coords(p, s):
            return (p % 4) * 32, (p // 4) * 32 + s * 16

        for ch in range(nch):
            @blk.vector
            def _(v, ch=ch):
                v.wait_ge(mm.h, mm.n)
                v.tensor_copy(w1sb[:, :], ps_w1[:, :])
                last = None
                for g4 in range(2):
                    srcg = bass.AP(tensor=alias, offset=XBp_OFF + (g4 * 4) * S + ch * 16,
                                   ap=[[16 * S, 128], [S, 4], [1, 16]])
                    dstp = bass.AP(tensor=gbuf, offset=g4 * 128,
                                   ap=[[256, 128], [32, 4], [1, 16]])
                    last = v.tensor_copy(dstp, srcg)
                ev.bump(last)
            ev_w1 = ev.n

            @blk.tensor
            def _(t, ch=ch, ev_w1=ev_w1):
                t.wait_ge(ev.h, ev_w1)
                for _ in range(8):
                    t.nop()
                last = None
                for s2 in range(2):
                    for _ in range(8):
                        t.nop()
                    for p in range(8):
                        lhsT = w1sb[s2 * 64:(s2 + 1) * 64, p * 64:(p + 1) * 64]
                        t.matmul(ps_z1[s2 * 64:(s2 + 1) * 64, p * 32:p * 32 + 16],
                                 lhsT,
                                 xb_t[s2 * 64:(s2 + 1) * 64, p * S + ch * 16:
                                      p * S + ch * 16 + 16],
                                 start=True, stop=True, skip_group_check=True,
                                 tile_position=(s2 * 64, s2 * 64))
                        last = t.matmul(
                            ps_zb[s2 * 64:(s2 + 1) * 64, p * 32:p * 32 + 16],
                            lhsT,
                            xc_t[s2 * 64:(s2 + 1) * 64, p * S + ch * 16:
                                 p * S + ch * 16 + 16],
                            start=True, stop=True, skip_group_check=True,
                            tile_position=(s2 * 64, s2 * 64))
                mm.bump(last)
            mm_12 = mm.n

            @blk.vector
            def _(v, ch=ch, mm_12=mm_12):
                v.wait_ge(mm.h, mm_12)
                srcz = bass.AP(tensor=ps_z1, offset=0,
                               ap=[[512, 128], [32, 8], [1, 16]])
                srca = bass.AP(tensor=xa_t, offset=ch * 16,
                               ap=[[8 * S, 128], [S, 8], [1, 16]])
                dstz = bass.AP(tensor=z1sb, offset=0,
                               ap=[[256, 128], [32, 8], [1, 16]])
                ev.bump(v.tensor_tensor(dstz, srcz, srca, ALU.subtract))
            ev_z1 = ev.n

            if SCAN_PARTS >= 3:
                pass
            if SCAN_PARTS >= 3:
                @blk.tensor
                def _(t, ch=ch):
                    last = None
                    for s2 in range(2):
                        for _ in range(8):
                            t.nop()
                        for p in range(8):
                            pb_, cb = head_coords(p, s2)
                            last = t.matmul(
                                ps_at[pb_:pb_ + 16, cb:cb + 16],
                                xb_t[s2 * 64:(s2 + 1) * 64, p * S + ch * 16:
                                     p * S + ch * 16 + 16],
                                XCp[s2 * 64:(s2 + 1) * 64, p * S + ch * 16:
                                    p * S + ch * 16 + 16],
                                start=True, stop=True, skip_group_check=True,
                                tile_position=(s2 * 64, pb_))
                    mm.bump(last)
                mm_3 = mm.n

                if SCAN_PARTS >= 3.5:
                    @blk.vector
                    def _(v, ch=ch, mm_3=mm_3):
                        v.wait_ge(mm.h, mm_3)
                        last = None
                        for pb_ in (0, 32, 64, 96):
                            last = v.tensor_tensor(at_s[pb_:pb_ + 16, 0:64],
                                                   ps_at[pb_:pb_ + 16, 0:64],
                                                   tri_b[pb_:pb_ + 16, 0:64], ALU.mult)
                        ev.bump(last)
                ev_mk = ev.n

            if SCAN_PARTS >= 2:
                @blk.tensor
                def _(t, ch=ch, ev_z1=ev_z1):
                    t.wait_ge(ev.h, ev_z1)
                    for _ in range(8):
                        t.nop()
                    t.transpose(ps_t2[:, 0:128], gbuf[:, 0:128], id_b[:, :])
                    t.transpose(ps_t2[:, 128:256], gbuf[:, 128:256], id_b[:, :])
                    t.transpose(ps_t1[:, 0:128], z1sb[:, 0:128], id_b[:, :])
                    last = t.transpose(ps_t1[:, 128:256], z1sb[:, 128:256], id_b[:, :])
                    mm.bump(last)
                mm_tr = mm.n

                @blk.vector
                def _(v, ch=ch, mm_tr=mm_tr):
                    v.wait_ge(mm.h, mm_tr)
                    v.tensor_copy(xbn[:, :], ps_t2[:, 0:256])
                    ev.bump(v.tensor_copy(z1n[:, :], ps_t1[:, 0:256]))
                ev_tr = ev.n

            if SCAN_PARTS >= 4:
                @blk.tensor
                def _(t, ch=ch, ev_tr=ev_tr, ev_mk=ev_mk):
                    t.wait_ge(ev.h, max(ev_tr, ev_mk))
                    for _ in range(8):
                        t.nop()
                    last = None
                    for po in range(4):
                        for _ in range(8):
                            t.nop()
                        for g4 in range(2):
                            p = g4 * 4 + po
                            for s2 in range(2):
                                pb_, cb = head_coords(p, s2)
                                last = t.matmul(
                                    ps_az[s2 * 64:(s2 + 1) * 64, p * 32:p * 32 + 16],
                                z1n[po * 32:po * 32 + 16,
                                    g4 * 128 + s2 * 64:g4 * 128 + (s2 + 1) * 64],
                                at_s[pb_:pb_ + 16, cb:cb + 16],
                                start=True, stop=True, skip_group_check=True,
                                tile_position=(po * 32, s2 * 64))
                    mm.bump(last)
                mm_4 = mm.n

            if SCAN_PARTS >= 5:
                @blk.tensor
                def _(t, ch=ch):
                    last = None
                    for po in range(4):
                        for _ in range(8):
                            t.nop()
                        for g4 in range(2):
                            p = g4 * 4 + po
                            for s2 in range(2):
                                last = t.matmul(
                                    ps_w1[s2 * 64:(s2 + 1) * 64, p * 64:(p + 1) * 64],
                                xbn[po * 32:po * 32 + 16,
                                    g4 * 128 + s2 * 64:g4 * 128 + (s2 + 1) * 64],
                                z1n[po * 32:po * 32 + 16,
                                    g4 * 128 + s2 * 64:g4 * 128 + (s2 + 1) * 64],
                                start=False, stop=False, skip_group_check=True,
                                tile_position=(po * 32, s2 * 64))
                    mm.bump(last)

            mm_pre_evac = mm.n

            @blk.vector
            def _(v, ch=ch, mm_pre_evac=mm_pre_evac):
                v.wait_ge(mm.h, mm_pre_evac)
                if ch >= 2:
                    v.wait_ge(st.h, 16 * (ch - 1))
                srcz = bass.AP(tensor=ps_zb, offset=0,
                               ap=[[512, 128], [32, 8], [1, 16]])
                srca2 = bass.AP(tensor=ps_az, offset=0,
                                ap=[[512, 128], [32, 8], [1, 16]])
                dstz = bass.AP(tensor=zbuf[ch % 2], offset=0,
                               ap=[[128, 128], [16, 8], [1, 16]])
                v.tensor_copy(dstz, srcz)
                if SCAN_PARTS >= 4:
                    ev.bump(v.tensor_tensor(dstz, dstz, srca2, ALU.add))
                else:
                    ev.bump(v.tensor_copy(dstz, srcz))
            @blk.sync
            def _(s, ch=ch):
                s.wait_ge(ev.h, ev.n)
                st.bump(s.dma_start(out=Zraw[:, ch * 128:(ch + 1) * 128],
                                    in_=zbuf[ch % 2][:, :]), 16)

        @blk.sync
        def _(s):
            s.wait_ge(st.h, st.n)

        sems.close()
    es.close()
    return st


def build_l1_attn(nch=128, SCAN_PARTS=5):
    nc = bass.Bass("TRN2", target_bir_lowering=False, debug=False,
                   detect_race_conditions=False)
    io = {
        "x2Tb": nc.dram_tensor("x2Tb", [D, S], F32, kind="ExternalInput"),
        "l1wqT": nc.dram_tensor("wqT", [D, 1024], F32, kind="ExternalInput"),
        "l1wkT": nc.dram_tensor("wkT", [D, 1024], F32, kind="ExternalInput"),
        "l1wvT": nc.dram_tensor("wvT", [D, 1024], F32, kind="ExternalInput"),
        "wilrT": nc.dram_tensor("wilrT", [D, 16], F32, kind="ExternalInput"),
        "bilr": nc.dram_tensor("bilr", [16, 1], F32, kind="ExternalInput"),
        "w1in": nc.dram_tensor("w1in", [128, 512], F32, kind="ExternalInput"),
        "pen": nc.dram_tensor("pen", [1, S], F32, kind="ExternalInput"),
        "ident": nc.dram_tensor("ident", [128, 128], F32, kind="ExternalInput"),
        "trimask": nc.dram_tensor("trimask", [128, 128], F32, kind="ExternalInput"),
        "Zraw": nc.dram_tensor("Zraw", [128, 128 * 128], BF16, kind="ExternalOutput"),
        "ilrD": nc.dram_tensor("ilrD", [16, S], F32),
    }
    with nc.Block() as blk:
        emit_l1_attn(nc, blk, io, nch=nch, SCAN_PARTS=SCAN_PARTS)
    return nc


def host_prep_l1_attn(inputs, x2_full):
    wqT = np.ascontiguousarray(inputs["L1_attn_Wq"].T)
    wkT = np.ascontiguousarray(inputs["L1_attn_Wk"].T)
    wvT = np.ascontiguousarray(inputs["L1_attn_Wv"].T)
    wiT = np.ascontiguousarray(inputs["L1_attn_W_ilr"].T)
    bilr = inputs["L1_attn_b_ilr"].astype(np.float32)
    W1 = inputs["L1_attn_W1"].astype(np.float32)
    pen = (1.0 / (np.arange(S) % 16 + 1.0) / 64.0).astype(np.float32)[None, :]
    ident = np.eye(128, dtype=np.float32)
    tm = np.zeros((128, 128), np.float32)
    for p in range(8):
        for s2 in range(2):
            pb_, cb = (p % 4) * 32, (p // 4) * 32 + s2 * 16
            for d_ in range(16):
                for c_ in range(16):
                    if d_ <= c_:
                        tm[pb_ + d_, cb + c_] = -1.0
    maps = []
    for c in range(NCORES):
        b, half = c // 2, c % 2
        g0 = 16 * half
        cols = slice(g0 * 64, g0 * 64 + 1024)
        w1v = np.zeros((128, 512), np.float32)
        for p in range(8):
            for s2 in range(2):
                w1v[s2 * 64:(s2 + 1) * 64, p * 64:(p + 1) * 64] = W1[g0 + 2 * p + s2]
        maps.append({
            "x2Tb": np.ascontiguousarray(x2_full[b]),
            "wqT": np.ascontiguousarray(wqT[:, cols]),
            "wkT": np.ascontiguousarray(wkT[:, cols]),
            "wvT": np.ascontiguousarray(wvT[:, cols]),
            "wilrT": np.ascontiguousarray(wiT[:, g0:g0 + 16]),
            "bilr": np.ascontiguousarray(bilr[g0:g0 + 16].reshape(16, 1)),
            "w1in": w1v,
            "pen": pen, "ident": ident, "trimask": tm,
        })
    return maps


def zraw_to_zt(zraw):
    z = np.asarray(zraw).reshape(128, 128, 8, 16)
    return np.ascontiguousarray(z.transpose(2, 0, 1, 3).reshape(1024, 2048))





from concourse.bass_core_rust import IndirectOffsetOnAxis

MAGICF = 1597000001
POLL_ITERS = 20000
GCW = 2816
NBAR = 3


def build_fused():
    nc = bass.Bass("TRN2", target_bir_lowering=False, debug=False,
                   detect_race_conditions=False)

    xTb = nc.dram_tensor("xTb", [D, S], F32, kind="ExternalInput")
    wqT = nc.dram_tensor("wqT", [D, 1024], F32, kind="ExternalInput")
    wkT = nc.dram_tensor("wkT", [D, 1024], F32, kind="ExternalInput")
    wvT = nc.dram_tensor("wvT", [D, 1024], F32, kind="ExternalInput")
    triu = nc.dram_tensor("triu", [128, 128], F32, kind="ExternalInput")
    onesc = nc.dram_tensor("onesc", [128, 128], F32, kind="ExternalInput")
    xresT = nc.dram_tensor("xresT", [D, TOK], F32, kind="ExternalInput")
    woP = nc.dram_tensor("woP", [16, 16, 128, 128], F32, kind="ExternalInput")
    invn = nc.dram_tensor("invn", [128, 1], F32, kind="ExternalInput")
    gidxF = nc.dram_tensor("gidxF", [D, GCW], I32, kind="ExternalInput")
    uidxF = nc.dram_tensor("uidxF", [D, GCW], I32, kind="ExternalInput")
    didxF = nc.dram_tensor("didxF", [GCW, D], I32, kind="ExternalInput")
    lutb = nc.dram_tensor("lutb", [128, 16], F32, kind="ExternalInput")
    h128 = nc.dram_tensor("h128", [128, 128], F32, kind="ExternalInput")
    wupR = nc.dram_tensor("wupR", [64, 16, 128, 128], F32, kind="ExternalInput")
    wgtR = nc.dram_tensor("wgtR", [64, 16, 128, 128], F32, kind="ExternalInput")
    wdnTF = nc.dram_tensor("wdnTF", [8192, 1024], F32, kind="ExternalInput")
    zwqT = nc.dram_tensor("zwqT", [D, 1024], F32, kind="ExternalInput")
    zwkT = nc.dram_tensor("zwkT", [D, 1024], F32, kind="ExternalInput")
    zwvT = nc.dram_tensor("zwvT", [D, 1024], F32, kind="ExternalInput")
    wilrT = nc.dram_tensor("wilrT", [D, 16], F32, kind="ExternalInput")
    bilr = nc.dram_tensor("bilr", [16, 1], F32, kind="ExternalInput")
    w1in = nc.dram_tensor("w1in", [128, 512], F32, kind="ExternalInput")
    pen = nc.dram_tensor("pen", [1, S], F32, kind="ExternalInput")
    ident = nc.dram_tensor("ident", [128, 128], F32, kind="ExternalInput")
    trimask = nc.dram_tensor("trimask", [128, 128], F32, kind="ExternalInput")
    l1woP = nc.dram_tensor("l1woP", [16, 16, 128, 128], F32, kind="ExternalInput")
    su_col = nc.dram_tensor("su_col", [128, 16], F32, kind="ExternalInput")
    wsu_col = nc.dram_tensor("wsu_col", [128, 64], F32, kind="ExternalInput")
    wsg_col = nc.dram_tensor("wsg_col", [128, 64], F32, kind="ExternalInput")
    sudp_col = nc.dram_tensor("sudp_col", [128, 64], F32, kind="ExternalInput")
    wsd_col = nc.dram_tensor("wsd_col", [128, 16], F32, kind="ExternalInput")
    ix_g = nc.dram_tensor("ix_g", [128, 16], U32, kind="ExternalInput")
    ix_d = nc.dram_tensor("ix_d", [128, 22], U32, kind="ExternalInput")
    ix_dn = nc.dram_tensor("ix_dn", [128, 256], U32, kind="ExternalInput")
    ix_att = nc.dram_tensor("ix_att", [128, 16], U32, kind="ExternalInput")
    gxi = nc.dram_tensor("gxi", [128, 16], U32, kind="ExternalInput")
    ix_flg = nc.dram_tensor("ix_flg", [128, NBAR], U32, kind="ExternalInput")
    flgv = nc.dram_tensor("flgv", [128, NBAR], I32, kind="ExternalInput")

    attnT = nc.dram_tensor("attnT", [1024, S], BF16)
    attnTb = nc.dram_tensor("attnTb", [D, TOK], BF16)
    x2T = nc.dram_tensor("x2T", [D, TOK], F32)
    x1T = nc.dram_tensor("x1T", [D, TOK], F32)
    rowsD = nc.dram_tensor("rowsD", [2, 512], F32)
    x2Tb = nc.dram_tensor("x2Tb", [D, S], F32)
    Zraw = nc.dram_tensor("Zraw", [128, 128 * 128], BF16)
    ilrD = nc.dram_tensor("ilrD", [16, S], F32)
    ZTb = nc.dram_tensor("ZTb", [D, TOK], BF16)
    x3T = nc.dram_tensor("x3T", [D, TOK], F32)
    zDf = nc.dram_tensor("zDf", [8192, 1024], BF16)
    dumD = nc.dram_tensor("dumD", [2, 16], F32)
    x4T = nc.dram_tensor("x4T", [D, TOK], F32, kind="ExternalOutput")

    pool_g = nc.dram_tensor("pool_g", [4096, GCW], BF16, addr_space="Shared")
    pool_u = nc.dram_tensor("pool_u", [4096, GCW], BF16, addr_space="Shared")
    pool_d = nc.dram_tensor("pool_d", [5504, 2048], BF16, addr_space="Shared")
    pool_dn = nc.dram_tensor("pool_dn", [65536, 256], BF16, addr_space="Shared")
    pool_att = nc.dram_tensor("pool_att", [4096, 1024], BF16, addr_space="Shared")
    pool_x2 = nc.dram_tensor("pool_x2", [4096, 1024], F32, addr_space="Shared")
    pool_z = nc.dram_tensor("pool_z", [4096, 1024], BF16, addr_space="Shared")
    pool_flg = nc.dram_tensor("pool_flg", [64, 1], I32, addr_space="Shared")

    def wgP_pool_ap(pool):
        def f(i):
            if i <= 21:
                off = i * 128
            else:
                off = 2048 * GCW + (i - 21) * 128
            return bass.AP(tensor=pool, offset=off,
                           ap=[[GCW, 128], [128 * GCW, 16], [1, 128]])
        return f

    def wdP_pool_ap(oc):
        return bass.AP(tensor=pool_d, offset=oc * 128,
                       ap=[[2048, 128], [128 * 2048, NIO], [1, 128]])

    def wdnP_pool_ap(oc, ioh):
        par, lc = oc // 8, oc % 8
        chunk, cb = lc // 2, (lc % 2) * 128
        off = ioh * 32 * 128 * 2048 + (par * 4 + chunk) * 256 + cb
        return bass.AP(tensor=pool_dn, offset=off,
                       ap=[[2048, 128], [128 * 2048, 32], [1, 128]])

    es0 = ExitStack()
    sbg = lambda name, shape, dt: es0.enter_context(nc.sbuf_tensor(name, shape, dt))
    ixg_sb = sbg("ixg_sb", [128, 16], U32)
    ixd_sb = sbg("ixd_sb", [128, 22], U32)
    ixdn_sb = sbg("ixdn_sb", [128, 256], U32)
    ixatt_sb = sbg("ixatt_sb", [128, 16], U32)
    gx_sb = sbg("gx_sb", [128, 16], U32)
    ixflg_sb = sbg("ixflg_sb", [128, NBAR], U32)
    flgv_sb = sbg("flgv_sb", [128, NBAR], I32)
    flg4 = sbg("flg4", [4, 1], I32)
    dum = sbg("dum", [2, 16], F32)
    dum2 = sbg("dum2", [2, 16], F32)

    with nc.Block() as blk:
        sems0 = ExitStack()
        phc = sems0.enter_context(nc.semaphore("phc"))
        phs = sems0.enter_context(nc.semaphore("phs"))
        phg = sems0.enter_context(nc.semaphore("phg"))
        gpb = Ctr(nc, sems0, "gpb")
        spb = Ctr(nc, sems0, "spb")
        stb = Ctr(nc, sems0, "stb")
        plf = [sems0.enter_context(nc.semaphore(f"plf{b}")) for b in range(NBAR)]

        fence_n = [0]

        def fence():
            fence_n[0] += 1
            k = fence_n[0]

            @blk.tensor
            def _(t, k=k):
                t.nop().then_inc(phc, 1)
                t.wait_ge(phc, 3 * k)
                t.wait_ge(phs, 16 * k)
                t.wait_ge(phg, 16 * k)

            @blk.vector
            def _(v, k=k):
                v.tensor_copy(dum2[0:1, 0:8], dum[0:1, 0:8]).then_inc(phc, 1)
                v.wait_ge(phc, 3 * k)
                v.wait_ge(phs, 16 * k)
                v.wait_ge(phg, 16 * k)

            @blk.scalar
            def _(sc, k=k):
                sc.activation(dum2[0:1, 8:16], dum[0:1, 8:16],
                              AF.Copy).then_inc(phc, 1)
                sc.wait_ge(phc, 3 * k)
                sc.wait_ge(phs, 16 * k)
                sc.wait_ge(phg, 16 * k)

            @blk.sync
            def _(s, k=k):
                s.dma_start(out=dum[0:1, 0:8], in_=dumD[0:1, 0:8]).then_inc(phs, 16)
                s.wait_ge(phc, 3 * k)
                s.wait_ge(phs, 16 * k)
                s.wait_ge(phg, 16 * k)

            @blk.gpsimd
            def _(g, k=k):
                g.dma_start(out=dum[1:2, 0:8], in_=dumD[1:2, 0:8],
                            allow_slow_non_contiguous=True).then_inc(phg, 16)
                g.wait_ge(phc, 3 * k)
                g.wait_ge(phs, 16 * k)
                g.wait_ge(phg, 16 * k)

        @blk.gpsimd
        def _(g):
            for dst, srcd in ((ixg_sb, ix_g), (ixd_sb, ix_d),
                              (ixdn_sb, ix_dn), (ixatt_sb, ix_att), (gx_sb, gxi),
                              (ixflg_sb, ix_flg), (flgv_sb, flgv)):
                gpb.bump(g.dma_start(out=dst[:, :], in_=srcd[:, :]), 16)
        IX_LD = gpb.n

        esW = ExitStack()
        sbW = lambda name, shape, dt: esW.enter_context(nc.sbuf_tensor(name, shape, dt))
        psW = lambda name, shape, dt: esW.enter_context(nc.psum_tensor(name, shape, dt))
        idxb = [sbW(f"Widxb{i}", [128, GCW], BF16) for i in range(2)]
        accb = [sbW(f"Waccb{i}", [128, GCW], BF16) for i in range(2)]
        maskb = sbW("Wmaskb", [128, GCW], BF16)
        lut_sb = sbW("Wlut_sb", [128, 16], F32)
        h128b = sbW("Wh128b", [128, 128], BF16)
        z_sb = sbW("Wz_sb", [128, 16 * 1024], BF16)
        z2_sb = sbW("Wz2_sb", [128, 64 * 256], BF16)
        ps_h = [psW(f"Wps_h{i}", [128, 512], F32) for i in range(2)]

        semsW = ExitStack()
        gld = Ctr(nc, semsW, "Wgld")
        mm = Ctr(nc, semsW, "Wmm")
        ev = Ctr(nc, semsW, "Wev")
        gsc = Ctr(nc, semsW, "Wgsc")
        bd = Ctr(nc, semsW, "Wbd")

        @blk.gpsimd
        def _(g):
            g.wait_ge(gpb.h, IX_LD)
            gld.bump(g.dma_start(out=lut_sb[:, :], in_=lutb[:, :]), 16)
            gld.bump(g.dma_start(out=h128b[:, :], in_=h128[:, :]), 16)

        ghost_jobs = []
        for (idx_dr, pool, ixsb, ntile, cols) in (
                (gidxF, pool_g, ixg_sb, 16, GCW),
                (uidxF, pool_u, ixg_sb, 16, GCW),
                (didxF, pool_d, ixd_sb, 22, 2048)):
            for kt in range(ntile):
                ghost_jobs.append((idx_dr, pool, ixsb, kt, cols))

        for ji, (idx_dr, pool, ixsb, kt, cols) in enumerate(ghost_jobs):
            @blk.gpsimd
            def _(g, ji=ji, idx_dr=idx_dr, kt=kt, cols=cols):
                if ji >= 2:
                    g.wait_ge(ev.h, ji - 1)
                gld.bump(g.dma_start(
                    out=idxb[ji % 2][:, 0:cols],
                    in_=idx_dr[kt * 128:(kt + 1) * 128, :]), 16)

            @blk.vector
            def _(v, ji=ji, cols=cols):
                v.wait_ge(gld.h, 32 + 16 * (ji + 1))
                if ji >= 2:
                    v.wait_ge(gsc.h, 16 * (ji - 1))
                ib = idxb[ji % 2][:, 0:cols]
                ab = accb[ji % 2][:, 0:cols]
                mk = maskb[:, 0:cols]
                v.tensor_scalar(ab, ib, 0.0, lut_sb[:, 0:1], ALU.is_equal, ALU.mult)
                last = None
                for j in range(1, 16):
                    v.tensor_scalar(mk, ib, float(j), None, ALU.is_equal)
                    last = v.scalar_tensor_tensor(ab, mk, lut_sb[:, j:j + 1], ab,
                                                  ALU.mult, ALU.add)
                ev.bump(last)

            @blk.gpsimd
            def _(g, ji=ji, pool=pool, ixsb=ixsb, kt=kt, cols=cols):
                g.wait_ge(ev.h, ji + 1)
                gsc.bump(g.indirect_dma_start(
                    pool[:, :],
                    IndirectOffsetOnAxis(ap=ixsb[:, kt:kt + 1], axis=0),
                    accb[ji % 2][:, 0:cols], None), 16)

        def emit_wprime_f(w_in, c0, cols, ntile, nbf_stages, pool, ixsb, ixc0):
            for kt in range(ntile):
                @blk.gpsimd
                def _(g, kt=kt, cols=cols, w_in=w_in, c0=c0, evn=ev.n):
                    g.wait_ge(ev.h, evn)
                    g.wait_ge(mm.h, mm.n)
                    gld.bump(g.dma_start(
                        out=idxb[kt % 2][:, 0:cols],
                        in_=w_in[kt * 128:(kt + 1) * 128, c0:c0 + cols]), 16)

                @blk.tensor
                def _(t, kt=kt, cols=cols):
                    t.wait_ge(gld.h, gld.n)
                    t.wait_ge(ev.h, ev.n)
                    nch = (cols + 511) // 512
                    last = None
                    for n in range(nch):
                        lo, hi = n * 512, min((n + 1) * 512, cols)
                        last = t.matmul(ps_h[n % 2][:, 0:hi - lo], h128b[:, :],
                                        idxb[kt % 2][:, lo:hi], start=True, stop=True)
                    mm.bump(last)

                @blk.vector
                def _(v, kt=kt, cols=cols, gscn=gsc.n):
                    if kt == 0:
                        v.wait_ge(gsc.h, gscn)
                    v.wait_ge(mm.h, mm.n)
                    nch = (cols + 511) // 512
                    lastv = None
                    for n in range(nch):
                        lo, hi = n * 512, min((n + 1) * 512, cols)
                        lastv = v.tensor_copy(z_sb[:, lo:hi], ps_h[n % 2][:, 0:hi - lo])
                    ev.bump(lastv)

                @blk.sync
                def _(s, kt=kt, cols=cols, bdn=bd.n):
                    if kt == 0:
                        s.wait_ge(bd.h, bdn)
                    s.wait_ge(ev.h, ev.n)
                    bd.bump(s.dma_start(out=zDf[kt * 128:(kt + 1) * 128, 0:cols],
                                        in_=z_sb[:, 0:cols]), 16)

            @blk.sync
            def _(s, ntile=ntile, cols=cols, gscn=gsc.n):
                s.wait_ge(gsc.h, gscn)
                s.wait_ge(bd.h, bd.n)
                i = s.dma_start(
                    out=z2_sb[:, 0:ntile * cols],
                    in_=zDf[0:ntile * 128, 0:cols].rearrange("(j p) o -> p j o", p=128))
                bd.bump(i, 16)

            @blk.vector
            def _(v, ntile=ntile, cols=cols, nbf=nbf_stages):
                v.wait_ge(bd.h, bd.n)
                hstep = 1
                last = None
                for s_i in range(nbf):
                    ngrp = ntile // (2 * hstep)
                    apA = bass.AP(tensor=z2_sb, offset=0,
                                  ap=[[z2_sb.shape[1], 128],
                                      [2 * hstep * cols, ngrp], [1, hstep * cols]])
                    apB = bass.AP(tensor=z2_sb, offset=hstep * cols,
                                  ap=[[z2_sb.shape[1], 128],
                                      [2 * hstep * cols, ngrp], [1, hstep * cols]])
                    apA2 = bass.AP(tensor=z_sb, offset=0,
                                   ap=[[z_sb.shape[1], 128],
                                       [2 * hstep * cols, ngrp], [1, hstep * cols]])
                    apB2 = bass.AP(tensor=z_sb, offset=hstep * cols,
                                   ap=[[z_sb.shape[1], 128],
                                       [2 * hstep * cols, ngrp], [1, hstep * cols]])
                    srcA, srcB = (apA, apB) if s_i % 2 == 0 else (apA2, apB2)
                    dstA, dstB = (apA2, apB2) if s_i % 2 == 0 else (apA, apB)
                    v.tensor_tensor(dstA, srcA, srcB, ALU.add)
                    last = v.tensor_tensor(dstB, srcA, srcB, ALU.subtract)
                    hstep *= 2
                ev.bump(last)

            final_buf = z2_sb if (nbf_stages % 2 == 0) else z_sb

            @blk.gpsimd
            def _(g, ntile=ntile, cols=cols, pool=pool, ixsb=ixsb, ixc0=ixc0,
                  final_buf=final_buf):
                g.wait_ge(ev.h, ev.n)
                for j in range(ntile):
                    gsc.bump(g.indirect_dma_start(
                        pool[:, :],
                        IndirectOffsetOnAxis(ap=ixsb[:, ixc0 + j:ixc0 + j + 1], axis=0),
                        final_buf[:, j * cols:(j + 1) * cols], None), 16)

        for ch in range(4):
            emit_wprime_f(wdnTF, ch * 256, 256, 64, 6, pool_dn, ixdn_sb, ch * 64)

        @blk.gpsimd
        def _(g):
            g.wait_ge(gsc.h, gsc.n)

        semsW.close()
        esW.close()
        fence()

        ioA = {"xTb": xTb, "wqT": wqT, "wkT": wkT, "wvT": wvT,
               "triu": triu, "onesc": onesc, "attnT": attnT}
        emit_l0_attn(nc, blk, ioA)
        fence()

        esG1 = ExitStack()
        astg = [esG1.enter_context(nc.sbuf_tensor(f"G1a{i}", [128, 2048], BF16))
                for i in range(2)]
        gst = [esG1.enter_context(nc.sbuf_tensor(f"G1g{i}", [128, 1024], BF16))
               for i in range(2)]

        s0, g0 = spb.n, gpb.n

        @blk.sync
        def _(s):
            for t in range(8):
                if t >= 2:
                    s.wait_ge(gpb.h, g0 + 32 * (t - 1))
                spb.bump(s.dma_start(out=astg[t % 2][:, :],
                                     in_=attnT[t * 128:(t + 1) * 128, :]), 16)

        @blk.gpsimd
        def _(g):
            for t in range(8):
                g.wait_ge(spb.h, s0 + 16 * (t + 1))
                for th2 in range(2):
                    gpb.bump(g.indirect_dma_start(
                        pool_att[:, :],
                        IndirectOffsetOnAxis(
                            ap=ixatt_sb[:, 2 * t + th2:2 * t + th2 + 1], axis=0),
                        astg[t % 2][:, th2 * 1024:(th2 + 1) * 1024], None), 16)
            g.wait_ge(gpb.h, gpb.n)
            gpb.bump(g.indirect_dma_start(
                pool_flg[:, :],
                IndirectOffsetOnAxis(ap=ixflg_sb[0:2, 0:1], axis=0),
                flgv_sb[0:2, 0:1], None), 16)
            g.wait_ge(gpb.h, gpb.n)

        emit_poll(nc, blk, plf[0], pool_flg, flg4, 0)

        gi0, st0 = gpb.n, stb.n

        @blk.gpsimd
        def _(g):
            for k in range(16):
                if k >= 2:
                    g.wait_ge(stb.h, st0 + 16 * (k - 1))
                gpb.bump(g.indirect_dma_start(
                    gst[k % 2][:, :], None,
                    pool_att[:, :],
                    IndirectOffsetOnAxis(ap=gx_sb[:, k:k + 1], axis=0)), 16)

        @blk.scalar
        def _(sc):
            for k in range(16):
                sc.wait_ge(gpb.h, gi0 + 16 * (k + 1))
                stb.bump(sc.dma_start(out=attnTb[k * 128:(k + 1) * 128, :],
                                      in_=gst[k % 2][:, :]), 16)
            sc.wait_ge(stb.h, stb.n)

        esG1.close()
        fence()

        ioF = {"attnTb": attnTb, "xresT": xresT, "invn": invn,
               "x2T": x2T, "x1T": x1T, "rowsD": rowsD,
               "woP_ap": lambda i: woP[i].rearrange("k p c -> p k c"),
               "wgP_ap": wgP_pool_ap(pool_g),
               "wuP_ap": wgP_pool_ap(pool_u),
               "wdP_ap": wdP_pool_ap}
        emit_l0_ffn(nc, blk, ioF)
        fence()

        esG2 = ExitStack()
        xstg = [esG2.enter_context(nc.sbuf_tensor(f"G2a{i}", [128, 1024], F32))
                for i in range(2)]
        x2g = [esG2.enter_context(nc.sbuf_tensor(f"G2b{i}", [128, 2048], F32))
               for i in range(2)]

        s0, g0 = spb.n, gpb.n

        @blk.sync
        def _(s):
            for k in range(16):
                if k >= 2:
                    s.wait_ge(gpb.h, g0 + 16 * (k - 1))
                spb.bump(s.dma_start(out=xstg[k % 2][:, :],
                                     in_=x2T[k * 128:(k + 1) * 128, :]), 16)

        @blk.gpsimd
        def _(g):
            for k in range(16):
                g.wait_ge(spb.h, s0 + 16 * (k + 1))
                gpb.bump(g.indirect_dma_start(
                    pool_x2[:, :],
                    IndirectOffsetOnAxis(ap=gx_sb[:, k:k + 1], axis=0),
                    xstg[k % 2][:, :], None), 16)
            g.wait_ge(gpb.h, gpb.n)
            gpb.bump(g.indirect_dma_start(
                pool_flg[:, :],
                IndirectOffsetOnAxis(ap=ixflg_sb[0:2, 1:2], axis=0),
                flgv_sb[0:2, 1:2], None), 16)
            g.wait_ge(gpb.h, gpb.n)

        emit_poll(nc, blk, plf[1], pool_flg, flg4, 1)

        @blk.gpsimd
        def _(g):
            gpb.bump(g.dma_start(out=flg4[:, :], in_=pool_flg[8:12, 0:1],
                                 allow_slow_non_contiguous=True), 16)
        PD1 = gpb.n

        si0, st0 = spb.n, stb.n

        @blk.sync
        def _(s):
            s.wait_ge(gpb.h, PD1)
            for k in range(16):
                if k >= 2:
                    s.wait_ge(stb.h, st0 + 16 * (k - 1))
                for t2 in range(2):
                    srca = bass.AP(tensor=pool_x2,
                                   offset=(t2 * 2048 + k * 128) * 1024,
                                   ap=[[1024, 128], [1, 1024]])
                    spb.bump(s.dma_start(
                        out=x2g[k % 2][:, t2 * 1024:(t2 + 1) * 1024], in_=srca), 16)

        @blk.scalar
        def _(sc):
            for k in range(16):
                sc.wait_ge(spb.h, si0 + 32 * (k + 1))
                stb.bump(sc.dma_start(out=x2Tb[k * 128:(k + 1) * 128, :],
                                      in_=x2g[k % 2][:, :]), 16)
            sc.wait_ge(stb.h, stb.n)

        esG2.close()
        fence()

        ioS = {"x2Tb": x2Tb, "l1wqT": zwqT, "l1wkT": zwkT, "l1wvT": zwvT,
               "wilrT": wilrT, "bilr": bilr, "w1in": w1in, "pen": pen,
               "ident": ident, "trimask": trimask, "Zraw": Zraw, "ilrD": ilrD}
        emit_l1_attn(nc, blk, ioS)
        fence()

        esG3 = ExitStack()
        zstg = [esG3.enter_context(nc.sbuf_tensor(f"G3a{i}", [128, 2048], BF16))
                for i in range(2)]
        zgt = [esG3.enter_context(nc.sbuf_tensor(f"G3g{i}", [128, 1024], BF16))
               for i in range(2)]

        s0, g0 = spb.n, gpb.n

        @blk.sync
        def _(s):
            for p in range(8):
                if p >= 2:
                    s.wait_ge(gpb.h, g0 + 32 * (p - 1))
                for chh in range(2):
                    srca = bass.AP(tensor=Zraw, offset=chh * 64 * 128 + p * 16,
                                   ap=[[16384, 128], [128, 64], [1, 16]])
                    spb.bump(s.dma_start(
                        out=zstg[p % 2][:, chh * 1024:(chh + 1) * 1024],
                        in_=srca), 16)

        @blk.gpsimd
        def _(g):
            for p in range(8):
                g.wait_ge(spb.h, s0 + 32 * (p + 1))
                for th2 in range(2):
                    gpb.bump(g.indirect_dma_start(
                        pool_z[:, :],
                        IndirectOffsetOnAxis(
                            ap=ixatt_sb[:, 2 * p + th2:2 * p + th2 + 1], axis=0),
                        zstg[p % 2][:, th2 * 1024:(th2 + 1) * 1024], None), 16)
            g.wait_ge(gpb.h, gpb.n)
            gpb.bump(g.indirect_dma_start(
                pool_flg[:, :],
                IndirectOffsetOnAxis(ap=ixflg_sb[0:2, 2:3], axis=0),
                flgv_sb[0:2, 2:3], None), 16)
            g.wait_ge(gpb.h, gpb.n)

        emit_poll(nc, blk, plf[2], pool_flg, flg4, 2)

        gi0, st0 = gpb.n, stb.n

        @blk.gpsimd
        def _(g):
            for k in range(16):
                if k >= 2:
                    g.wait_ge(stb.h, st0 + 16 * (k - 1))
                gpb.bump(g.indirect_dma_start(
                    zgt[k % 2][:, :], None,
                    pool_z[:, :],
                    IndirectOffsetOnAxis(ap=gx_sb[:, k:k + 1], axis=0)), 16)

        @blk.scalar
        def _(sc):
            for k in range(16):
                sc.wait_ge(gpb.h, gi0 + 16 * (k + 1))
                stb.bump(sc.dma_start(out=ZTb[k * 128:(k + 1) * 128, :],
                                      in_=zgt[k % 2][:, :]), 16)
            sc.wait_ge(stb.h, stb.n)

        esG3.close()
        fence()

        ioL = {"ZTb": ZTb, "xresT": x2T, "invn": invn,
               "su_col": su_col, "wsu_col": wsu_col, "wsg_col": wsg_col,
               "sudp_col": sudp_col, "wsd_col": wsd_col,
               "x4T": x4T, "x3T": x3T, "rowsD": rowsD, "h128": h128,
               "l1woP_ap": lambda i: l1woP[i].rearrange("k p c -> p k c"),
               "wupP_ap": lambda i: wupR[i].rearrange("k p c -> p k c"),
               "wgtP_ap": lambda i: wgtR[i].rearrange("k p c -> p k c"),
               "wdnP_ap": wdnP_pool_ap}
        emit_l1_ffn(nc, blk, ioL)

        sems0.close()
    es0.close()
    return nc


def emit_poll(nc, blk, sem, pool_flg, flg4, bar):
    @blk.gpsimd
    def _(g):
        want = MAGICF + bar
        cnt = nc.alloc_register(mybir.EngineType.Pool, f"pl{bar}_cnt")
        iters = nc.alloc_register(mybir.EngineType.Pool, f"pl{bar}_it")
        wm = nc.alloc_register(mybir.EngineType.Pool, f"pl{bar}_wm")
        t1 = nc.alloc_register(mybir.EngineType.Pool, f"pl{bar}_t1")
        t2 = nc.alloc_register(mybir.EngineType.Pool, f"pl{bar}_t2")
        c = nc.alloc_register(mybir.EngineType.Pool, f"pl{bar}_c")
        g.reg_mov(cnt, 0)
        g.reg_mov(iters, 0)
        g.reg_mov(wm, 0)
        g.reg_mov(c, 1)
        with g.While(c):
            g.reg_add(iters, iters, 1)
            g.reg_add(wm, wm, 16)
            g.dma_start(out=flg4[:, :], in_=pool_flg[bar * 8:bar * 8 + 4, 0:1],
                        allow_slow_non_contiguous=True).then_inc(sem, 16)
            g.wait_ge(sem, wm)
            g.reg_mov(cnt, 0)
            for r in range(4):
                g.load(t1, flg4[r:r + 1, 0:1])
                g.reg_alu(t1, t1, want, ALU.is_equal)
                g.reg_alu(cnt, cnt, t1, ALU.add)
            g.reg_alu(t1, cnt, 4, ALU.is_lt)
            g.reg_alu(t2, iters, POLL_ITERS, ALU.is_lt)
            g.reg_alu(c, t1, t2, ALU.bitwise_and)




def host_prep_fused(inputs):
    mA = host_prep_l0_attn(inputs)
    x = inputs["L0_attn_hidden_states"]
    woT = np.ascontiguousarray(inputs["L0_attn_w_o"].T)
    woP = np.ascontiguousarray(woT.reshape(16, 128, 16, 128).transpose(2, 0, 1, 3))
    invn = np.full((128, 1), 1.0 / 2048.0, np.float32)
    gT = np.ascontiguousarray(inputs["L0_ffn_gate_idx"].T)
    uT = np.ascontiguousarray(inputs["L0_ffn_up_idx"].T)
    dTc = np.ascontiguousarray(inputs["L0_ffn_down_idx"].T)
    upT = np.ascontiguousarray(inputs["L1_ffn_W_up"].T)
    gtT = np.ascontiguousarray(inputs["L1_ffn_W_gate"].T)
    dnT = np.ascontiguousarray(inputs["L1_ffn_W_down"].T)
    lut = np.ascontiguousarray(
        np.broadcast_to(inputs["L0_ffn_lut"][None, :], (128, 16))).astype(np.float32)
    H = np.array([[1.0]], np.float32)
    while H.shape[0] < 128:
        H = np.block([[H, H], [H, -H]])
    H = np.ascontiguousarray(H)
    zwqT = np.ascontiguousarray(inputs["L1_attn_Wq"].T)
    zwkT = np.ascontiguousarray(inputs["L1_attn_Wk"].T)
    zwvT = np.ascontiguousarray(inputs["L1_attn_Wv"].T)
    wiT = np.ascontiguousarray(inputs["L1_attn_W_ilr"].T)
    bilr_np = inputs["L1_attn_b_ilr"].astype(np.float32)
    W1 = inputs["L1_attn_W1"].astype(np.float32)
    pen_np = (1.0 / (np.arange(S) % 16 + 1.0) / 64.0).astype(np.float32)[None, :]
    ident = np.eye(128, dtype=np.float32)
    tm = np.zeros((128, 128), np.float32)
    for p in range(8):
        for s2 in range(2):
            pb_, cb = (p % 4) * 32, (p // 4) * 32 + s2 * 16
            for d_ in range(16):
                for c_ in range(16):
                    if d_ <= c_:
                        tm[pb_ + d_, cb + c_] = -1.0
    l1woT = np.ascontiguousarray(inputs["L1_attn_Wo"].T)
    l1woP = np.ascontiguousarray(
        l1woT.reshape(16, 128, 16, 128).transpose(2, 0, 1, 3))
    su_col = np.ascontiguousarray(
        inputs["L1_ffn_SU_ug"].reshape(16, 128).T).astype(np.float32)
    wsu_col = np.ascontiguousarray(
        inputs["L1_ffn_Wscale_ug"][:8192].reshape(64, 128).T).astype(np.float32)
    wsg_col = np.ascontiguousarray(
        inputs["L1_ffn_Wscale_ug"][8192:].reshape(64, 128).T).astype(np.float32)
    sudp_col = np.ascontiguousarray(
        inputs["L1_ffn_SU_dp"].reshape(64, 128).T).astype(np.float32)
    wsd_col = np.ascontiguousarray(
        inputs["L1_ffn_Wscale_dp"].reshape(16, 128).T).astype(np.float32)

    wupR_np = np.ascontiguousarray(
        upT.reshape(16, 128, 64, 128).transpose(2, 0, 1, 3))
    wgtR_np = np.ascontiguousarray(
        gtT.reshape(16, 128, 64, 128).transpose(2, 0, 1, 3))
    pv = np.arange(128, dtype=np.uint32)
    maps = []
    for c in range(NCORES):
        b, par = c // 2, c % 2
        tc = slice(par * 1024, par * 1024 + 1024)
        gsl = slice(0, GCW) if par == 0 else slice(5504 - GCW, 5504)
        g0 = 16 * par
        zcols = slice(g0 * 64, g0 * 64 + 1024)
        w1v = np.zeros((128, 512), np.float32)
        for p in range(8):
            for s2 in range(2):
                w1v[s2 * 64:(s2 + 1) * 64, p * 64:(p + 1) * 64] = W1[g0 + 2 * p + s2]
        ixg = (par * 2048 + np.arange(16, dtype=np.uint32)[None, :] * 128
               + pv[:, None]).astype(np.uint32)
        ixd = (par * (5504 - GCW) + np.arange(22, dtype=np.uint32)[None, :] * 128
               + pv[:, None]).astype(np.uint32)
        ixdn = np.zeros((128, 256), np.uint32)
        for ch in range(4):
            for j in range(64):
                ixdn[:, ch * 64 + j] = (j * 128 + pv) * 8 + par * 4 + ch
        ixatt = np.zeros((128, 16), np.uint32)
        for t in range(8):
            for th2 in range(2):
                ixatt[:, 2 * t + th2] = th2 * 2048 + par * 1024 + t * 128 + pv
        gx = (par * 2048 + np.arange(16, dtype=np.uint32)[None, :] * 128
              + pv[:, None]).astype(np.uint32)
        ixflg = np.zeros((128, NBAR), np.uint32)
        for bb in range(NBAR):
            ixflg[0:2, bb] = bb * 8 + par * 2 + np.arange(2)
        flgv_np = np.zeros((128, NBAR), np.int32)
        for bb in range(NBAR):
            flgv_np[:, bb] = MAGICF + bb

        m = dict(mA[c])
        m.update({
            "xresT": np.ascontiguousarray(x[b].T[:, tc]),
            "woP": woP, "invn": invn,
            "gidxF": np.ascontiguousarray(gT[:, gsl]),
            "uidxF": np.ascontiguousarray(uT[:, gsl]),
            "didxF": np.ascontiguousarray(dTc[gsl, :]),
            "lutb": lut, "h128": H,
            "wupR": wupR_np, "wgtR": wgtR_np,
            "wdnTF": np.ascontiguousarray(dnT[:, par * 1024:(par + 1) * 1024]),
            "zwqT": np.ascontiguousarray(zwqT[:, zcols]),
            "zwkT": np.ascontiguousarray(zwkT[:, zcols]),
            "zwvT": np.ascontiguousarray(zwvT[:, zcols]),
            "wilrT": np.ascontiguousarray(wiT[:, g0:g0 + 16]),
            "bilr": np.ascontiguousarray(bilr_np[g0:g0 + 16].reshape(16, 1)),
            "w1in": w1v, "pen": pen_np, "ident": ident, "trimask": tm,
            "l1woP": l1woP,
            "su_col": su_col, "wsu_col": wsu_col, "wsg_col": wsg_col,
            "sudp_col": sudp_col, "wsd_col": wsd_col,
            "ix_g": ixg, "ix_d": ixd, "ix_dn": ixdn,
            "ix_att": ixatt, "gxi": gx, "ix_flg": ixflg, "flgv": flgv_np,
        })
        maps.append(m)
    return maps



_NC_CACHE = {}


def _get_nc(name, builder):
    if name not in _NC_CACHE:
        _NC_CACHE[name] = builder()
    return _NC_CACHE[name]


def kernel_unfused(**inputs):
    inputs = {k: np.asarray(v) for k, v in inputs.items()}
    cores = list(range(NCORES))

    nc = _get_nc("wprep", build_wprep)
    res = run_bass_kernel_spmd(nc, host_prep_wprep(inputs), core_ids=cores)
    WgT = np.concatenate([np.asarray(r["wg_o"]) for r in res.results], axis=1)
    WuT = np.concatenate([np.asarray(r["wu_o"]) for r in res.results], axis=1)
    WdT = np.concatenate([np.asarray(r["wd_o"]) for r in res.results], axis=1)
    upP = np.concatenate([np.asarray(r["wupP"]) for r in res.results], axis=1)
    gtP = np.concatenate([np.asarray(r["wgtP"]) for r in res.results], axis=1)
    dnP = np.concatenate([np.asarray(r["wdnP"]) for r in res.results], axis=1)

    nc = _get_nc("l0a", build_l0_attn)
    res = run_bass_kernel_spmd(nc, host_prep_l0_attn(inputs), core_ids=cores)
    attnT_full = [np.concatenate([np.asarray(res.results[2 * b]["attnT"]),
                                  np.asarray(res.results[2 * b + 1]["attnT"])], axis=0)
                  for b in range(4)]

    nc = _get_nc("l0f", build_l0_ffn)
    res = run_bass_kernel_spmd(
        nc, host_prep_l0_ffn(inputs, attnT_full, WgT, WuT, WdT), core_ids=cores)
    x2_full = [np.concatenate([np.asarray(res.results[2 * b]["x2T"]),
                               np.asarray(res.results[2 * b + 1]["x2T"])], axis=1)
               for b in range(4)]

    nc = _get_nc("l1a", build_l1_attn)
    res = run_bass_kernel_spmd(nc, host_prep_l1_attn(inputs, x2_full), core_ids=cores)
    ZT_full = [np.concatenate([zraw_to_zt(np.asarray(res.results[2 * b]["Zraw"])),
                               zraw_to_zt(np.asarray(res.results[2 * b + 1]["Zraw"]))],
                              axis=0)
               for b in range(4)]

    nc = _get_nc("l1f", build_l1_ffn)
    maps = host_prep_l1_ffn(inputs, ZT_full, x2_full)
    up, gt, dn = panelize_l1_w(upP, gtP, dnP)
    for m in maps:
        m["wupP"] = up
        m["wgtP"] = gt
        m["wdnP"] = dn
    res = run_bass_kernel_spmd(nc, maps, core_ids=cores)

    out = np.empty((4, S, D), np.float32)
    for b in range(4):
        x4 = np.concatenate([np.asarray(res.results[2 * b]["x4T"]),
                             np.asarray(res.results[2 * b + 1]["x4T"])], axis=1)
        out[b] = x4.T
    return out


def kernel(**inputs):
    inputs = {k: np.asarray(v) for k, v in inputs.items()}
    nc = _get_nc("fused", build_fused)
    maps = host_prep_fused(inputs)
    res = run_bass_kernel_spmd(nc, maps, core_ids=list(range(NCORES)))
    out = np.empty((4, S, D), np.float32)
    for b in range(4):
        x4 = np.concatenate([np.asarray(res.results[2 * b]["x4T"]),
                             np.asarray(res.results[2 * b + 1]["x4T"])], axis=1)
        out[b] = x4.T
    return out


# version 04
# speedup vs baseline: 1.1749x; 1.0138x over previous
import numpy as np
import ml_dtypes
from contextlib import ExitStack

import concourse.bass as bass
import concourse.mybir as mybir
from concourse.bass_utils import run_bass_kernel_spmd

F32 = mybir.dt.float32
BF16 = mybir.dt.bfloat16
I32 = mybir.dt.int32
U32 = mybir.dt.uint32
AF = mybir.ActivationFunctionType
ALU = mybir.AluOpType

B, S, D = 4, 2048, 2048
NCORES = 8
TOK = 1024
RSQRT_HD = float(1.0 / np.sqrt(128.0))


class Ctr:

    def __init__(self, nc, es, name):
        self.h = es.enter_context(nc.semaphore(name))
        self.n = 0

    def bump(self, inst, k=1):
        inst.then_inc(self.h, k)
        self.n += k
        return self.n


def bfcast(x):
    return np.ascontiguousarray(x).astype(ml_dtypes.bfloat16)



def emit_l0_attn(nc, blk_outer, io, pfx="a"):
    xTb, wqT, wkT, wvT = io["xTb"], io["wqT"], io["wkT"], io["wvT"]
    triu, onesc, attnT = io["triu"], io["onesc"], io["attnT"]
    dbg_q = dbg_k = dbg_vn = dbg_rc = None

    es = ExitStack()
    sb = lambda name, shape, dt: es.enter_context(nc.sbuf_tensor(pfx + name, shape, dt))
    ps = lambda name, shape, dt: es.enter_context(nc.psum_tensor(pfx + name, shape, dt))

    xT = sb("xT", [128, 16 * S], BF16)
    wq = sb("wq", [128, 16 * 1024], BF16)
    wk = sb("wk", [128, 16 * 1024], BF16)
    wv = sb("wv", [128, 16 * 1024], BF16)
    triu_b = sb("triu_b", [128, 128], BF16)
    ones_b = sb("ones_b", [128, 128], BF16)
    qT = sb("qT", [128, S], BF16)
    kT = sb("kT", [128, S], BF16)
    vT = sb("vT", [128, S], BF16)
    vn = sb("vn", [128, S], BF16)
    exps = [sb(f"exps{i}", [128, 1024], BF16) for i in range(2)]
    recip = sb("recip", [128, 1024], F32)
    att_o = sb("att_o", [128, S], BF16)

    ps_a = ps("ps_a", [128, 1024], F32)
    ps_b = ps("ps_b", [128, 1024], F32)
    ps_den = ps("ps_den", [128, 1024], F32)
    ps_att = ps("ps_att", [128, 1024], F32)

    if True:
        blk = blk_outer
        sems = ExitStack()
        g_ld = Ctr(nc, sems, pfx + "g_ld")
        mm = Ctr(nc, sems, pfx + "mm")
        ev = Ctr(nc, sems, pfx + "ev")
        ex = Ctr(nc, sems, pfx + "ex")
        mk = Ctr(nc, sems, pfx + "mk")
        tr = Ctr(nc, sems, pfx + "tr")
        st = Ctr(nc, sems, pfx + "st")

        @blk.gpsimd
        def _(g):
            for k in range(16):
                i = g.dma_start(out=xT[:, k * S:(k + 1) * S],
                                in_=xTb[k * 128:(k + 1) * 128, :])
                g_ld.bump(i, 16)
            for w_sb, w_dr in ((wq, wqT), (wk, wkT), (wv, wvT)):
                for k in range(16):
                    i = g.dma_start(out=w_sb[:, k * 1024:(k + 1) * 1024],
                                    in_=w_dr[k * 128:(k + 1) * 128, :])
                    g_ld.bump(i, 16)
            g_ld.bump(g.dma_start(out=triu_b[:, :], in_=triu[:, :]), 16)
            g_ld.bump(g.dma_start(out=ones_b[:, :], in_=onesc[:, :]), 16)

        LD_ALL = g_ld.n

        for h in range(8):
            ev_base = ev.n

            @blk.tensor
            def _(t, h=h, ev_base=ev_base):
                if h == 0:
                    t.wait_ge(g_ld.h, LD_ALL)
                for pi, w_sb in enumerate((wq, wk, wv)):
                    if pi > 0:
                        t.wait_ge(ev.h, ev_base + pi)
                    elif h > 0:
                        t.wait_ge(ev.h, ev_base)
                    last = None
                    for k in range(16):
                        lhsT = w_sb[:, k * 1024 + h * 128: k * 1024 + (h + 1) * 128]
                        for n in range(4):
                            dst = (ps_a if n < 2 else ps_b)[:, (n % 2) * 512:(n % 2 + 1) * 512]
                            last = t.matmul(dst, lhsT,
                                            xT[:, k * S + n * 512: k * S + (n + 1) * 512],
                                            start=(k == 0), stop=(k == 15))
                    mm.bump(last)

            mm_pbase = mm.n - 3

            @blk.vector
            def _(v, h=h, mm_pbase=mm_pbase):
                for i, dst in enumerate((qT, kT, vT)):
                    v.wait_ge(mm.h, mm_pbase + i + 1)
                    if dst is vT:
                        v.wait_ge(tr.h, tr.n)
                    if i == 0 and h > 0:
                        v.wait_ge(st.h, st.n)
                    v.tensor_copy(dst[:, 0:1024], ps_a[:, :])
                    ev.bump(v.tensor_copy(dst[:, 1024:2048], ps_b[:, :]))

            @blk.sync
            def _(s):
                s.wait_ge(ev.h, ev.n)
                for kt in range(16):
                    i = s.dma_start_transpose(vn[:, kt * 128:(kt + 1) * 128],
                                              vT[:, kt * 128:(kt + 1) * 128])
                    tr.bump(i, 16)

            for qh in range(2):
                kts = list(range(0, 8 * (qh + 1)))
                K = len(kts)
                q0 = qh * 1024
                mm_base = mm.n
                ex_base = ex.n
                mk_base = mk.n
                ev_evac = ev.n
                diag = {kt: kt * 128 - q0 for kt in kts if q0 <= kt * 128 < q0 + 1024}

                def emit_scores(t, i, kt):
                    pss = (ps_a, ps_b)[i % 2]
                    co = max(0, kt * 128 - q0)
                    last = None
                    for n in range(2):
                        lo, hi = max(n * 512, co), (n + 1) * 512
                        if lo >= hi:
                            continue
                        last = t.matmul(pss[:, lo:hi],
                                        kT[:, kt * 128:(kt + 1) * 128],
                                        qT[:, q0 + lo:q0 + hi],
                                        start=True, stop=True)
                    mm.bump(last)

                @blk.tensor
                def _(t, kts=kts, K=K, q0=q0, mm_base=mm_base, ex_base=ex_base,
                      mk_base=mk_base, ev_evac=ev_evac, diag=diag, qh=qh, h=h):
                    if qh == 0 and h == 0:
                        t.wait_ge(tr.h, tr.n)
                    elif qh == 0:
                        t.wait_ge(tr.h, tr.n)
                    t.wait_ge(ev.h, ev_evac)
                    emit_scores(t, 0, kts[0])
                    if K > 1:
                        emit_scores(t, 1, kts[1])
                    ndiag_seen = 0
                    for i, kt in enumerate(kts):
                        t.wait_ge(ex.h, ex_base + i + 1)
                        if kt in diag:
                            ndiag_seen += 1
                            t.wait_ge(mk.h, mk_base + ndiag_seen)
                        first, last_i = (i == 0), (i == K - 1)
                        co = max(0, kt * 128 - q0)
                        last = None
                        for n in range(2):
                            lo, hi = max(n * 512, co), (n + 1) * 512
                            if lo >= hi:
                                continue
                            t.matmul(ps_den[:, lo:hi], ones_b[:, :],
                                     exps[i % 2][:, lo:hi], start=first, stop=last_i)
                            last = t.matmul(ps_att[:, lo:hi],
                                            vn[:, kt * 128:(kt + 1) * 128],
                                            exps[i % 2][:, lo:hi], start=first, stop=last_i)
                        mm.bump(last)
                        if i + 2 < K:
                            emit_scores(t, i + 2, kts[i + 2])

                @blk.scalar
                def _(sc, kts=kts, K=K, q0=q0, mm_base=mm_base, mk_base=mk_base,
                      diag=diag):
                    ndiag_seen = 0
                    for i, kt in enumerate(kts):
                        sc.wait_ge(mm.h, mm_base + (i + 1 if i < 2 else 2 * i))
                        if i >= 2:
                            sc.wait_ge(mm.h, mm_base + 3 + 2 * (i - 2))
                            if kts[i - 2] in diag:
                                ndiag_seen += 1
                                sc.wait_ge(mk.h, mk_base + ndiag_seen)
                        co = max(0, kt * 128 - q0)
                        pss = (ps_a, ps_b)[i % 2]
                        a = sc.activation(exps[i % 2][:, co:1024], pss[:, co:1024],
                                          AF.Exp, scale=RSQRT_HD)
                        ex.bump(a)

                @blk.vector
                def _(v, kts=kts, q0=q0, ex_base=ex_base, diag=diag):
                    for kt, co in diag.items():
                        i = kts.index(kt)
                        v.wait_ge(ex.h, ex_base + i + 1)
                        m = v.tensor_tensor(exps[i % 2][:, co:co + 128],
                                            exps[i % 2][:, co:co + 128],
                                            triu_b[:, :], ALU.mult)
                        mk.bump(m)

                @blk.vector
                def _(v, q0=q0):
                    v.wait_ge(mm.h, mm.n)
                    v.reciprocal(recip[:, :], ps_den[:, :])
                    e = v.tensor_tensor(att_o[:, q0:q0 + 1024], ps_att[:, :],
                                        recip[:, :], ALU.mult)
                    ev.bump(e)

            @blk.sync
            def _(s, h=h):
                s.wait_ge(ev.h, ev.n)
                i = s.dma_start(out=attnT[h * 128:(h + 1) * 128, :],
                                in_=att_o[:, :])
                st.bump(i, 16)
                if h == 0 and dbg_q is not None:
                    st.bump(s.dma_start(out=dbg_q[:, :], in_=qT[:, :]), 16)
                    st.bump(s.dma_start(out=dbg_k[:, :], in_=kT[:, :]), 16)
                    st.bump(s.dma_start(out=dbg_vn[:, :], in_=vn[:, :]), 16)
                    st.bump(s.dma_start(out=dbg_rc[:, :], in_=recip[:, :]), 16)

        @blk.sync
        def _(s):
            s.wait_ge(st.h, st.n)

        sems.close()
    es.close()
    return st


def build_l0_attn():
    nc = bass.Bass("TRN2", target_bir_lowering=False, debug=False,
                   detect_race_conditions=False)
    io = {
        "xTb": nc.dram_tensor("xTb", [D, S], F32, kind="ExternalInput"),
        "wqT": nc.dram_tensor("wqT", [D, 1024], F32, kind="ExternalInput"),
        "wkT": nc.dram_tensor("wkT", [D, 1024], F32, kind="ExternalInput"),
        "wvT": nc.dram_tensor("wvT", [D, 1024], F32, kind="ExternalInput"),
        "triu": nc.dram_tensor("triu", [128, 128], F32, kind="ExternalInput"),
        "onesc": nc.dram_tensor("onesc", [128, 128], F32, kind="ExternalInput"),
        "attnT": nc.dram_tensor("attnT", [1024, S], BF16, kind="ExternalOutput"),
    }
    with nc.Block() as blk:
        emit_l0_attn(nc, blk, io)
    return nc


def host_prep_l0_attn(inputs):
    x = inputs["L0_attn_hidden_states"]
    w_qkvT = np.ascontiguousarray(inputs["L0_attn_w_qkv"].T)
    triu = np.ascontiguousarray(np.triu(np.ones((128, 128), np.float32)))
    ones = np.ones((128, 128), np.float32)
    maps = []
    for c in range(NCORES):
        b, half = c // 2, c % 2
        lo = 8 * half * 128
        maps.append({
            "xTb": np.ascontiguousarray(x[b].T),
            "wqT": np.ascontiguousarray(w_qkvT[:, lo:lo + 1024]),
            "wkT": np.ascontiguousarray(w_qkvT[:, 2048 + lo:2048 + lo + 1024]),
            "wvT": np.ascontiguousarray(w_qkvT[:, 4096 + lo:4096 + lo + 1024]),
            "triu": triu,
            "onesc": ones,
        })
    return maps





NIO = 43


def emit_l0_ffn(nc, blk_outer, io, mkctr=None, upto="ALL", pfx="f"):
    attnTb, xresT, invn = io["attnTb"], io["xresT"], io["invn"]
    woP_ap, wgP_ap, wuP_ap, wdP_ap = (io["woP_ap"], io["wgP_ap"],
                                      io["wuP_ap"], io["wdP_ap"])
    x2T, x1T, rowsD = io["x2T"], io["x1T"], io["rowsD"]

    es = ExitStack()
    sb = lambda name, shape, dt: es.enter_context(nc.sbuf_tensor(pfx + name, shape, dt))
    ps = lambda name, shape, dt: es.enter_context(nc.psum_tensor(pfx + name, shape, dt))

    xr_sb = sb("xr_sb", [128, 16 * 512], F32)
    y_sb = sb("y_sb", [128, 16 * 512], F32)
    ybf = [sb(f"ybf{i}", [128, 512], BF16) for i in range(2)]
    x1b = sb("x1b", [128, 16 * 1024], BF16)
    wo_sb = [sb(f"wo_sb{i}", [128, 16 * 128], BF16) for i in range(2)]
    wg_sb = [sb(f"wg_sb{i}", [128, 16 * 128], BF16) for i in range(2)]
    wu_sb = [sb(f"wu_sb{i}", [128, 16 * 128], BF16) for i in range(2)]
    wd_sb = [sb(f"wd_sb{i}", [128, NIO * 128], BF16) for i in range(2)]
    h_sb = sb("h_sb", [128, NIO * 512], BF16)
    at_sb = h_sb
    sg_sb = sb("sg_sb", [128, 512], BF16)
    sq_sb = sb("sq_sb", [128, 512], BF16)
    inv_sb = sb("inv_sb", [128, 1], BF16)
    r1 = sb("r1", [1, 512], F32)
    r2 = sb("r2", [1, 512], F32)
    rtmp = sb("rtmp", [1, 512], F32)
    mb = sb("mb", [128, 512], F32)
    rbb = sb("rbb", [128, 512], F32)
    x1f = [sb(f"x1f{i}", [128, 512], F32) for i in range(2)]

    ps_y = [ps(f"ps_y{i}", [128, 512], F32) for i in range(2)]
    ps_u = [ps(f"ps_u{i}", [128, 512], F32) for i in range(2)]
    ps_m = ps("ps_m", [1, 512], F32)
    ps_v = ps("ps_v", [1, 512], F32)

    if True:
        blk = blk_outer
        sems = ExitStack()
        if mkctr is None:
            mkctr = lambda name: Ctr(nc, sems, pfx + name)
        gld = mkctr("gld")
        sld = mkctr("sld")
        mm = mkctr("mm")
        ev = mkctr("ev")
        ac = mkctr("ac")
        st = mkctr("st")
        bc = mkctr("bc")

        @blk.gpsimd
        def _(g):
            gld.bump(g.dma_start(out=inv_sb[:, :], in_=invn[:, :]), 16)

        def emit_stats(oc, src_bf):
            @blk.scalar
            def _(sc, oc=oc, src_bf=src_bf):
                sc.wait_ge(ev.h, ev.n)
                sc.wait_ge(mm.h, mm.n)
                ac.bump(sc.activation(sq_sb[:, :], src_bf[:, :], AF.Square))

            @blk.tensor
            def _(t, oc=oc, src_bf=src_bf):
                t.wait_ge(ev.h, ev.n)
                t.matmul(ps_m[:, :], inv_sb[:, :], src_bf[:, :],
                         start=(oc == 0), stop=(oc == 15))
                t.wait_ge(ac.h, ac.n)
                mm.bump(t.matmul(ps_v[:, :], inv_sb[:, :], sq_sb[:, :],
                                 start=(oc == 0), stop=(oc == 15)))

        def emit_ln(th, out_bf_full, out_dram):
            @blk.vector
            def _(v):
                v.wait_ge(mm.h, mm.n)
                v.tensor_copy(r1[:, :], ps_m[:, :])
                v.tensor_copy(r2[:, :], ps_v[:, :])
                v.tensor_tensor(rtmp[0:1, :], r1[0:1, :], r1[0:1, :], ALU.mult)
                v.tensor_tensor(r2[0:1, :], r2[0:1, :], rtmp[0:1, :],
                                ALU.subtract)
                ev.bump(v.tensor_scalar(r2[0:1, :], r2[0:1, :], 1e-5, None,
                                        ALU.add))

            @blk.scalar
            def _(sc):
                sc.wait_ge(ev.h, ev.n)
                ac.bump(sc.activation(r2[0:1, :], r2[0:1, :], AF.Sqrt))

            @blk.vector
            def _(v):
                v.wait_ge(ac.h, ac.n)
                ev.bump(v.reciprocal(r2[0:1, :], r2[0:1, :]))

            @blk.gpsimd
            def _(g):
                g.wait_ge(ev.h, ev.n)
                g.dma_start(out=rowsD[0:1, :], in_=r1[0:1, :]).then_inc(bc.h, 16)
                g.dma_start(out=rowsD[1:2, :], in_=r2[0:1, :]).then_inc(bc.h, 16)
                g.wait_ge(bc.h, bc.n + 32)
                bm = bass.AP(tensor=rowsD, offset=0, ap=[[0, 128], [1, 512]])
                br = bass.AP(tensor=rowsD, offset=512, ap=[[0, 128], [1, 512]])
                g.dma_start(out=mb[:, :], in_=bm).then_inc(bc.h, 16)
                g.dma_start(out=rbb[:, :], in_=br).then_inc(bc.h, 16)
            bc.n += 64

            for oc in range(16):
                @blk.vector
                def _(v, oc=oc, th=th):
                    if oc == 0:
                        v.wait_ge(bc.h, bc.n)
                    if oc >= 2:
                        v.wait_ge(st.h, st.n - 16)
                    xf = x1f[oc % 2]
                    v.tensor_tensor(xf[:, :], y_sb[:, oc * 512:(oc + 1) * 512],
                                    mb[:, :], ALU.subtract)
                    v.tensor_tensor(xf[:, :], xf[:, :], rbb[:, :], ALU.mult)
                    ev.bump(v.tensor_copy(
                        out_bf_full[:, oc * 1024 + th * 512: oc * 1024 + th * 512 + 512],
                        xf[:, :]))

                @blk.scalar
                def _(sc, oc=oc, th=th):
                    sc.wait_ge(ev.h, ev.n)
                    st.bump(sc.dma_start(
                        out=out_dram[oc * 128:(oc + 1) * 128, th * 512:th * 512 + 512],
                        in_=x1f[oc % 2][:, :]), 16)

        for th in range(2):
            t0 = th * 512

            @blk.sync
            def _(s, th=th, t0=t0):
                if th == 1:
                    s.wait_ge(mm.h, mm.n)
                for k in range(16):
                    sld.bump(s.dma_start(
                        out=at_sb[:, k * 512:(k + 1) * 512],
                        in_=attnTb[k * 128:(k + 1) * 128, t0:t0 + 512]), 16)
                for k in range(16):
                    sld.bump(s.dma_start(
                        out=xr_sb[:, k * 512:(k + 1) * 512],
                        in_=xresT[k * 128:(k + 1) * 128, t0:t0 + 512]), 16)
            LD_A = sld.n

            @blk.gpsimd
            def _(g, th=th):
                if th == 1:
                    g.wait_ge(mm.h, mm.n)
                for oc in range(2):
                    gld.bump(g.dma_start(
                        out=wo_sb[oc][:, :], in_=woP_ap(oc)), 16)
            GP_A = gld.n

            for oc in range(16):
                @blk.gpsimd
                def _(g, oc=oc, th=th):
                    if oc + 2 < 16:
                        g.wait_ge(mm.h, mm.n + 1)
                        gld.bump(g.dma_start(
                            out=wo_sb[oc % 2][:, :], in_=woP_ap(oc + 2)), 16)

                @blk.tensor
                def _(t, oc=oc, th=th, GP_A=GP_A, LD_A=LD_A):
                    if oc == 0:
                        t.wait_ge(sld.h, LD_A)
                    t.wait_ge(gld.h, GP_A + 16 * max(0, oc - 1))
                    if oc >= 2:
                        t.wait_ge(ev.h, ev.n)
                    last = None
                    for k in range(16):
                        last = t.matmul(ps_y[oc % 2][:, :],
                                        wo_sb[oc % 2][:, k * 128:(k + 1) * 128],
                                        at_sb[:, k * 512:(k + 1) * 512],
                                        start=(k == 0), stop=(k == 15))
                    mm.bump(last)

                @blk.vector
                def _(v, oc=oc):
                    v.wait_ge(mm.h, mm.n)
                    v.tensor_tensor(y_sb[:, oc * 512:(oc + 1) * 512],
                                    ps_y[oc % 2][:, :],
                                    xr_sb[:, oc * 512:(oc + 1) * 512], ALU.add)
                    ev.bump(v.tensor_copy(ybf[oc % 2][:, :],
                                          y_sb[:, oc * 512:(oc + 1) * 512]))

                if upto != "A0":
                    emit_stats(oc, ybf[oc % 2])

            if upto == "A0":
                for oc in range(16):
                    @blk.scalar
                    def _(sc, oc=oc, th=th):
                        sc.wait_ge(ev.h, ev.n)
                        st.bump(sc.dma_start(
                            out=x2T[oc * 128:(oc + 1) * 128, th * 512:th * 512 + 512],
                            in_=y_sb[:, oc * 512:(oc + 1) * 512]), 16)
                continue
            emit_ln(th, x1b, x1T)
            if upto == "A":
                continue

            @blk.sync
            def _(s, th=th):
                s.wait_ge(st.h, st.n)
                for buf in range(2):
                    sld.bump(s.dma_start(
                        out=wg_sb[buf][:, :], in_=wgP_ap(buf)), 16)
                    sld.bump(s.dma_start(
                        out=wu_sb[buf][:, :], in_=wuP_ap(buf)), 16)
            LD_B = sld.n

            for io in range(NIO):
                @blk.sync
                def _(s, io=io):
                    if io + 2 < NIO:
                        s.wait_ge(mm.h, mm.n + 2)
                        sld.bump(s.dma_start(
                            out=wg_sb[io % 2][:, :], in_=wgP_ap(io + 2)), 16)
                        sld.bump(s.dma_start(
                            out=wu_sb[io % 2][:, :], in_=wuP_ap(io + 2)), 16)

                @blk.tensor
                def _(t, io=io, th=th, LD_B=LD_B):
                    t.wait_ge(sld.h, LD_B + 32 * max(0, io - 1))
                    if io >= 2:
                        t.wait_ge(ev.h, ev.n)
                    last = None
                    for k in range(16):
                        last = t.matmul(ps_y[io % 2][:, :],
                                        wg_sb[io % 2][:, k * 128:(k + 1) * 128],
                                        x1b[:, k * 1024 + th * 512: k * 1024 + th * 512 + 512],
                                        start=(k == 0), stop=(k == 15))
                    mm.bump(last)
                    last = None
                    for k in range(16):
                        last = t.matmul(ps_u[io % 2][:, :],
                                        wu_sb[io % 2][:, k * 128:(k + 1) * 128],
                                        x1b[:, k * 1024 + th * 512: k * 1024 + th * 512 + 512],
                                        start=(k == 0), stop=(k == 15))
                    mm.bump(last)

                @blk.scalar
                def _(sc, io=io):
                    sc.wait_ge(mm.h, mm.n - 1)
                    sc.wait_ge(ev.h, ev.n)
                    ac.bump(sc.activation(sg_sb[:, :], ps_y[io % 2][:, :], AF.Silu))

                @blk.vector
                def _(v, io=io):
                    v.wait_ge(mm.h, mm.n)
                    v.wait_ge(ac.h, ac.n)
                    ev.bump(v.tensor_tensor(h_sb[:, io * 512:(io + 1) * 512],
                                            sg_sb[:, :], ps_u[io % 2][:, :], ALU.mult))

            if upto == "B":
                continue
            @blk.sync
            def _(s, th=th):
                s.wait_ge(st.h, st.n)
                for buf in range(2):
                    sld.bump(s.dma_start(
                        out=wd_sb[buf][:, :], in_=wdP_ap(buf)), 16)
                for k in range(16):
                    sld.bump(s.dma_start(
                        out=xr_sb[:, k * 512:(k + 1) * 512],
                        in_=x1T[k * 128:(k + 1) * 128, th * 512:th * 512 + 512]), 16)
            LD_B2 = sld.n

            for oc in range(16):
                @blk.sync
                def _(s, oc=oc):
                    if oc + 2 < 16:
                        s.wait_ge(mm.h, mm.n + 1)
                        sld.bump(s.dma_start(
                            out=wd_sb[oc % 2][:, :], in_=wdP_ap(oc + 2)), 16)

                @blk.tensor
                def _(t, oc=oc, LD_B2=LD_B2):
                    t.wait_ge(sld.h, LD_B2 + 16 * max(0, oc - 1))
                    if oc == 0:
                        t.wait_ge(ev.h, ev.n)
                    if oc >= 2:
                        t.wait_ge(ev.h, ev.n)
                    last = None
                    for io in range(NIO):
                        last = t.matmul(ps_y[oc % 2][:, :],
                                        wd_sb[oc % 2][:, io * 128:(io + 1) * 128],
                                        h_sb[:, io * 512:(io + 1) * 512],
                                        start=(io == 0), stop=(io == NIO - 1))
                    mm.bump(last)

                @blk.vector
                def _(v, oc=oc):
                    v.wait_ge(mm.h, mm.n)
                    v.tensor_tensor(y_sb[:, oc * 512:(oc + 1) * 512],
                                    ps_y[oc % 2][:, :],
                                    xr_sb[:, oc * 512:(oc + 1) * 512], ALU.add)
                    ev.bump(v.tensor_copy(ybf[oc % 2][:, :],
                                          y_sb[:, oc * 512:(oc + 1) * 512]))

                emit_stats(oc, ybf[oc % 2])

            emit_ln(th, x1b, x2T)

        @blk.scalar
        def _(sc):
            sc.wait_ge(st.h, st.n)

        sems.close()
    es.close()
    return st


def build_l0_ffn(upto="ALL", race=False):
    nc = bass.Bass("TRN2", target_bir_lowering=False, debug=False,
                   detect_race_conditions=race)
    woP = nc.dram_tensor("woP", [16, 16, 128, 128], F32, kind="ExternalInput")
    wgP = nc.dram_tensor("wgP", [NIO, 16, 128, 128], BF16, kind="ExternalInput")
    wuP = nc.dram_tensor("wuP", [NIO, 16, 128, 128], BF16, kind="ExternalInput")
    wdP = nc.dram_tensor("wdP", [16, NIO, 128, 128], BF16, kind="ExternalInput")
    io = {
        "attnTb": nc.dram_tensor("attnTb", [D, TOK], BF16, kind="ExternalInput"),
        "xresT": nc.dram_tensor("xresT", [D, TOK], F32, kind="ExternalInput"),
        "invn": nc.dram_tensor("invn", [128, 1], F32, kind="ExternalInput"),
        "x2T": nc.dram_tensor("x2T", [D, TOK], F32, kind="ExternalOutput"),
        "x1T": nc.dram_tensor("x1T", [D, TOK], F32),
        "rowsD": nc.dram_tensor("rowsD", [2, 512], F32),
        "woP_ap": lambda i: woP[i].rearrange("k p c -> p k c"),
        "wgP_ap": lambda i: wgP[i].rearrange("k p c -> p k c"),
        "wuP_ap": lambda i: wuP[i].rearrange("k p c -> p k c"),
        "wdP_ap": lambda i: wdP[i].rearrange("k p c -> p k c"),
    }
    with nc.Block() as blk:
        emit_l0_ffn(nc, blk, io, upto=upto)
    return nc


def host_prep_l0_ffn(inputs, attnT_full, WgT, WuT, WdT):
    x = inputs["L0_attn_hidden_states"]
    woT = np.ascontiguousarray(inputs["L0_attn_w_o"].T)
    woP = np.ascontiguousarray(
        woT.reshape(16, 128, 16, 128).transpose(2, 0, 1, 3))
    wg = np.ascontiguousarray(
        WgT.reshape(16, 128, NIO, 128).transpose(2, 0, 1, 3))
    wu = np.ascontiguousarray(
        WuT.reshape(16, 128, NIO, 128).transpose(2, 0, 1, 3))
    wd = np.ascontiguousarray(
        WdT.reshape(NIO, 128, 16, 128).transpose(2, 0, 1, 3))
    invn = np.full((128, 1), 1.0 / 2048.0, np.float32)
    maps = []
    for c in range(NCORES):
        b, half = c // 2, c % 2
        tc = slice(half * 1024, half * 1024 + 1024)
        maps.append({
            "attnTb": np.ascontiguousarray(attnT_full[b][:, tc]),
            "xresT": np.ascontiguousarray(x[b].T[:, tc]),
            "woP": woP, "wgP": wg, "wuP": wu, "wdP": wd, "invn": invn,
        })
    return maps



GC = 5504 // 8
DC = 2048 // 8
UPC = 8192 // 8
DNC = 2048 // 8


def build_wprep():
    nc = bass.Bass("TRN2", target_bir_lowering=False, debug=False,
                   detect_race_conditions=False)
    gidx = nc.dram_tensor("gidx", [D, GC], I32, kind="ExternalInput")
    uidx = nc.dram_tensor("uidx", [D, GC], I32, kind="ExternalInput")
    didx = nc.dram_tensor("didx", [5504, DC], I32, kind="ExternalInput")
    lutb = nc.dram_tensor("lutb", [128, 16], F32, kind="ExternalInput")
    h128 = nc.dram_tensor("h128", [128, 128], F32, kind="ExternalInput")
    wupT = nc.dram_tensor("wupT", [D, UPC], F32, kind="ExternalInput")
    wgtT = nc.dram_tensor("wgtT", [D, UPC], F32, kind="ExternalInput")
    wdnT = nc.dram_tensor("wdnT", [8192, DNC], F32, kind="ExternalInput")

    wg_o = nc.dram_tensor("wg_o", [D, GC], BF16, kind="ExternalOutput")
    wu_o = nc.dram_tensor("wu_o", [D, GC], BF16, kind="ExternalOutput")
    wd_o = nc.dram_tensor("wd_o", [5504, DC], BF16, kind="ExternalOutput")
    wupP = nc.dram_tensor("wupP", [D, UPC], BF16, kind="ExternalOutput")
    wgtP = nc.dram_tensor("wgtP", [D, UPC], BF16, kind="ExternalOutput")
    wdnP = nc.dram_tensor("wdnP", [8192, DNC], BF16, kind="ExternalOutput")

    zD = nc.dram_tensor("zD", [8192, UPC], BF16)

    es = ExitStack()
    sb = lambda name, shape, dt: es.enter_context(nc.sbuf_tensor(name, shape, dt))
    ps = lambda name, shape, dt: es.enter_context(nc.psum_tensor(name, shape, dt))

    idxb = [sb(f"idxb{i}", [128, 1024], BF16) for i in range(2)]
    accb = [sb(f"accb{i}", [128, GC], BF16) for i in range(2)]
    maskb = sb("maskb", [128, GC], BF16)
    lut_sb = sb("lut_sb", [128, 16], F32)
    h128b = sb("h128b", [128, 128], BF16)
    z_sb = sb("z_sb", [128, 16 * 1024], BF16)
    z2_sb = sb("z2_sb", [128, 64 * 256], BF16)
    ps_h = [ps(f"ps_h{i}", [128, 512], F32) for i in range(2)]

    with nc.Block() as blk:
        sems = ExitStack()
        gld = Ctr(nc, sems, "gld")
        mm = Ctr(nc, sems, "mm")
        ev = Ctr(nc, sems, "ev")
        st = Ctr(nc, sems, "st")
        bd = Ctr(nc, sems, "bd")

        @blk.gpsimd
        def _(g):
            gld.bump(g.dma_start(out=lut_sb[:, :], in_=lutb[:, :]), 16)
            gld.bump(g.dma_start(out=h128b[:, :], in_=h128[:, :]), 16)

        ghost_jobs = []
        for (idx_dr, out_dr, ntile, cols) in ((gidx, wg_o, 16, GC),
                                              (uidx, wu_o, 16, GC),
                                              (didx, wd_o, 43, DC)):
            for kt in range(ntile):
                ghost_jobs.append((idx_dr, out_dr, kt, cols))

        for ji, (idx_dr, out_dr, kt, cols) in enumerate(ghost_jobs):
            @blk.gpsimd
            def _(g, ji=ji, idx_dr=idx_dr, kt=kt, cols=cols):
                if ji >= 2:
                    g.wait_ge(ev.h, ji - 1)
                gld.bump(g.dma_start(
                    out=idxb[ji % 2][:, 0:cols],
                    in_=idx_dr[kt * 128:(kt + 1) * 128, :]), 16)

            @blk.vector
            def _(v, ji=ji, cols=cols):
                v.wait_ge(gld.h, 32 + 16 * (ji + 1))
                if ji >= 2:
                    v.wait_ge(st.h, 16 * (ji - 1))
                ib = idxb[ji % 2][:, 0:cols]
                ab = accb[ji % 2][:, 0:cols]
                mk = maskb[:, 0:cols]
                v.tensor_scalar(ab, ib, 0.0, lut_sb[:, 0:1], ALU.is_equal, ALU.mult)
                last = None
                for j in range(1, 16):
                    v.tensor_scalar(mk, ib, float(j), None, ALU.is_equal)
                    last = v.scalar_tensor_tensor(ab, mk, lut_sb[:, j:j + 1], ab,
                                                  ALU.mult, ALU.add)
                ev.bump(last)

            @blk.sync
            def _(s, ji=ji, out_dr=out_dr, kt=kt, cols=cols):
                s.wait_ge(ev.h, ji + 1)
                st.bump(s.dma_start(out=out_dr[kt * 128:(kt + 1) * 128, :],
                                    in_=accb[ji % 2][:, 0:cols]), 16)
        NGHOST = len(ghost_jobs)

        def emit_wprime(w_in, w_out, ntile, cols, nbf_stages):
            for kt in range(ntile):
                @blk.gpsimd
                def _(g, kt=kt, cols=cols, w_in=w_in, evn=ev.n):
                    g.wait_ge(ev.h, evn)
                    g.wait_ge(mm.h, mm.n)
                    gld.bump(g.dma_start(
                        out=idxb[kt % 2][:, 0:cols],
                        in_=w_in[kt * 128:(kt + 1) * 128, :]), 16)

                @blk.tensor
                def _(t, kt=kt, cols=cols):
                    t.wait_ge(gld.h, gld.n)
                    t.wait_ge(ev.h, ev.n)
                    nch = (cols + 511) // 512
                    last = None
                    for n in range(nch):
                        lo, hi = n * 512, min((n + 1) * 512, cols)
                        last = t.matmul(ps_h[n % 2][:, 0:hi - lo], h128b[:, :],
                                        idxb[kt % 2][:, lo:hi], start=True, stop=True)
                    mm.bump(last)

                @blk.vector
                def _(v, kt=kt, cols=cols, stn=st.n):
                    if kt == 0:
                        v.wait_ge(st.h, stn)
                    v.wait_ge(mm.h, mm.n)
                    nch = (cols + 511) // 512
                    lastv = None
                    for n in range(nch):
                        lo, hi = n * 512, min((n + 1) * 512, cols)
                        lastv = v.tensor_copy(z_sb[:, lo:hi], ps_h[n % 2][:, 0:hi - lo])
                    ev.bump(lastv)

                @blk.sync
                def _(s, kt=kt, cols=cols, bdn=bd.n):
                    if kt == 0:
                        s.wait_ge(bd.h, bdn)
                    s.wait_ge(ev.h, ev.n)
                    bd.bump(s.dma_start(out=zD[kt * 128:(kt + 1) * 128, 0:cols],
                                        in_=z_sb[:, 0:cols]), 16)

            @blk.sync
            def _(s, ntile=ntile, cols=cols, stn=st.n):
                s.wait_ge(st.h, stn)
                s.wait_ge(bd.h, bd.n)
                i = s.dma_start(
                    out=z2_sb[:, 0:ntile * cols],
                    in_=zD[0:ntile * 128, 0:cols].rearrange("(j p) o -> p j o", p=128))
                bd.bump(i, 16)

            @blk.vector
            def _(v, ntile=ntile, cols=cols, nbf=nbf_stages):
                v.wait_ge(bd.h, bd.n)
                hstep = 1
                for s_i in range(nbf):
                    ngrp = ntile // (2 * hstep)
                    apA = bass.AP(tensor=z2_sb, offset=0,
                                  ap=[[z2_sb.shape[1], 128],
                                      [2 * hstep * cols, ngrp], [1, hstep * cols]])
                    apB = bass.AP(tensor=z2_sb, offset=hstep * cols,
                                  ap=[[z2_sb.shape[1], 128],
                                      [2 * hstep * cols, ngrp], [1, hstep * cols]])
                    apA2 = bass.AP(tensor=z_sb, offset=0,
                                   ap=[[z_sb.shape[1], 128],
                                       [2 * hstep * cols, ngrp], [1, hstep * cols]])
                    apB2 = bass.AP(tensor=z_sb, offset=hstep * cols,
                                   ap=[[z_sb.shape[1], 128],
                                       [2 * hstep * cols, ngrp], [1, hstep * cols]])
                    srcA, srcB = (apA, apB) if s_i % 2 == 0 else (apA2, apB2)
                    dstA, dstB = (apA2, apB2) if s_i % 2 == 0 else (apA, apB)
                    v.tensor_tensor(dstA, srcA, srcB, ALU.add)
                    last = v.tensor_tensor(dstB, srcA, srcB, ALU.subtract)
                    hstep *= 2
                ev.bump(last)

            final_buf = z2_sb if (nbf_stages % 2 == 0) else z_sb

            @blk.sync
            def _(s, ntile=ntile, cols=cols, w_out=w_out, final_buf=final_buf):
                s.wait_ge(ev.h, ev.n)
                i = s.dma_start(
                    out=w_out[0:ntile * 128, 0:cols].rearrange("(j p) o -> p j o", p=128),
                    in_=bass.AP(tensor=final_buf, offset=0,
                                ap=[[final_buf.shape[1], 128], [cols, ntile], [1, cols]]))
                st.bump(i, 16)

        emit_wprime(wupT, wupP, 16, UPC, 4)
        emit_wprime(wgtT, wgtP, 16, UPC, 4)
        emit_wprime(wdnT, wdnP, 64, DNC, 6)

        @blk.sync
        def _(s):
            s.wait_ge(st.h, st.n)

        sems.close()
    es.close()
    return nc


def host_prep_wprep(inputs):
    gT = np.ascontiguousarray(inputs["L0_ffn_gate_idx"].T)
    uT = np.ascontiguousarray(inputs["L0_ffn_up_idx"].T)
    dT = np.ascontiguousarray(inputs["L0_ffn_down_idx"].T)
    upT = np.ascontiguousarray(inputs["L1_ffn_W_up"].T)
    gtT = np.ascontiguousarray(inputs["L1_ffn_W_gate"].T)
    dnT = np.ascontiguousarray(inputs["L1_ffn_W_down"].T)
    lut = np.ascontiguousarray(
        np.broadcast_to(inputs["L0_ffn_lut"][None, :], (128, 16))).astype(np.float32)
    H = np.array([[1.0]], np.float32)
    while H.shape[0] < 128:
        H = np.block([[H, H], [H, -H]])
    maps = []
    for c in range(NCORES):
        maps.append({
            "gidx": np.ascontiguousarray(gT[:, c * GC:(c + 1) * GC]),
            "uidx": np.ascontiguousarray(uT[:, c * GC:(c + 1) * GC]),
            "didx": np.ascontiguousarray(dT[:, c * DC:(c + 1) * DC]),
            "lutb": lut,
            "h128": np.ascontiguousarray(H),
            "wupT": np.ascontiguousarray(upT[:, c * UPC:(c + 1) * UPC]),
            "wgtT": np.ascontiguousarray(gtT[:, c * UPC:(c + 1) * UPC]),
            "wdnT": np.ascontiguousarray(dnT[:, c * DNC:(c + 1) * DNC]),
        })
    return maps



def emit_l1_ffn(nc, blk_outer, io, mkctr=None, pfx="L"):
    ZTb, xresT, invn = io["ZTb"], io["xresT"], io["invn"]
    h128d = io["h128"]
    woP_ap = io["l1woP_ap"]
    wupP_ap, wgtP_ap, wdnP_ap = io["wupP_ap"], io["wgtP_ap"], io["wdnP_ap"]
    su_col, wsu_col, wsg_col = io["su_col"], io["wsu_col"], io["wsg_col"]
    sudp_col, wsd_col = io["sudp_col"], io["wsd_col"]
    x4T, x3T, rowsD = io["x4T"], io["x3T"], io["rowsD"]
    dbg_h, dbg_y0 = io.get("dbg_h"), io.get("dbg_y0")

    es = ExitStack()
    sb = lambda name, shape, dt: es.enter_context(nc.sbuf_tensor(name, shape, dt))
    ps = lambda name, shape, dt: es.enter_context(nc.psum_tensor(name, shape, dt))

    xr_sb = sb("xr_sb", [128, 16 * 512], F32)
    y_sb = sb("y_sb", [128, 16 * 512], F32)
    ybf = [sb(f"ybf{i}", [128, 512], BF16) for i in range(2)]
    x3b = sb("x3b", [128, 16 * 1024], BF16)
    h_sb = sb("h_sb", [128, 32 * 512], BF16)
    at_sb = sb("at_sb", [128, 16 * 512], BF16)
    wo_sb = [sb(f"wo_sb{i}", [128, 16 * 128], BF16) for i in range(2)]
    wg_sb = [sb(f"wg_sb{i}", [128, 16 * 128], BF16) for i in range(2)]
    wu_sb = [sb(f"wu_sb{i}", [128, 16 * 128], BF16) for i in range(2)]
    wd_sb = [sb(f"wd_sb{i}", [128, 32 * 128], BF16) for i in range(2)]
    sg_sb = sb("sg_sb", [128, 512], BF16)
    sq_sb = sb("sq_sb", [128, 512], BF16)
    gsc_sb = sb("gsc_sb", [128, 512], BF16)
    inv_sb = sb("inv_sb", [128, 1], BF16)
    su_sb = sb("su_sb", [128, 16], F32)
    h128L = sb("h128L", [128, 128], BF16)
    wsu_sb = sb("wsu_sb", [128, 64], F32)
    wsg_sb = sb("wsg_sb", [128, 64], F32)
    sudp_sb = sb("sudp_sb", [128, 64], F32)
    wsd_sb = sb("wsd_sb", [128, 16], F32)
    sup_sb = sb("sup_sb", [128, 64], F32)
    r1 = sb("r1", [1, 512], F32)
    r2 = sb("r2", [1, 512], F32)
    rtmp = sb("rtmp", [1, 512], F32)
    mbb = sb("mbb", [128, 512], F32)
    rbb = sb("rbb", [128, 512], F32)
    x1f = [sb(f"x1f{i}", [128, 512], F32) for i in range(2)]

    ps_y = [ps(f"ps_y{i}", [128, 512], F32) for i in range(2)]
    ps_u = [ps(f"ps_u{i}", [128, 512], F32) for i in range(2)]
    ps_m = ps("ps_m", [1, 512], F32)
    ps_v = ps("ps_v", [1, 512], F32)

    RS2048 = float(1.0 / np.sqrt(2048.0))
    RS8192 = float(1.0 / np.sqrt(8192.0))

    if True:
        blk = blk_outer
        sems = ExitStack()
        if mkctr is None:
            mkctr = lambda name: Ctr(nc, sems, pfx + name)
        gld = mkctr("gld")
        sld = mkctr("sld")
        mm = mkctr("mm")
        ev = mkctr("ev")
        ac = mkctr("ac")
        st = mkctr("st")
        bc = mkctr("bc")

        @blk.gpsimd
        def _(g):
            gld.bump(g.dma_start(out=inv_sb[:, :], in_=invn[:, :]), 16)
            gld.bump(g.dma_start(out=h128L[:, :], in_=h128d[:, :]), 16)
            for dst, srcd in ((su_sb, su_col), (wsu_sb, wsu_col), (wsg_sb, wsg_col),
                              (sudp_sb, sudp_col), (wsd_sb, wsd_col)):
                gld.bump(g.dma_start(out=dst[:, :], in_=srcd[:, :]), 16)

        @blk.vector
        def _(v):
            v.wait_ge(gld.h, gld.n)
            v.tensor_tensor(sup_sb[:, :], wsu_sb[:, :], sudp_sb[:, :], ALU.mult)
            v.tensor_scalar(sup_sb[:, :], sup_sb[:, :], RS2048, None, ALU.mult)
            v.tensor_scalar(wsg_sb[:, :], wsg_sb[:, :], RS2048, None, ALU.mult)
            ev.bump(v.tensor_scalar(wsd_sb[:, :], wsd_sb[:, :], RS8192, None,
                                    ALU.mult))
        SCALES_EV = ev.n

        def emit_stats(oc, src_bf):
            @blk.scalar
            def _(sc, oc=oc, src_bf=src_bf):
                sc.wait_ge(ev.h, ev.n)
                sc.wait_ge(mm.h, mm.n)
                ac.bump(sc.activation(sq_sb[:, :], src_bf[:, :], AF.Square))

            @blk.tensor
            def _(t, oc=oc, src_bf=src_bf):
                t.wait_ge(ev.h, ev.n)
                t.matmul(ps_m[:, :], inv_sb[:, :], src_bf[:, :],
                         start=(oc == 0), stop=(oc == 15))
                t.wait_ge(ac.h, ac.n)
                mm.bump(t.matmul(ps_v[:, :], inv_sb[:, :], sq_sb[:, :],
                                 start=(oc == 0), stop=(oc == 15)))

        def emit_ln(th, out_mode):
            @blk.vector
            def _(v):
                v.wait_ge(mm.h, mm.n)
                v.tensor_copy(r1[:, :], ps_m[:, :])
                v.tensor_copy(r2[:, :], ps_v[:, :])
                v.tensor_tensor(rtmp[0:1, :], r1[0:1, :], r1[0:1, :], ALU.mult)
                v.tensor_tensor(r2[0:1, :], r2[0:1, :], rtmp[0:1, :], ALU.subtract)
                ev.bump(v.tensor_scalar(r2[0:1, :], r2[0:1, :], 1e-5, None, ALU.add))

            @blk.scalar
            def _(sc):
                sc.wait_ge(ev.h, ev.n)
                ac.bump(sc.activation(r2[0:1, :], r2[0:1, :], AF.Sqrt))

            @blk.vector
            def _(v):
                v.wait_ge(ac.h, ac.n)
                ev.bump(v.reciprocal(r2[0:1, :], r2[0:1, :]))

            @blk.gpsimd
            def _(g):
                g.wait_ge(ev.h, ev.n)
                g.dma_start(out=rowsD[0:1, :], in_=r1[0:1, :]).then_inc(bc.h, 16)
                g.dma_start(out=rowsD[1:2, :], in_=r2[0:1, :]).then_inc(bc.h, 16)
                g.wait_ge(bc.h, bc.n + 32)
                bm = bass.AP(tensor=rowsD, offset=0, ap=[[0, 128], [1, 512]])
                br = bass.AP(tensor=rowsD, offset=512, ap=[[0, 128], [1, 512]])
                g.dma_start(out=mbb[:, :], in_=bm).then_inc(bc.h, 16)
                g.dma_start(out=rbb[:, :], in_=br).then_inc(bc.h, 16)
            bc.n += 64

            for oc in range(16):
                @blk.vector
                def _(v, oc=oc, th=th, out_mode=out_mode):
                    if oc == 0:
                        v.wait_ge(bc.h, bc.n)
                    if oc >= 2:
                        v.wait_ge(st.h, st.n - 16)
                    xf = x1f[oc % 2]
                    v.tensor_tensor(xf[:, :], y_sb[:, oc * 512:(oc + 1) * 512],
                                    mbb[:, :], ALU.subtract)
                    last = v.tensor_tensor(xf[:, :], xf[:, :], rbb[:, :], ALU.mult)
                    if out_mode == "x3":
                        last = v.tensor_scalar(
                            x3b[:, oc * 1024 + th * 512: oc * 1024 + th * 512 + 512],
                            xf[:, :], su_sb[:, oc:oc + 1], None, ALU.mult)
                    ev.bump(last)

                @blk.scalar
                def _(sc, oc=oc, th=th, out_mode=out_mode):
                    sc.wait_ge(ev.h, ev.n)
                    dstd = x3T if out_mode == "x3" else x4T
                    st.bump(sc.dma_start(
                        out=dstd[oc * 128:(oc + 1) * 128, th * 512:th * 512 + 512],
                        in_=x1f[oc % 2][:, :]), 16)

        for th in range(2):
            t0 = th * 512

            @blk.sync
            def _(s, th=th, t0=t0):
                if th == 1:
                    s.wait_ge(mm.h, mm.n)
                for k in range(16):
                    sld.bump(s.dma_start(
                        out=at_sb[:, k * 512:(k + 1) * 512],
                        in_=ZTb[k * 128:(k + 1) * 128, t0:t0 + 512]), 16)
                for k in range(16):
                    sld.bump(s.dma_start(
                        out=xr_sb[:, k * 512:(k + 1) * 512],
                        in_=xresT[k * 128:(k + 1) * 128, t0:t0 + 512]), 16)
            LD_A = sld.n

            @blk.gpsimd
            def _(g, th=th):
                if th == 1:
                    g.wait_ge(mm.h, mm.n)
                for oc in range(2):
                    gld.bump(g.dma_start(
                        out=wo_sb[oc][:, :], in_=woP_ap(oc)), 16)
            GP_A = gld.n

            for oc in range(16):
                @blk.gpsimd
                def _(g, oc=oc):
                    if oc + 2 < 16:
                        g.wait_ge(mm.h, mm.n + 1)
                        gld.bump(g.dma_start(
                            out=wo_sb[oc % 2][:, :], in_=woP_ap(oc + 2)), 16)

                @blk.tensor
                def _(t, oc=oc, GP_A=GP_A, LD_A=LD_A):
                    if oc == 0:
                        t.wait_ge(sld.h, LD_A)
                    t.wait_ge(gld.h, GP_A + 16 * max(0, oc - 1))
                    if oc >= 2:
                        t.wait_ge(ev.h, ev.n)
                    last = None
                    for k in range(16):
                        last = t.matmul(ps_y[oc % 2][:, :],
                                        wo_sb[oc % 2][:, k * 128:(k + 1) * 128],
                                        at_sb[:, k * 512:(k + 1) * 512],
                                        start=(k == 0), stop=(k == 15))
                    mm.bump(last)

                @blk.vector
                def _(v, oc=oc):
                    v.wait_ge(mm.h, mm.n)
                    v.tensor_tensor(y_sb[:, oc * 512:(oc + 1) * 512],
                                    ps_y[oc % 2][:, :],
                                    xr_sb[:, oc * 512:(oc + 1) * 512], ALU.add)
                    ev.bump(v.tensor_copy(ybf[oc % 2][:, :],
                                          y_sb[:, oc * 512:(oc + 1) * 512]))

                emit_stats(oc, ybf[oc % 2])

            emit_ln(th, "x3")

            for k in range(16):
                @blk.tensor
                def _(t, k=k, th=th):
                    t.wait_ge(ev.h, ev.n)
                    c0 = k * 1024 + th * 512
                    mm.bump(t.matmul(ps_y[k % 2][:, :], h128L[:, :],
                                     x3b[:, c0:c0 + 512], start=True, stop=True))

                @blk.vector
                def _(v, k=k, th=th):
                    v.wait_ge(mm.h, mm.n)
                    c0 = k * 1024 + th * 512
                    ev.bump(v.tensor_copy(x3b[:, c0:c0 + 512], ps_y[k % 2][:, :]))

            @blk.vector
            def _(v, th=th):
                v.wait_ge(ev.h, ev.n)
                rl = x3b.shape[1]
                cur = th
                last = None
                h = 1
                for s_i in range(4):
                    ngrp = 16 // (2 * h)
                    dsth = 1 - cur
                    for j in range(h):
                        apSA = bass.AP(tensor=x3b, offset=j * 1024 + cur * 512,
                                       ap=[[rl, 128], [2 * h * 1024, ngrp], [1, 512]])
                        apSB = bass.AP(tensor=x3b,
                                       offset=(j + h) * 1024 + cur * 512,
                                       ap=[[rl, 128], [2 * h * 1024, ngrp], [1, 512]])
                        apDA = bass.AP(tensor=x3b, offset=j * 1024 + dsth * 512,
                                       ap=[[rl, 128], [2 * h * 1024, ngrp], [1, 512]])
                        apDB = bass.AP(tensor=x3b,
                                       offset=(j + h) * 1024 + dsth * 512,
                                       ap=[[rl, 128], [2 * h * 1024, ngrp], [1, 512]])
                        v.tensor_tensor(apDA, apSA, apSB, ALU.add)
                        last = v.tensor_tensor(apDB, apSA, apSB, ALU.subtract)
                    cur = dsth
                    h *= 2
                assert cur == th
                ev.bump(last)

            @blk.tensor
            def _(t):
                t.wait_ge(ev.h, ev.n)

            for ioh in range(2):
                @blk.gpsimd
                def _(g, ioh=ioh, th=th):
                    g.wait_ge(st.h, st.n)
                    g.wait_ge(mm.h, mm.n)
                    for buf in range(2):
                        gld.bump(g.dma_start(
                            out=wg_sb[buf][:, :], in_=wgtP_ap(ioh * 32 + buf)), 16)
                        gld.bump(g.dma_start(
                            out=wu_sb[buf][:, :], in_=wupP_ap(ioh * 32 + buf)), 16)
                LD_B = gld.n

                for io in range(32):
                    gio = ioh * 32 + io

                    @blk.gpsimd
                    def _(g, io=io, gio=gio):
                        if io + 2 < 32:
                            g.wait_ge(mm.h, mm.n + 2)
                            gld.bump(g.dma_start(
                                out=wg_sb[io % 2][:, :], in_=wgtP_ap(gio + 2)), 16)
                            gld.bump(g.dma_start(
                                out=wu_sb[io % 2][:, :], in_=wupP_ap(gio + 2)), 16)

                    @blk.tensor
                    def _(t, io=io, th=th, LD_B=LD_B):
                        t.wait_ge(gld.h, LD_B + 32 * max(0, io - 1))
                        t.wait_ge(ev.h, ev.n)
                        last = None
                        for k in range(16):
                            last = t.matmul(
                                ps_y[io % 2][:, :],
                                wg_sb[io % 2][:, k * 128:(k + 1) * 128],
                                x3b[:, k * 1024 + th * 512: k * 1024 + th * 512 + 512],
                                start=(k == 0), stop=(k == 15))
                        mm.bump(last)
                        last = None
                        for k in range(16):
                            last = t.matmul(
                                ps_u[io % 2][:, :],
                                wu_sb[io % 2][:, k * 128:(k + 1) * 128],
                                x3b[:, k * 1024 + th * 512: k * 1024 + th * 512 + 512],
                                start=(k == 0), stop=(k == 15))
                        mm.bump(last)

                    @blk.vector
                    def _(v, io=io, gio=gio):
                        v.wait_ge(mm.h, mm.n - 1)
                        v.wait_ge(ac.h, ac.n)
                        ev.bump(v.tensor_scalar(gsc_sb[:, :], ps_y[io % 2][:, :],
                                                wsg_sb[:, gio:gio + 1], None,
                                                ALU.mult))

                    @blk.scalar
                    def _(sc, io=io):
                        sc.wait_ge(ev.h, ev.n)
                        ac.bump(sc.activation(sg_sb[:, :], gsc_sb[:, :], AF.Silu))

                    @blk.vector
                    def _(v, io=io, gio=gio):
                        v.wait_ge(mm.h, mm.n)
                        v.wait_ge(ac.h, ac.n)
                        ev.bump(v.scalar_tensor_tensor(
                            h_sb[:, io * 512:(io + 1) * 512],
                            ps_u[io % 2][:, :], sup_sb[:, gio:gio + 1], sg_sb[:, :],
                            ALU.mult, ALU.mult))

                @blk.sync
                def _(s, ioh=ioh):
                    s.wait_ge(mm.h, mm.n)
                    for buf in range(2):
                        sld.bump(s.dma_start(
                            out=wd_sb[buf][:, :], in_=wdnP_ap(buf, ioh)), 16)
                LD_D = sld.n

                for oc in range(16):
                    @blk.sync
                    def _(s, oc=oc, ioh=ioh):
                        if oc + 2 < 16:
                            s.wait_ge(mm.h, mm.n + 1)
                            sld.bump(s.dma_start(
                                out=wd_sb[oc % 2][:, :], in_=wdnP_ap(oc + 2, ioh)), 16)

                    @blk.tensor
                    def _(t, oc=oc, LD_D=LD_D):
                        t.wait_ge(sld.h, LD_D + 16 * max(0, oc - 1))
                        if oc == 0:
                            t.wait_ge(ev.h, ev.n)
                        if oc >= 2:
                            t.wait_ge(ev.h, ev.n)
                        last = None
                        for io in range(32):
                            last = t.matmul(ps_y[oc % 2][:, :],
                                            wd_sb[oc % 2][:, io * 128:(io + 1) * 128],
                                            h_sb[:, io * 512:(io + 1) * 512],
                                            start=(io == 0), stop=(io == 31))
                        mm.bump(last)

                    @blk.vector
                    def _(v, oc=oc, ioh=ioh):
                        v.wait_ge(mm.h, mm.n)
                        if ioh == 0:
                            ev.bump(v.tensor_copy(y_sb[:, oc * 512:(oc + 1) * 512],
                                                  ps_y[oc % 2][:, :]))
                        else:
                            ev.bump(v.tensor_tensor(y_sb[:, oc * 512:(oc + 1) * 512],
                                                    y_sb[:, oc * 512:(oc + 1) * 512],
                                                    ps_y[oc % 2][:, :], ALU.add))

            if dbg_h is not None and th == 0:
                @blk.sync
                def _(s):
                    s.wait_ge(ev.h, ev.n)
                    st.bump(s.dma_start(out=dbg_h[:, :], in_=h_sb[:, :]), 16)
                    st.bump(s.dma_start(out=dbg_y0[:, :], in_=y_sb[:, :]), 16)

            @blk.sync
            def _(s, th=th):
                s.wait_ge(st.h, st.n)
                for k in range(16):
                    sld.bump(s.dma_start(
                        out=xr_sb[:, k * 512:(k + 1) * 512],
                        in_=x3T[k * 128:(k + 1) * 128, th * 512:th * 512 + 512]), 16)
            LD_R = sld.n

            for oc in range(16):
                @blk.vector
                def _(v, oc=oc, LD_R=LD_R):
                    if oc == 0:
                        v.wait_ge(sld.h, LD_R)
                        v.wait_ge(st.h, st.n)
                    if oc >= 2:
                        v.wait_ge(mm.h, mm.n - 1)
                    v.scalar_tensor_tensor(
                        y_sb[:, oc * 512:(oc + 1) * 512],
                        y_sb[:, oc * 512:(oc + 1) * 512], wsd_sb[:, oc:oc + 1],
                        xr_sb[:, oc * 512:(oc + 1) * 512], ALU.mult, ALU.add)
                    ev.bump(v.tensor_copy(ybf[oc % 2][:, :],
                                          y_sb[:, oc * 512:(oc + 1) * 512]))

                emit_stats(oc, ybf[oc % 2])

            emit_ln(th, "x4")

        @blk.scalar
        def _(sc):
            sc.wait_ge(st.h, st.n)

        sems.close()
    es.close()
    return st


def build_l1_ffn(dbg=False):
    nc = bass.Bass("TRN2", target_bir_lowering=False, debug=False,
                   detect_race_conditions=False)
    woP = nc.dram_tensor("woP", [16, 16, 128, 128], F32, kind="ExternalInput")
    wupP = nc.dram_tensor("wupP", [64, 16, 128, 128], BF16, kind="ExternalInput")
    wgtP = nc.dram_tensor("wgtP", [64, 16, 128, 128], BF16, kind="ExternalInput")
    wdnP = nc.dram_tensor("wdnP", [16, 64, 128, 128], BF16, kind="ExternalInput")
    io = {
        "ZTb": nc.dram_tensor("ZTb", [D, TOK], BF16, kind="ExternalInput"),
        "xresT": nc.dram_tensor("xresT", [D, TOK], F32, kind="ExternalInput"),
        "su_col": nc.dram_tensor("su_col", [128, 16], F32, kind="ExternalInput"),
        "wsu_col": nc.dram_tensor("wsu_col", [128, 64], F32, kind="ExternalInput"),
        "wsg_col": nc.dram_tensor("wsg_col", [128, 64], F32, kind="ExternalInput"),
        "sudp_col": nc.dram_tensor("sudp_col", [128, 64], F32, kind="ExternalInput"),
        "wsd_col": nc.dram_tensor("wsd_col", [128, 16], F32, kind="ExternalInput"),
        "invn": nc.dram_tensor("invn", [128, 1], F32, kind="ExternalInput"),
        "h128": nc.dram_tensor("h128", [128, 128], F32, kind="ExternalInput"),
        "x4T": nc.dram_tensor("x4T", [D, TOK], F32, kind="ExternalOutput"),
        "x3T": nc.dram_tensor("x3T", [D, TOK], F32, kind="ExternalOutput"),
        "dbg_h": nc.dram_tensor("dbg_h", [128, 32 * 512], BF16,
                                kind="ExternalOutput" if dbg else "Internal"),
        "dbg_y0": nc.dram_tensor("dbg_y0", [128, 16 * 512], F32,
                                 kind="ExternalOutput" if dbg else "Internal"),
        "rowsD": nc.dram_tensor("rowsD", [2, 512], F32),
        "l1woP_ap": lambda i: woP[i].rearrange("k p c -> p k c"),
        "wupP_ap": lambda i: wupP[i].rearrange("k p c -> p k c"),
        "wgtP_ap": lambda i: wgtP[i].rearrange("k p c -> p k c"),
        "wdnP_ap": lambda oc, ioh: wdnP[oc, ioh * 32:(ioh + 1) * 32].rearrange(
            "k p c -> p k c"),
    }
    with nc.Block() as blk:
        emit_l1_ffn(nc, blk, io)
    return nc


def host_prep_l1_ffn(inputs, ZT_full, x2_full):
    woT = np.ascontiguousarray(inputs["L1_attn_Wo"].T)
    woP = np.ascontiguousarray(woT.reshape(16, 128, 16, 128).transpose(2, 0, 1, 3))
    invn = np.full((128, 1), 1.0 / 2048.0, np.float32)
    su_col = np.ascontiguousarray(
        inputs["L1_ffn_SU_ug"].reshape(16, 128).T).astype(np.float32)
    wsu_col = np.ascontiguousarray(
        inputs["L1_ffn_Wscale_ug"][:8192].reshape(64, 128).T).astype(np.float32)
    wsg_col = np.ascontiguousarray(
        inputs["L1_ffn_Wscale_ug"][8192:].reshape(64, 128).T).astype(np.float32)
    sudp_col = np.ascontiguousarray(
        inputs["L1_ffn_SU_dp"].reshape(64, 128).T).astype(np.float32)
    wsd_col = np.ascontiguousarray(
        inputs["L1_ffn_Wscale_dp"].reshape(16, 128).T).astype(np.float32)
    maps = []
    for c in range(NCORES):
        b, half = c // 2, c % 2
        tc = slice(half * 1024, half * 1024 + 1024)
        maps.append({
            "ZTb": np.ascontiguousarray(ZT_full[b][:, tc]),
            "xresT": np.ascontiguousarray(x2_full[b][:, tc]),
            "woP": woP, "invn": invn,
            "su_col": su_col, "wsu_col": wsu_col, "wsg_col": wsg_col,
            "sudp_col": sudp_col, "wsd_col": wsd_col,
        })
    return maps


def panelize_l1_w(wupP_full, wgtP_full, wdnP_full):
    up = np.ascontiguousarray(
        wupP_full.reshape(16, 128, 64, 128).transpose(2, 0, 1, 3))
    gt = np.ascontiguousarray(
        wgtP_full.reshape(16, 128, 64, 128).transpose(2, 0, 1, 3))
    dn = np.ascontiguousarray(
        wdnP_full.reshape(64, 128, 16, 128).transpose(2, 0, 1, 3))
    return up, gt, dn



def emit_l1_attn(nc, blk_outer, io, mkctr=None, nch=128, SCAN_PARTS=5, pfx="S"):
    x2Tb, wqT, wkT, wvT = io["x2Tb"], io["l1wqT"], io["l1wkT"], io["l1wvT"]
    wilrT, bilr, w1in, pen = io["wilrT"], io["bilr"], io["w1in"], io["pen"]
    ident, trimask, Zraw, ilrD = io["ident"], io["trimask"], io["Zraw"], io["ilrD"]

    es = ExitStack()
    sb = lambda name, shape, dt: es.enter_context(nc.sbuf_tensor(pfx + name, shape, dt))
    ps = lambda name, shape, dt: es.enter_context(nc.psum_tensor(pfx + name, shape, dt))

    xb_t = sb("xb_t", [128, 8 * S], BF16)
    xc_t = sb("xc_t", [128, 8 * S], BF16)
    xa_t = sb("xa_t", [128, 8 * S], BF16)
    alias = sb("alias", [128, 16 * S], BF16)
    ilr_sb = sb("ilr_sb", [32, S], F32)
    penb = sb("penb", [32, S], F32)
    s_sb = sb("s_sb", [128, S], F32)
    w1i_sb = sb("w1i_sb", [128, 512], F32)
    id_f = sb("id_f", [128, 128], F32)
    id_b = sb("id_b", [128, 128], BF16)
    tri_b = sb("tri_b", [128, 128], BF16)
    bilr_sb = sb("bilr_sb", [16, 1], F32)
    wq_sb = sb("wq_sb", [128, 16 * 128], BF16)
    wk_sb = sb("wk_sb", [128, 16 * 128], BF16)
    wv_sb = sb("wv_sb", [128, 16 * 128], BF16)
    wi_sb = sb("wi_sb", [128, 16 * 16], BF16)
    w1sb = sb("w1sb", [128, 512], BF16)
    z1sb = sb("z1sb", [128, 256], BF16)
    z1n = sb("z1n", [128, 256], BF16)
    xbn = sb("xbn", [128, 256], BF16)
    at_s = sb("at_s", [128, 128], BF16)
    gbuf = sb("gbuf", [128, 256], BF16)
    zbuf = [sb(f"zbuf{i}", [128, 128], BF16) for i in range(2)]

    ps_w1 = ps("ps_w1", [128, 512], F32)
    ps_z1 = ps("ps_z1", [128, 512], F32)
    ps_zb = ps("ps_zb", [128, 512], F32)
    ps_at = ps("ps_at", [128, 512], F32)
    ps_az = ps("ps_az", [128, 512], F32)
    ps_t1 = ps("ps_t1", [128, 512], BF16)
    ps_t2 = ps("ps_t2", [128, 512], BF16)

    XCp = alias
    XBp_OFF = 8 * S

    if True:
        blk = blk_outer
        sems = ExitStack()
        if mkctr is None:
            mkctr = lambda name: Ctr(nc, sems, pfx + name)
        gld = mkctr("gld")
        mm = mkctr("mm")
        ev = mkctr("ev")
        ac = mkctr("ac")
        st = mkctr("st")
        bc = mkctr("bc")

        @blk.gpsimd
        def _(g):
            for k in range(16):
                gld.bump(g.dma_start(out=alias[:, k * S:(k + 1) * S],
                                     in_=x2Tb[k * 128:(k + 1) * 128, :]), 16)
            gld.bump(g.dma_start(out=w1i_sb[:, :], in_=w1in[:, :]), 16)
            gld.bump(g.dma_start(out=id_f[:, :], in_=ident[:, :]), 16)
            gld.bump(g.dma_start(out=id_b[:, :], in_=ident[:, :]), 16)
            gld.bump(g.dma_start(out=tri_b[:, :], in_=trimask[:, :]), 16)
            gld.bump(g.dma_start(out=bilr_sb[:, :], in_=bilr[:, :]), 16)
            pb = bass.AP(tensor=pen, offset=0, ap=[[0, 32], [1, S]])
            gld.bump(g.dma_start(out=penb[:, :], in_=pb), 16)
            for k in range(16):
                gld.bump(g.dma_start(out=wi_sb[:, k * 16:(k + 1) * 16],
                                     in_=wilrT[k * 128:(k + 1) * 128, :]), 16)
        LD0 = gld.n

        for p in range(8):
            @blk.gpsimd
            def _(g, p=p):
                if p > 0:
                    g.wait_ge(mm.h, mm.n)
                for w_sb, w_dr in ((wq_sb, wqT), (wk_sb, wkT), (wv_sb, wvT)):
                    for k in range(16):
                        gld.bump(g.dma_start(
                            out=w_sb[:, k * 128:(k + 1) * 128],
                            in_=w_dr[k * 128:(k + 1) * 128,
                                     p * 128:(p + 1) * 128]), 16)

            for wi, (w_sb, dst) in enumerate(((wq_sb, xc_t), (wk_sb, xb_t),
                                              (wv_sb, xa_t))):
                for nh2 in range(2):
                    @blk.tensor
                    def _(t, p=p, wi=wi, w_sb=w_sb, nh2=nh2):
                        t.wait_ge(gld.h, gld.n)
                        if not (p == 0 and wi == 0 and nh2 == 0):
                            t.wait_ge(ev.h, ev.n)
                        last = None
                        for k in range(16):
                            for n in range(2):
                                c0 = k * S + nh2 * 1024 + n * 512
                                last = t.matmul((ps_z1 if n == 0 else ps_zb)[:, 0:512],
                                                w_sb[:, k * 128:(k + 1) * 128],
                                                alias[:, c0:c0 + 512],
                                                start=(k == 0), stop=(k == 15))
                        mm.bump(last)

                    @blk.vector
                    def _(v, p=p, dst=dst, nh2=nh2):
                        v.wait_ge(mm.h, mm.n)
                        c0 = p * S + nh2 * 1024
                        v.tensor_copy(dst[:, c0:c0 + 512], ps_z1[:, 0:512])
                        ev.bump(v.tensor_copy(dst[:, c0 + 512:c0 + 1024],
                                              ps_zb[:, 0:512]))

        for ch4 in range(4):
            @blk.tensor
            def _(t, ch4=ch4):
                t.wait_ge(ev.h, ev.n)
                t.wait_ge(ac.h, ac.n)
                last = None
                for k in range(16):
                    c0 = k * S + ch4 * 512
                    last = t.matmul(ps_z1[0:16, 0:512],
                                    wi_sb[:, k * 16:(k + 1) * 16],
                                    alias[:, c0:c0 + 512],
                                    start=(k == 0), stop=(k == 15))
                mm.bump(last)

            @blk.scalar
            def _(sc, ch4=ch4):
                sc.wait_ge(mm.h, mm.n)
                ac.bump(sc.activation(ilr_sb[0:16, ch4 * 512:(ch4 + 1) * 512],
                                      ps_z1[0:16, 0:512], AF.Sigmoid,
                                      bias=bilr_sb[0:16, 0:1]))

        @blk.vector
        def _(v):
            v.wait_ge(ac.h, ac.n)
            ev.bump(v.tensor_tensor(ilr_sb[0:16, :], ilr_sb[0:16, :],
                                    penb[0:16, :], ALU.mult))

        @blk.gpsimd
        def _(g):
            g.wait_ge(ev.h, ev.n)
            bc.bump(g.dma_start(out=ilrD[:, :], in_=ilr_sb[0:16, :]), 16)

        for p in range(8):
            @blk.gpsimd
            def _(g, p=p):
                g.wait_ge(bc.h, 16)
                if p == 0:
                    g.wait_ge(ev.h, ev.n)
                else:
                    g.wait_ge(ev.h, ev.n)
                b0 = bass.AP(tensor=ilrD, offset=(2 * p) * S, ap=[[0, 64], [1, S]])
                b1 = bass.AP(tensor=ilrD, offset=(2 * p + 1) * S, ap=[[0, 64], [1, S]])
                bc.bump(g.dma_start(out=s_sb[0:64, :], in_=b0), 16)
                bc.bump(g.dma_start(out=s_sb[64:128, :], in_=b1), 16)

            @blk.vector
            def _(v, p=p):
                v.wait_ge(bc.h, bc.n)
                v.tensor_tensor(XCp[:, p * S:(p + 1) * S], xc_t[:, p * S:(p + 1) * S],
                                s_sb[:, :], ALU.mult)
                ev.bump(v.scalar_tensor_tensor(
                    alias[:, XBp_OFF + p * S: XBp_OFF + (p + 1) * S],
                    xb_t[:, p * S:(p + 1) * S], -1.0, s_sb[:, :],
                    ALU.mult, ALU.mult))

        @blk.vector
        def _(v):
            v.wait_ge(ev.h, ev.n)
            v.memset(gbuf[:, :], 0.0)
            ev.bump(v.memset(z1sb[:, :], 0.0))

        @blk.tensor
        def _(t):
            t.wait_ge(gld.h, LD0)
            t.wait_ge(ev.h, ev.n)
            mm.bump(t.matmul(ps_w1[:, :], id_f[:, :], w1i_sb[:, :],
                             start=True, stop=False, skip_group_check=True))

        def head_coords(p, s):
            return (p % 4) * 32, (p // 4) * 32 + s * 16

        for ch in range(nch):
            @blk.vector
            def _(v, ch=ch):
                v.wait_ge(mm.h, mm.n)
                v.tensor_copy(w1sb[:, :], ps_w1[:, :])
                last = None
                for g4 in range(2):
                    srcg = bass.AP(tensor=alias, offset=XBp_OFF + (g4 * 4) * S + ch * 16,
                                   ap=[[16 * S, 128], [S, 4], [1, 16]])
                    dstp = bass.AP(tensor=gbuf, offset=g4 * 128,
                                   ap=[[256, 128], [32, 4], [1, 16]])
                    last = v.tensor_copy(dstp, srcg)
                ev.bump(last)
            ev_w1 = ev.n

            @blk.tensor
            def _(t, ch=ch, ev_w1=ev_w1):
                t.wait_ge(ev.h, ev_w1)
                for _ in range(8):
                    t.nop()
                last = None
                for s2 in range(2):
                    for _ in range(8):
                        t.nop()
                    for p in range(8):
                        lhsT = w1sb[s2 * 64:(s2 + 1) * 64, p * 64:(p + 1) * 64]
                        t.matmul(ps_z1[s2 * 64:(s2 + 1) * 64, p * 32:p * 32 + 16],
                                 lhsT,
                                 xb_t[s2 * 64:(s2 + 1) * 64, p * S + ch * 16:
                                      p * S + ch * 16 + 16],
                                 start=True, stop=True, skip_group_check=True,
                                 tile_position=(s2 * 64, s2 * 64))
                        last = t.matmul(
                            ps_zb[s2 * 64:(s2 + 1) * 64, p * 32:p * 32 + 16],
                            lhsT,
                            xc_t[s2 * 64:(s2 + 1) * 64, p * S + ch * 16:
                                 p * S + ch * 16 + 16],
                            start=True, stop=True, skip_group_check=True,
                            tile_position=(s2 * 64, s2 * 64))
                mm.bump(last)
            mm_12 = mm.n

            @blk.vector
            def _(v, ch=ch, mm_12=mm_12):
                v.wait_ge(mm.h, mm_12)
                srcz = bass.AP(tensor=ps_z1, offset=0,
                               ap=[[512, 128], [32, 8], [1, 16]])
                srca = bass.AP(tensor=xa_t, offset=ch * 16,
                               ap=[[8 * S, 128], [S, 8], [1, 16]])
                dstz = bass.AP(tensor=z1sb, offset=0,
                               ap=[[256, 128], [32, 8], [1, 16]])
                ev.bump(v.tensor_tensor(dstz, srcz, srca, ALU.subtract))
            ev_z1 = ev.n

            if SCAN_PARTS >= 3:
                pass
            if SCAN_PARTS >= 3:
                @blk.tensor
                def _(t, ch=ch):
                    last = None
                    for s2 in range(2):
                        for _ in range(8):
                            t.nop()
                        for p in range(8):
                            pb_, cb = head_coords(p, s2)
                            last = t.matmul(
                                ps_at[pb_:pb_ + 16, cb:cb + 16],
                                xb_t[s2 * 64:(s2 + 1) * 64, p * S + ch * 16:
                                     p * S + ch * 16 + 16],
                                XCp[s2 * 64:(s2 + 1) * 64, p * S + ch * 16:
                                    p * S + ch * 16 + 16],
                                start=True, stop=True, skip_group_check=True,
                                tile_position=(s2 * 64, pb_))
                    mm.bump(last)
                mm_3 = mm.n

                if SCAN_PARTS >= 3.5:
                    @blk.vector
                    def _(v, ch=ch, mm_3=mm_3):
                        v.wait_ge(mm.h, mm_3)
                        last = None
                        for pb_ in (0, 32, 64, 96):
                            last = v.tensor_tensor(at_s[pb_:pb_ + 16, 0:64],
                                                   ps_at[pb_:pb_ + 16, 0:64],
                                                   tri_b[pb_:pb_ + 16, 0:64], ALU.mult)
                        ev.bump(last)
                ev_mk = ev.n

            if SCAN_PARTS >= 2:
                @blk.tensor
                def _(t, ch=ch, ev_z1=ev_z1):
                    t.wait_ge(ev.h, ev_z1)
                    for _ in range(8):
                        t.nop()
                    t.transpose(ps_t2[:, 0:128], gbuf[:, 0:128], id_b[:, :])
                    t.transpose(ps_t2[:, 128:256], gbuf[:, 128:256], id_b[:, :])
                    t.transpose(ps_t1[:, 0:128], z1sb[:, 0:128], id_b[:, :])
                    last = t.transpose(ps_t1[:, 128:256], z1sb[:, 128:256], id_b[:, :])
                    mm.bump(last)
                mm_tr = mm.n

                @blk.vector
                def _(v, ch=ch, mm_tr=mm_tr):
                    v.wait_ge(mm.h, mm_tr)
                    v.tensor_copy(xbn[:, :], ps_t2[:, 0:256])
                    ev.bump(v.tensor_copy(z1n[:, :], ps_t1[:, 0:256]))
                ev_tr = ev.n

            if SCAN_PARTS >= 4:
                @blk.tensor
                def _(t, ch=ch, ev_tr=ev_tr, ev_mk=ev_mk):
                    t.wait_ge(ev.h, max(ev_tr, ev_mk))
                    for _ in range(8):
                        t.nop()
                    last = None
                    for po in range(4):
                        for _ in range(8):
                            t.nop()
                        for g4 in range(2):
                            p = g4 * 4 + po
                            for s2 in range(2):
                                pb_, cb = head_coords(p, s2)
                                last = t.matmul(
                                    ps_az[s2 * 64:(s2 + 1) * 64, p * 32:p * 32 + 16],
                                z1n[po * 32:po * 32 + 16,
                                    g4 * 128 + s2 * 64:g4 * 128 + (s2 + 1) * 64],
                                at_s[pb_:pb_ + 16, cb:cb + 16],
                                start=True, stop=True, skip_group_check=True,
                                tile_position=(po * 32, s2 * 64))
                    mm.bump(last)
                mm_4 = mm.n

            if SCAN_PARTS >= 5:
                @blk.tensor
                def _(t, ch=ch):
                    last = None
                    for po in range(4):
                        for _ in range(8):
                            t.nop()
                        for g4 in range(2):
                            p = g4 * 4 + po
                            for s2 in range(2):
                                last = t.matmul(
                                    ps_w1[s2 * 64:(s2 + 1) * 64, p * 64:(p + 1) * 64],
                                xbn[po * 32:po * 32 + 16,
                                    g4 * 128 + s2 * 64:g4 * 128 + (s2 + 1) * 64],
                                z1n[po * 32:po * 32 + 16,
                                    g4 * 128 + s2 * 64:g4 * 128 + (s2 + 1) * 64],
                                start=False, stop=False, skip_group_check=True,
                                tile_position=(po * 32, s2 * 64))
                    mm.bump(last)

            mm_pre_evac = mm.n

            @blk.vector
            def _(v, ch=ch, mm_pre_evac=mm_pre_evac):
                v.wait_ge(mm.h, mm_pre_evac)
                if ch >= 2:
                    v.wait_ge(st.h, 16 * (ch - 1))
                srcz = bass.AP(tensor=ps_zb, offset=0,
                               ap=[[512, 128], [32, 8], [1, 16]])
                srca2 = bass.AP(tensor=ps_az, offset=0,
                                ap=[[512, 128], [32, 8], [1, 16]])
                dstz = bass.AP(tensor=zbuf[ch % 2], offset=0,
                               ap=[[128, 128], [16, 8], [1, 16]])
                v.tensor_copy(dstz, srcz)
                if SCAN_PARTS >= 4:
                    ev.bump(v.tensor_tensor(dstz, dstz, srca2, ALU.add))
                else:
                    ev.bump(v.tensor_copy(dstz, srcz))
            @blk.sync
            def _(s, ch=ch):
                s.wait_ge(ev.h, ev.n)
                st.bump(s.dma_start(out=Zraw[:, ch * 128:(ch + 1) * 128],
                                    in_=zbuf[ch % 2][:, :]), 16)

        @blk.sync
        def _(s):
            s.wait_ge(st.h, st.n)

        sems.close()
    es.close()
    return st


def build_l1_attn(nch=128, SCAN_PARTS=5):
    nc = bass.Bass("TRN2", target_bir_lowering=False, debug=False,
                   detect_race_conditions=False)
    io = {
        "x2Tb": nc.dram_tensor("x2Tb", [D, S], F32, kind="ExternalInput"),
        "l1wqT": nc.dram_tensor("wqT", [D, 1024], F32, kind="ExternalInput"),
        "l1wkT": nc.dram_tensor("wkT", [D, 1024], F32, kind="ExternalInput"),
        "l1wvT": nc.dram_tensor("wvT", [D, 1024], F32, kind="ExternalInput"),
        "wilrT": nc.dram_tensor("wilrT", [D, 16], F32, kind="ExternalInput"),
        "bilr": nc.dram_tensor("bilr", [16, 1], F32, kind="ExternalInput"),
        "w1in": nc.dram_tensor("w1in", [128, 512], F32, kind="ExternalInput"),
        "pen": nc.dram_tensor("pen", [1, S], F32, kind="ExternalInput"),
        "ident": nc.dram_tensor("ident", [128, 128], F32, kind="ExternalInput"),
        "trimask": nc.dram_tensor("trimask", [128, 128], F32, kind="ExternalInput"),
        "Zraw": nc.dram_tensor("Zraw", [128, 128 * 128], BF16, kind="ExternalOutput"),
        "ilrD": nc.dram_tensor("ilrD", [16, S], F32),
    }
    with nc.Block() as blk:
        emit_l1_attn(nc, blk, io, nch=nch, SCAN_PARTS=SCAN_PARTS)
    return nc


def host_prep_l1_attn(inputs, x2_full):
    wqT = np.ascontiguousarray(inputs["L1_attn_Wq"].T)
    wkT = np.ascontiguousarray(inputs["L1_attn_Wk"].T)
    wvT = np.ascontiguousarray(inputs["L1_attn_Wv"].T)
    wiT = np.ascontiguousarray(inputs["L1_attn_W_ilr"].T)
    bilr = inputs["L1_attn_b_ilr"].astype(np.float32)
    W1 = inputs["L1_attn_W1"].astype(np.float32)
    pen = (1.0 / (np.arange(S) % 16 + 1.0) / 64.0).astype(np.float32)[None, :]
    ident = np.eye(128, dtype=np.float32)
    tm = np.zeros((128, 128), np.float32)
    for p in range(8):
        for s2 in range(2):
            pb_, cb = (p % 4) * 32, (p // 4) * 32 + s2 * 16
            for d_ in range(16):
                for c_ in range(16):
                    if d_ <= c_:
                        tm[pb_ + d_, cb + c_] = -1.0
    maps = []
    for c in range(NCORES):
        b, half = c // 2, c % 2
        g0 = 16 * half
        cols = slice(g0 * 64, g0 * 64 + 1024)
        w1v = np.zeros((128, 512), np.float32)
        for p in range(8):
            for s2 in range(2):
                w1v[s2 * 64:(s2 + 1) * 64, p * 64:(p + 1) * 64] = W1[g0 + 2 * p + s2]
        maps.append({
            "x2Tb": np.ascontiguousarray(x2_full[b]),
            "wqT": np.ascontiguousarray(wqT[:, cols]),
            "wkT": np.ascontiguousarray(wkT[:, cols]),
            "wvT": np.ascontiguousarray(wvT[:, cols]),
            "wilrT": np.ascontiguousarray(wiT[:, g0:g0 + 16]),
            "bilr": np.ascontiguousarray(bilr[g0:g0 + 16].reshape(16, 1)),
            "w1in": w1v,
            "pen": pen, "ident": ident, "trimask": tm,
        })
    return maps


def zraw_to_zt(zraw):
    z = np.asarray(zraw).reshape(128, 128, 8, 16)
    return np.ascontiguousarray(z.transpose(2, 0, 1, 3).reshape(1024, 2048))





from concourse.bass_core_rust import IndirectOffsetOnAxis

MAGICF = 1597000001
POLL_ITERS = 20000
GCW = 2816
NBAR = 3


def build_fused():
    nc = bass.Bass("TRN2", target_bir_lowering=False, debug=False,
                   detect_race_conditions=False)

    xTb = nc.dram_tensor("xTb", [D, S], F32, kind="ExternalInput")
    wqT = nc.dram_tensor("wqT", [D, 1024], F32, kind="ExternalInput")
    wkT = nc.dram_tensor("wkT", [D, 1024], F32, kind="ExternalInput")
    wvT = nc.dram_tensor("wvT", [D, 1024], F32, kind="ExternalInput")
    triu = nc.dram_tensor("triu", [128, 128], F32, kind="ExternalInput")
    onesc = nc.dram_tensor("onesc", [128, 128], F32, kind="ExternalInput")
    xresT = nc.dram_tensor("xresT", [D, TOK], F32, kind="ExternalInput")
    woP = nc.dram_tensor("woP", [16, 16, 128, 128], F32, kind="ExternalInput")
    invn = nc.dram_tensor("invn", [128, 1], F32, kind="ExternalInput")
    gidxF = nc.dram_tensor("gidxF", [D, GCW], I32, kind="ExternalInput")
    uidxF = nc.dram_tensor("uidxF", [D, GCW], I32, kind="ExternalInput")
    didxF = nc.dram_tensor("didxF", [GCW, D], I32, kind="ExternalInput")
    lutb = nc.dram_tensor("lutb", [128, 16], F32, kind="ExternalInput")
    h128 = nc.dram_tensor("h128", [128, 128], F32, kind="ExternalInput")
    wupR = nc.dram_tensor("wupR", [64, 16, 128, 128], F32, kind="ExternalInput")
    wgtR = nc.dram_tensor("wgtR", [64, 16, 128, 128], F32, kind="ExternalInput")
    wdnTF = nc.dram_tensor("wdnTF", [8192, 1024], F32, kind="ExternalInput")
    zwqT = nc.dram_tensor("zwqT", [D, 1024], F32, kind="ExternalInput")
    zwkT = nc.dram_tensor("zwkT", [D, 1024], F32, kind="ExternalInput")
    zwvT = nc.dram_tensor("zwvT", [D, 1024], F32, kind="ExternalInput")
    wilrT = nc.dram_tensor("wilrT", [D, 16], F32, kind="ExternalInput")
    bilr = nc.dram_tensor("bilr", [16, 1], F32, kind="ExternalInput")
    w1in = nc.dram_tensor("w1in", [128, 512], F32, kind="ExternalInput")
    pen = nc.dram_tensor("pen", [1, S], F32, kind="ExternalInput")
    ident = nc.dram_tensor("ident", [128, 128], F32, kind="ExternalInput")
    trimask = nc.dram_tensor("trimask", [128, 128], F32, kind="ExternalInput")
    l1woP = nc.dram_tensor("l1woP", [16, 16, 128, 128], F32, kind="ExternalInput")
    su_col = nc.dram_tensor("su_col", [128, 16], F32, kind="ExternalInput")
    wsu_col = nc.dram_tensor("wsu_col", [128, 64], F32, kind="ExternalInput")
    wsg_col = nc.dram_tensor("wsg_col", [128, 64], F32, kind="ExternalInput")
    sudp_col = nc.dram_tensor("sudp_col", [128, 64], F32, kind="ExternalInput")
    wsd_col = nc.dram_tensor("wsd_col", [128, 16], F32, kind="ExternalInput")
    ix_g = nc.dram_tensor("ix_g", [128, 16], U32, kind="ExternalInput")
    ix_d = nc.dram_tensor("ix_d", [128, 22], U32, kind="ExternalInput")
    ix_dn = nc.dram_tensor("ix_dn", [128, 256], U32, kind="ExternalInput")
    ix_att = nc.dram_tensor("ix_att", [128, 16], U32, kind="ExternalInput")
    gxi = nc.dram_tensor("gxi", [128, 16], U32, kind="ExternalInput")
    ix_flg = nc.dram_tensor("ix_flg", [128, NBAR], U32, kind="ExternalInput")
    flgv = nc.dram_tensor("flgv", [128, NBAR], I32, kind="ExternalInput")

    attnT = nc.dram_tensor("attnT", [1024, S], BF16)
    attnTb = nc.dram_tensor("attnTb", [D, TOK], BF16)
    x2T = nc.dram_tensor("x2T", [D, TOK], F32)
    x1T = nc.dram_tensor("x1T", [D, TOK], F32)
    rowsD = nc.dram_tensor("rowsD", [2, 512], F32)
    x2Tb = nc.dram_tensor("x2Tb", [D, S], F32)
    Zraw = nc.dram_tensor("Zraw", [128, 128 * 128], BF16)
    ilrD = nc.dram_tensor("ilrD", [16, S], F32)
    ZTb = nc.dram_tensor("ZTb", [D, TOK], BF16)
    x3T = nc.dram_tensor("x3T", [D, TOK], F32)
    zDf = nc.dram_tensor("zDf", [8192, 1024], BF16)
    dumD = nc.dram_tensor("dumD", [2, 16], F32)
    x4T = nc.dram_tensor("x4T", [D, TOK], F32, kind="ExternalOutput")

    pool_g = nc.dram_tensor("pool_g", [4096, GCW], BF16, addr_space="Shared")
    pool_u = nc.dram_tensor("pool_u", [4096, GCW], BF16, addr_space="Shared")
    pool_d = nc.dram_tensor("pool_d", [5504, 2048], BF16, addr_space="Shared")
    pool_dn = nc.dram_tensor("pool_dn", [65536, 256], BF16, addr_space="Shared")
    pool_att = nc.dram_tensor("pool_att", [4096, 1024], BF16, addr_space="Shared")
    pool_x2 = nc.dram_tensor("pool_x2", [4096, 1024], F32, addr_space="Shared")
    pool_z = nc.dram_tensor("pool_z", [4096, 1024], BF16, addr_space="Shared")
    pool_flg = nc.dram_tensor("pool_flg", [64, 1], I32, addr_space="Shared")

    def wgP_pool_ap(pool):
        def f(i):
            if i <= 21:
                off = i * 128
            else:
                off = 2048 * GCW + (i - 21) * 128
            return bass.AP(tensor=pool, offset=off,
                           ap=[[GCW, 128], [128 * GCW, 16], [1, 128]])
        return f

    def wdP_pool_ap(oc):
        return bass.AP(tensor=pool_d, offset=oc * 128,
                       ap=[[2048, 128], [128 * 2048, NIO], [1, 128]])

    def wdnP_pool_ap(oc, ioh):
        par, lc = oc // 8, oc % 8
        chunk, cb = lc // 2, (lc % 2) * 128
        off = ioh * 32 * 128 * 2048 + (par * 4 + chunk) * 256 + cb
        return bass.AP(tensor=pool_dn, offset=off,
                       ap=[[2048, 128], [128 * 2048, 32], [1, 128]])

    es0 = ExitStack()
    sbg = lambda name, shape, dt: es0.enter_context(nc.sbuf_tensor(name, shape, dt))
    ixg_sb = sbg("ixg_sb", [128, 16], U32)
    ixd_sb = sbg("ixd_sb", [128, 22], U32)
    ixdn_sb = sbg("ixdn_sb", [128, 256], U32)
    ixatt_sb = sbg("ixatt_sb", [128, 16], U32)
    gx_sb = sbg("gx_sb", [128, 16], U32)
    ixflg_sb = sbg("ixflg_sb", [128, NBAR], U32)
    flgv_sb = sbg("flgv_sb", [128, NBAR], I32)
    flg4 = sbg("flg4", [4, 1], I32)
    dum = sbg("dum", [2, 16], F32)
    dum2 = sbg("dum2", [2, 16], F32)

    with nc.Block() as blk:
        sems0 = ExitStack()
        phc = sems0.enter_context(nc.semaphore("phc"))
        phs = sems0.enter_context(nc.semaphore("phs"))
        phg = sems0.enter_context(nc.semaphore("phg"))
        gpb = Ctr(nc, sems0, "gpb")
        spb = Ctr(nc, sems0, "spb")
        stb = Ctr(nc, sems0, "stb")
        plf = [sems0.enter_context(nc.semaphore(f"plf{b}")) for b in range(NBAR)]

        fence_n = [0]

        def fence():
            fence_n[0] += 1
            k = fence_n[0]

            @blk.tensor
            def _(t, k=k):
                t.nop().then_inc(phc, 1)
                t.wait_ge(phc, 3 * k)
                t.wait_ge(phs, 16 * k)
                t.wait_ge(phg, 16 * k)

            @blk.vector
            def _(v, k=k):
                v.tensor_copy(dum2[0:1, 0:8], dum[0:1, 0:8]).then_inc(phc, 1)
                v.wait_ge(phc, 3 * k)
                v.wait_ge(phs, 16 * k)
                v.wait_ge(phg, 16 * k)

            @blk.scalar
            def _(sc, k=k):
                sc.activation(dum2[0:1, 8:16], dum[0:1, 8:16],
                              AF.Copy).then_inc(phc, 1)
                sc.wait_ge(phc, 3 * k)
                sc.wait_ge(phs, 16 * k)
                sc.wait_ge(phg, 16 * k)

            @blk.sync
            def _(s, k=k):
                s.dma_start(out=dum[0:1, 0:8], in_=dumD[0:1, 0:8]).then_inc(phs, 16)
                s.wait_ge(phc, 3 * k)
                s.wait_ge(phs, 16 * k)
                s.wait_ge(phg, 16 * k)

            @blk.gpsimd
            def _(g, k=k):
                g.dma_start(out=dum[1:2, 0:8], in_=dumD[1:2, 0:8],
                            allow_slow_non_contiguous=True).then_inc(phg, 16)
                g.wait_ge(phc, 3 * k)
                g.wait_ge(phs, 16 * k)
                g.wait_ge(phg, 16 * k)

        @blk.gpsimd
        def _(g):
            for dst, srcd in ((ixg_sb, ix_g), (ixd_sb, ix_d),
                              (ixdn_sb, ix_dn), (ixatt_sb, ix_att), (gx_sb, gxi),
                              (ixflg_sb, ix_flg), (flgv_sb, flgv)):
                gpb.bump(g.dma_start(out=dst[:, :], in_=srcd[:, :]), 16)
        IX_LD = gpb.n

        esW = ExitStack()
        sbW = lambda name, shape, dt: esW.enter_context(nc.sbuf_tensor(name, shape, dt))
        psW = lambda name, shape, dt: esW.enter_context(nc.psum_tensor(name, shape, dt))
        idxb = [sbW(f"Widxb{i}", [128, GCW], BF16) for i in range(2)]
        accb = [sbW(f"Waccb{i}", [128, GCW], BF16) for i in range(2)]
        maskb = sbW("Wmaskb", [128, GCW], BF16)
        lut_sb = sbW("Wlut_sb", [128, 16], F32)
        h128b = sbW("Wh128b", [128, 128], BF16)
        z_sb = sbW("Wz_sb", [128, 64 * 512], BF16)
        z2_sb = sbW("Wz2_sb", [128, 64 * 512], BF16)
        ps_h = [psW(f"Wps_h{i}", [128, 512], F32) for i in range(2)]

        semsW = ExitStack()
        gld = Ctr(nc, semsW, "Wgld")
        mm = Ctr(nc, semsW, "Wmm")
        ev = Ctr(nc, semsW, "Wev")
        gsc = Ctr(nc, semsW, "Wgsc")
        bd = Ctr(nc, semsW, "Wbd")

        @blk.gpsimd
        def _(g):
            g.wait_ge(gpb.h, IX_LD)
            gld.bump(g.dma_start(out=lut_sb[:, :], in_=lutb[:, :]), 16)
            gld.bump(g.dma_start(out=h128b[:, :], in_=h128[:, :]), 16)

        ghost_jobs = []
        for (idx_dr, pool, ixsb, ntile, cols) in (
                (gidxF, pool_g, ixg_sb, 16, GCW),
                (uidxF, pool_u, ixg_sb, 16, GCW),
                (didxF, pool_d, ixd_sb, 22, 2048)):
            for kt in range(ntile):
                ghost_jobs.append((idx_dr, pool, ixsb, kt, cols))

        for ji, (idx_dr, pool, ixsb, kt, cols) in enumerate(ghost_jobs):
            @blk.gpsimd
            def _(g, ji=ji, idx_dr=idx_dr, kt=kt, cols=cols):
                if ji >= 2:
                    g.wait_ge(ev.h, ji - 1)
                gld.bump(g.dma_start(
                    out=idxb[ji % 2][:, 0:cols],
                    in_=idx_dr[kt * 128:(kt + 1) * 128, :]), 16)

            @blk.vector
            def _(v, ji=ji, cols=cols):
                v.wait_ge(gld.h, 32 + 16 * (ji + 1))
                if ji >= 2:
                    v.wait_ge(gsc.h, 16 * (ji - 1))
                ib = idxb[ji % 2][:, 0:cols]
                ab = accb[ji % 2][:, 0:cols]
                mk = maskb[:, 0:cols]
                v.tensor_scalar(ab, ib, 0.0, lut_sb[:, 0:1], ALU.is_equal, ALU.mult)
                last = None
                for j in range(1, 16):
                    v.tensor_scalar(mk, ib, float(j), None, ALU.is_equal)
                    last = v.scalar_tensor_tensor(ab, mk, lut_sb[:, j:j + 1], ab,
                                                  ALU.mult, ALU.add)
                ev.bump(last)

            @blk.gpsimd
            def _(g, ji=ji, pool=pool, ixsb=ixsb, kt=kt, cols=cols):
                g.wait_ge(ev.h, ji + 1)
                gsc.bump(g.indirect_dma_start(
                    pool[:, :],
                    IndirectOffsetOnAxis(ap=ixsb[:, kt:kt + 1], axis=0),
                    accb[ji % 2][:, 0:cols], None), 16)

        def emit_wprime_f(w_in, c0, cols, ntile, nbf_stages, pool, ixsb, ixc0):
            for kt in range(ntile):
                @blk.gpsimd
                def _(g, kt=kt, cols=cols, w_in=w_in, c0=c0, evn=ev.n):
                    g.wait_ge(ev.h, evn)
                    g.wait_ge(mm.h, mm.n)
                    gld.bump(g.dma_start(
                        out=idxb[kt % 2][:, 0:cols],
                        in_=w_in[kt * 128:(kt + 1) * 128, c0:c0 + cols]), 16)

                @blk.tensor
                def _(t, kt=kt, cols=cols):
                    t.wait_ge(gld.h, gld.n)
                    t.wait_ge(ev.h, ev.n)
                    nch = (cols + 511) // 512
                    last = None
                    for n in range(nch):
                        lo, hi = n * 512, min((n + 1) * 512, cols)
                        last = t.matmul(ps_h[n % 2][:, 0:hi - lo], h128b[:, :],
                                        idxb[kt % 2][:, lo:hi], start=True, stop=True)
                    mm.bump(last)

                @blk.vector
                def _(v, kt=kt, cols=cols, gscn=gsc.n):
                    if kt == 0:
                        v.wait_ge(gsc.h, gscn)
                    v.wait_ge(mm.h, mm.n)
                    nch = (cols + 511) // 512
                    lastv = None
                    for n in range(nch):
                        lo, hi = n * 512, min((n + 1) * 512, cols)
                        lastv = v.tensor_copy(z_sb[:, lo:hi], ps_h[n % 2][:, 0:hi - lo])
                    ev.bump(lastv)

                @blk.sync
                def _(s, kt=kt, cols=cols, bdn=bd.n):
                    if kt == 0:
                        s.wait_ge(bd.h, bdn)
                    s.wait_ge(ev.h, ev.n)
                    bd.bump(s.dma_start(out=zDf[kt * 128:(kt + 1) * 128, 0:cols],
                                        in_=z_sb[:, 0:cols]), 16)

            @blk.sync
            def _(s, ntile=ntile, cols=cols, gscn=gsc.n):
                s.wait_ge(gsc.h, gscn)
                s.wait_ge(bd.h, bd.n)
                i = s.dma_start(
                    out=z2_sb[:, 0:ntile * cols],
                    in_=zDf[0:ntile * 128, 0:cols].rearrange("(j p) o -> p j o", p=128))
                bd.bump(i, 16)

            @blk.vector
            def _(v, ntile=ntile, cols=cols, nbf=nbf_stages):
                v.wait_ge(bd.h, bd.n)
                hstep = 1
                last = None
                for s_i in range(nbf):
                    ngrp = ntile // (2 * hstep)
                    apA = bass.AP(tensor=z2_sb, offset=0,
                                  ap=[[z2_sb.shape[1], 128],
                                      [2 * hstep * cols, ngrp], [1, hstep * cols]])
                    apB = bass.AP(tensor=z2_sb, offset=hstep * cols,
                                  ap=[[z2_sb.shape[1], 128],
                                      [2 * hstep * cols, ngrp], [1, hstep * cols]])
                    apA2 = bass.AP(tensor=z_sb, offset=0,
                                   ap=[[z_sb.shape[1], 128],
                                       [2 * hstep * cols, ngrp], [1, hstep * cols]])
                    apB2 = bass.AP(tensor=z_sb, offset=hstep * cols,
                                   ap=[[z_sb.shape[1], 128],
                                       [2 * hstep * cols, ngrp], [1, hstep * cols]])
                    srcA, srcB = (apA, apB) if s_i % 2 == 0 else (apA2, apB2)
                    dstA, dstB = (apA2, apB2) if s_i % 2 == 0 else (apA, apB)
                    v.tensor_tensor(dstA, srcA, srcB, ALU.add)
                    last = v.tensor_tensor(dstB, srcA, srcB, ALU.subtract)
                    hstep *= 2
                ev.bump(last)

            final_buf = z2_sb if (nbf_stages % 2 == 0) else z_sb

            @blk.gpsimd
            def _(g, ntile=ntile, cols=cols, pool=pool, ixsb=ixsb, ixc0=ixc0,
                  final_buf=final_buf):
                g.wait_ge(ev.h, ev.n)
                rw = pool.shape[1]
                nsub = cols // rw
                for j in range(ntile):
                    for sub in range(nsub):
                        ic = ixc0 + sub * ntile + j
                        gsc.bump(g.indirect_dma_start(
                            pool[:, :],
                            IndirectOffsetOnAxis(ap=ixsb[:, ic:ic + 1], axis=0),
                            final_buf[:, j * cols + sub * rw:
                                      j * cols + (sub + 1) * rw], None), 16)

        for ch in range(2):
            emit_wprime_f(wdnTF, ch * 512, 512, 64, 6, pool_dn, ixdn_sb, ch * 128)

        @blk.gpsimd
        def _(g):
            g.wait_ge(gsc.h, gsc.n)

        semsW.close()
        esW.close()
        fence()

        ioA = {"xTb": xTb, "wqT": wqT, "wkT": wkT, "wvT": wvT,
               "triu": triu, "onesc": onesc, "attnT": attnT}
        emit_l0_attn(nc, blk, ioA)
        fence()

        esG1 = ExitStack()
        astg = [esG1.enter_context(nc.sbuf_tensor(f"G1a{i}", [128, 2048], BF16))
                for i in range(2)]
        gst = [esG1.enter_context(nc.sbuf_tensor(f"G1g{i}", [128, 1024], BF16))
               for i in range(2)]

        s0, g0 = spb.n, gpb.n

        @blk.sync
        def _(s):
            for t in range(8):
                if t >= 2:
                    s.wait_ge(gpb.h, g0 + 32 * (t - 1))
                spb.bump(s.dma_start(out=astg[t % 2][:, :],
                                     in_=attnT[t * 128:(t + 1) * 128, :]), 16)

        @blk.gpsimd
        def _(g):
            for t in range(8):
                g.wait_ge(spb.h, s0 + 16 * (t + 1))
                for th2 in range(2):
                    gpb.bump(g.indirect_dma_start(
                        pool_att[:, :],
                        IndirectOffsetOnAxis(
                            ap=ixatt_sb[:, 2 * t + th2:2 * t + th2 + 1], axis=0),
                        astg[t % 2][:, th2 * 1024:(th2 + 1) * 1024], None), 16)
            g.wait_ge(gpb.h, gpb.n)
            gpb.bump(g.indirect_dma_start(
                pool_flg[:, :],
                IndirectOffsetOnAxis(ap=ixflg_sb[0:2, 0:1], axis=0),
                flgv_sb[0:2, 0:1], None), 16)
            g.wait_ge(gpb.h, gpb.n)

        emit_poll(nc, blk, plf[0], pool_flg, flg4, 0)

        gi0, st0 = gpb.n, stb.n

        @blk.gpsimd
        def _(g):
            for k in range(16):
                if k >= 2:
                    g.wait_ge(stb.h, st0 + 16 * (k - 1))
                gpb.bump(g.indirect_dma_start(
                    gst[k % 2][:, :], None,
                    pool_att[:, :],
                    IndirectOffsetOnAxis(ap=gx_sb[:, k:k + 1], axis=0)), 16)

        @blk.scalar
        def _(sc):
            for k in range(16):
                sc.wait_ge(gpb.h, gi0 + 16 * (k + 1))
                stb.bump(sc.dma_start(out=attnTb[k * 128:(k + 1) * 128, :],
                                      in_=gst[k % 2][:, :]), 16)
            sc.wait_ge(stb.h, stb.n)

        esG1.close()
        fence()

        ioF = {"attnTb": attnTb, "xresT": xresT, "invn": invn,
               "x2T": x2T, "x1T": x1T, "rowsD": rowsD,
               "woP_ap": lambda i: woP[i].rearrange("k p c -> p k c"),
               "wgP_ap": wgP_pool_ap(pool_g),
               "wuP_ap": wgP_pool_ap(pool_u),
               "wdP_ap": wdP_pool_ap}
        emit_l0_ffn(nc, blk, ioF)
        fence()

        esG2 = ExitStack()
        xstg = [esG2.enter_context(nc.sbuf_tensor(f"G2a{i}", [128, 1024], F32))
                for i in range(2)]
        x2g = [esG2.enter_context(nc.sbuf_tensor(f"G2b{i}", [128, 2048], F32))
               for i in range(2)]

        s0, g0 = spb.n, gpb.n

        @blk.sync
        def _(s):
            for k in range(16):
                if k >= 2:
                    s.wait_ge(gpb.h, g0 + 16 * (k - 1))
                spb.bump(s.dma_start(out=xstg[k % 2][:, :],
                                     in_=x2T[k * 128:(k + 1) * 128, :]), 16)

        @blk.gpsimd
        def _(g):
            for k in range(16):
                g.wait_ge(spb.h, s0 + 16 * (k + 1))
                gpb.bump(g.indirect_dma_start(
                    pool_x2[:, :],
                    IndirectOffsetOnAxis(ap=gx_sb[:, k:k + 1], axis=0),
                    xstg[k % 2][:, :], None), 16)
            g.wait_ge(gpb.h, gpb.n)
            gpb.bump(g.indirect_dma_start(
                pool_flg[:, :],
                IndirectOffsetOnAxis(ap=ixflg_sb[0:2, 1:2], axis=0),
                flgv_sb[0:2, 1:2], None), 16)
            g.wait_ge(gpb.h, gpb.n)

        emit_poll(nc, blk, plf[1], pool_flg, flg4, 1)

        @blk.gpsimd
        def _(g):
            gpb.bump(g.dma_start(out=flg4[:, :], in_=pool_flg[8:12, 0:1],
                                 allow_slow_non_contiguous=True), 16)
        PD1 = gpb.n

        si0, st0 = spb.n, stb.n

        @blk.sync
        def _(s):
            s.wait_ge(gpb.h, PD1)
            for k in range(16):
                if k >= 2:
                    s.wait_ge(stb.h, st0 + 16 * (k - 1))
                for t2 in range(2):
                    srca = bass.AP(tensor=pool_x2,
                                   offset=(t2 * 2048 + k * 128) * 1024,
                                   ap=[[1024, 128], [1, 1024]])
                    spb.bump(s.dma_start(
                        out=x2g[k % 2][:, t2 * 1024:(t2 + 1) * 1024], in_=srca), 16)

        @blk.scalar
        def _(sc):
            for k in range(16):
                sc.wait_ge(spb.h, si0 + 32 * (k + 1))
                stb.bump(sc.dma_start(out=x2Tb[k * 128:(k + 1) * 128, :],
                                      in_=x2g[k % 2][:, :]), 16)
            sc.wait_ge(stb.h, stb.n)

        esG2.close()
        fence()

        ioS = {"x2Tb": x2Tb, "l1wqT": zwqT, "l1wkT": zwkT, "l1wvT": zwvT,
               "wilrT": wilrT, "bilr": bilr, "w1in": w1in, "pen": pen,
               "ident": ident, "trimask": trimask, "Zraw": Zraw, "ilrD": ilrD}
        emit_l1_attn(nc, blk, ioS)
        fence()

        esG3 = ExitStack()
        zstg = [esG3.enter_context(nc.sbuf_tensor(f"G3a{i}", [128, 2048], BF16))
                for i in range(2)]
        zgt = [esG3.enter_context(nc.sbuf_tensor(f"G3g{i}", [128, 1024], BF16))
               for i in range(2)]

        s0, g0 = spb.n, gpb.n

        @blk.sync
        def _(s):
            for p in range(8):
                if p >= 2:
                    s.wait_ge(gpb.h, g0 + 32 * (p - 1))
                for chh in range(2):
                    srca = bass.AP(tensor=Zraw, offset=chh * 64 * 128 + p * 16,
                                   ap=[[16384, 128], [128, 64], [1, 16]])
                    spb.bump(s.dma_start(
                        out=zstg[p % 2][:, chh * 1024:(chh + 1) * 1024],
                        in_=srca), 16)

        @blk.gpsimd
        def _(g):
            for p in range(8):
                g.wait_ge(spb.h, s0 + 32 * (p + 1))
                for th2 in range(2):
                    gpb.bump(g.indirect_dma_start(
                        pool_z[:, :],
                        IndirectOffsetOnAxis(
                            ap=ixatt_sb[:, 2 * p + th2:2 * p + th2 + 1], axis=0),
                        zstg[p % 2][:, th2 * 1024:(th2 + 1) * 1024], None), 16)
            g.wait_ge(gpb.h, gpb.n)
            gpb.bump(g.indirect_dma_start(
                pool_flg[:, :],
                IndirectOffsetOnAxis(ap=ixflg_sb[0:2, 2:3], axis=0),
                flgv_sb[0:2, 2:3], None), 16)
            g.wait_ge(gpb.h, gpb.n)

        emit_poll(nc, blk, plf[2], pool_flg, flg4, 2)

        gi0, st0 = gpb.n, stb.n

        @blk.gpsimd
        def _(g):
            for k in range(16):
                if k >= 2:
                    g.wait_ge(stb.h, st0 + 16 * (k - 1))
                gpb.bump(g.indirect_dma_start(
                    zgt[k % 2][:, :], None,
                    pool_z[:, :],
                    IndirectOffsetOnAxis(ap=gx_sb[:, k:k + 1], axis=0)), 16)

        @blk.scalar
        def _(sc):
            for k in range(16):
                sc.wait_ge(gpb.h, gi0 + 16 * (k + 1))
                stb.bump(sc.dma_start(out=ZTb[k * 128:(k + 1) * 128, :],
                                      in_=zgt[k % 2][:, :]), 16)
            sc.wait_ge(stb.h, stb.n)

        esG3.close()
        fence()

        ioL = {"ZTb": ZTb, "xresT": x2T, "invn": invn,
               "su_col": su_col, "wsu_col": wsu_col, "wsg_col": wsg_col,
               "sudp_col": sudp_col, "wsd_col": wsd_col,
               "x4T": x4T, "x3T": x3T, "rowsD": rowsD, "h128": h128,
               "l1woP_ap": lambda i: l1woP[i].rearrange("k p c -> p k c"),
               "wupP_ap": lambda i: wupR[i].rearrange("k p c -> p k c"),
               "wgtP_ap": lambda i: wgtR[i].rearrange("k p c -> p k c"),
               "wdnP_ap": wdnP_pool_ap}
        emit_l1_ffn(nc, blk, ioL)

        sems0.close()
    es0.close()
    return nc


def emit_poll(nc, blk, sem, pool_flg, flg4, bar):
    @blk.gpsimd
    def _(g):
        want = MAGICF + bar
        cnt = nc.alloc_register(mybir.EngineType.Pool, f"pl{bar}_cnt")
        iters = nc.alloc_register(mybir.EngineType.Pool, f"pl{bar}_it")
        wm = nc.alloc_register(mybir.EngineType.Pool, f"pl{bar}_wm")
        t1 = nc.alloc_register(mybir.EngineType.Pool, f"pl{bar}_t1")
        t2 = nc.alloc_register(mybir.EngineType.Pool, f"pl{bar}_t2")
        c = nc.alloc_register(mybir.EngineType.Pool, f"pl{bar}_c")
        g.reg_mov(cnt, 0)
        g.reg_mov(iters, 0)
        g.reg_mov(wm, 0)
        g.reg_mov(c, 1)
        with g.While(c):
            g.reg_add(iters, iters, 1)
            g.reg_add(wm, wm, 16)
            g.dma_start(out=flg4[:, :], in_=pool_flg[bar * 8:bar * 8 + 4, 0:1],
                        allow_slow_non_contiguous=True).then_inc(sem, 16)
            g.wait_ge(sem, wm)
            g.reg_mov(cnt, 0)
            for r in range(4):
                g.load(t1, flg4[r:r + 1, 0:1])
                g.reg_alu(t1, t1, want, ALU.is_equal)
                g.reg_alu(cnt, cnt, t1, ALU.add)
            g.reg_alu(t1, cnt, 4, ALU.is_lt)
            g.reg_alu(t2, iters, POLL_ITERS, ALU.is_lt)
            g.reg_alu(c, t1, t2, ALU.bitwise_and)




def host_prep_fused(inputs):
    mA = host_prep_l0_attn(inputs)
    x = inputs["L0_attn_hidden_states"]
    woT = np.ascontiguousarray(inputs["L0_attn_w_o"].T)
    woP = np.ascontiguousarray(woT.reshape(16, 128, 16, 128).transpose(2, 0, 1, 3))
    invn = np.full((128, 1), 1.0 / 2048.0, np.float32)
    gT = np.ascontiguousarray(inputs["L0_ffn_gate_idx"].T)
    uT = np.ascontiguousarray(inputs["L0_ffn_up_idx"].T)
    dTc = np.ascontiguousarray(inputs["L0_ffn_down_idx"].T)
    upT = np.ascontiguousarray(inputs["L1_ffn_W_up"].T)
    gtT = np.ascontiguousarray(inputs["L1_ffn_W_gate"].T)
    dnT = np.ascontiguousarray(inputs["L1_ffn_W_down"].T)
    lut = np.ascontiguousarray(
        np.broadcast_to(inputs["L0_ffn_lut"][None, :], (128, 16))).astype(np.float32)
    H = np.array([[1.0]], np.float32)
    while H.shape[0] < 128:
        H = np.block([[H, H], [H, -H]])
    H = np.ascontiguousarray(H)
    zwqT = np.ascontiguousarray(inputs["L1_attn_Wq"].T)
    zwkT = np.ascontiguousarray(inputs["L1_attn_Wk"].T)
    zwvT = np.ascontiguousarray(inputs["L1_attn_Wv"].T)
    wiT = np.ascontiguousarray(inputs["L1_attn_W_ilr"].T)
    bilr_np = inputs["L1_attn_b_ilr"].astype(np.float32)
    W1 = inputs["L1_attn_W1"].astype(np.float32)
    pen_np = (1.0 / (np.arange(S) % 16 + 1.0) / 64.0).astype(np.float32)[None, :]
    ident = np.eye(128, dtype=np.float32)
    tm = np.zeros((128, 128), np.float32)
    for p in range(8):
        for s2 in range(2):
            pb_, cb = (p % 4) * 32, (p // 4) * 32 + s2 * 16
            for d_ in range(16):
                for c_ in range(16):
                    if d_ <= c_:
                        tm[pb_ + d_, cb + c_] = -1.0
    l1woT = np.ascontiguousarray(inputs["L1_attn_Wo"].T)
    l1woP = np.ascontiguousarray(
        l1woT.reshape(16, 128, 16, 128).transpose(2, 0, 1, 3))
    su_col = np.ascontiguousarray(
        inputs["L1_ffn_SU_ug"].reshape(16, 128).T).astype(np.float32)
    wsu_col = np.ascontiguousarray(
        inputs["L1_ffn_Wscale_ug"][:8192].reshape(64, 128).T).astype(np.float32)
    wsg_col = np.ascontiguousarray(
        inputs["L1_ffn_Wscale_ug"][8192:].reshape(64, 128).T).astype(np.float32)
    sudp_col = np.ascontiguousarray(
        inputs["L1_ffn_SU_dp"].reshape(64, 128).T).astype(np.float32)
    wsd_col = np.ascontiguousarray(
        inputs["L1_ffn_Wscale_dp"].reshape(16, 128).T).astype(np.float32)

    wupR_np = np.ascontiguousarray(
        upT.reshape(16, 128, 64, 128).transpose(2, 0, 1, 3))
    wgtR_np = np.ascontiguousarray(
        gtT.reshape(16, 128, 64, 128).transpose(2, 0, 1, 3))
    pv = np.arange(128, dtype=np.uint32)
    maps = []
    for c in range(NCORES):
        b, par = c // 2, c % 2
        tc = slice(par * 1024, par * 1024 + 1024)
        gsl = slice(0, GCW) if par == 0 else slice(5504 - GCW, 5504)
        g0 = 16 * par
        zcols = slice(g0 * 64, g0 * 64 + 1024)
        w1v = np.zeros((128, 512), np.float32)
        for p in range(8):
            for s2 in range(2):
                w1v[s2 * 64:(s2 + 1) * 64, p * 64:(p + 1) * 64] = W1[g0 + 2 * p + s2]
        ixg = (par * 2048 + np.arange(16, dtype=np.uint32)[None, :] * 128
               + pv[:, None]).astype(np.uint32)
        ixd = (par * (5504 - GCW) + np.arange(22, dtype=np.uint32)[None, :] * 128
               + pv[:, None]).astype(np.uint32)
        ixdn = np.zeros((128, 256), np.uint32)
        for ch in range(4):
            for j in range(64):
                ixdn[:, ch * 64 + j] = (j * 128 + pv) * 8 + par * 4 + ch
        ixatt = np.zeros((128, 16), np.uint32)
        for t in range(8):
            for th2 in range(2):
                ixatt[:, 2 * t + th2] = th2 * 2048 + par * 1024 + t * 128 + pv
        gx = (par * 2048 + np.arange(16, dtype=np.uint32)[None, :] * 128
              + pv[:, None]).astype(np.uint32)
        ixflg = np.zeros((128, NBAR), np.uint32)
        for bb in range(NBAR):
            ixflg[0:2, bb] = bb * 8 + par * 2 + np.arange(2)
        flgv_np = np.zeros((128, NBAR), np.int32)
        for bb in range(NBAR):
            flgv_np[:, bb] = MAGICF + bb

        m = dict(mA[c])
        m.update({
            "xresT": np.ascontiguousarray(x[b].T[:, tc]),
            "woP": woP, "invn": invn,
            "gidxF": np.ascontiguousarray(gT[:, gsl]),
            "uidxF": np.ascontiguousarray(uT[:, gsl]),
            "didxF": np.ascontiguousarray(dTc[gsl, :]),
            "lutb": lut, "h128": H,
            "wupR": wupR_np, "wgtR": wgtR_np,
            "wdnTF": np.ascontiguousarray(dnT[:, par * 1024:(par + 1) * 1024]),
            "zwqT": np.ascontiguousarray(zwqT[:, zcols]),
            "zwkT": np.ascontiguousarray(zwkT[:, zcols]),
            "zwvT": np.ascontiguousarray(zwvT[:, zcols]),
            "wilrT": np.ascontiguousarray(wiT[:, g0:g0 + 16]),
            "bilr": np.ascontiguousarray(bilr_np[g0:g0 + 16].reshape(16, 1)),
            "w1in": w1v, "pen": pen_np, "ident": ident, "trimask": tm,
            "l1woP": l1woP,
            "su_col": su_col, "wsu_col": wsu_col, "wsg_col": wsg_col,
            "sudp_col": sudp_col, "wsd_col": wsd_col,
            "ix_g": ixg, "ix_d": ixd, "ix_dn": ixdn,
            "ix_att": ixatt, "gxi": gx, "ix_flg": ixflg, "flgv": flgv_np,
        })
        maps.append(m)
    return maps



_NC_CACHE = {}


def _get_nc(name, builder):
    if name not in _NC_CACHE:
        _NC_CACHE[name] = builder()
    return _NC_CACHE[name]


def kernel_unfused(**inputs):
    inputs = {k: np.asarray(v) for k, v in inputs.items()}
    cores = list(range(NCORES))

    nc = _get_nc("wprep", build_wprep)
    res = run_bass_kernel_spmd(nc, host_prep_wprep(inputs), core_ids=cores)
    WgT = np.concatenate([np.asarray(r["wg_o"]) for r in res.results], axis=1)
    WuT = np.concatenate([np.asarray(r["wu_o"]) for r in res.results], axis=1)
    WdT = np.concatenate([np.asarray(r["wd_o"]) for r in res.results], axis=1)
    upP = np.concatenate([np.asarray(r["wupP"]) for r in res.results], axis=1)
    gtP = np.concatenate([np.asarray(r["wgtP"]) for r in res.results], axis=1)
    dnP = np.concatenate([np.asarray(r["wdnP"]) for r in res.results], axis=1)

    nc = _get_nc("l0a", build_l0_attn)
    res = run_bass_kernel_spmd(nc, host_prep_l0_attn(inputs), core_ids=cores)
    attnT_full = [np.concatenate([np.asarray(res.results[2 * b]["attnT"]),
                                  np.asarray(res.results[2 * b + 1]["attnT"])], axis=0)
                  for b in range(4)]

    nc = _get_nc("l0f", build_l0_ffn)
    res = run_bass_kernel_spmd(
        nc, host_prep_l0_ffn(inputs, attnT_full, WgT, WuT, WdT), core_ids=cores)
    x2_full = [np.concatenate([np.asarray(res.results[2 * b]["x2T"]),
                               np.asarray(res.results[2 * b + 1]["x2T"])], axis=1)
               for b in range(4)]

    nc = _get_nc("l1a", build_l1_attn)
    res = run_bass_kernel_spmd(nc, host_prep_l1_attn(inputs, x2_full), core_ids=cores)
    ZT_full = [np.concatenate([zraw_to_zt(np.asarray(res.results[2 * b]["Zraw"])),
                               zraw_to_zt(np.asarray(res.results[2 * b + 1]["Zraw"]))],
                              axis=0)
               for b in range(4)]

    nc = _get_nc("l1f", build_l1_ffn)
    maps = host_prep_l1_ffn(inputs, ZT_full, x2_full)
    up, gt, dn = panelize_l1_w(upP, gtP, dnP)
    for m in maps:
        m["wupP"] = up
        m["wgtP"] = gt
        m["wdnP"] = dn
    res = run_bass_kernel_spmd(nc, maps, core_ids=cores)

    out = np.empty((4, S, D), np.float32)
    for b in range(4):
        x4 = np.concatenate([np.asarray(res.results[2 * b]["x4T"]),
                             np.asarray(res.results[2 * b + 1]["x4T"])], axis=1)
        out[b] = x4.T
    return out


def kernel(**inputs):
    inputs = {k: np.asarray(v) for k, v in inputs.items()}
    nc = _get_nc("fused", build_fused)
    maps = host_prep_fused(inputs)
    res = run_bass_kernel_spmd(nc, maps, core_ids=list(range(NCORES)))
    out = np.empty((4, S, D), np.float32)
    for b in range(4):
        x4 = np.concatenate([np.asarray(res.results[2 * b]["x4T"]),
                             np.asarray(res.results[2 * b + 1]["x4T"])], axis=1)
        out[b] = x4.T
    return out
